# version 14
# baseline (speedup 1.0000x reference)
import numpy as np
import concourse.bass as bass
import concourse.mybir as mybir
from concourse.bass_utils import run_bass_kernel_spmd

F32 = mybir.dt.float32
BF16 = mybir.dt.bfloat16
AF = mybir.ActivationFunctionType
ALU = mybir.AluOpType
AX = mybir.AxisListType

B, T, C = 4, 8192, 1024
TL = T // 2
H, HD = 16, 64
M, R = 32, 2
FD = R * M
CH = 512
NCH = TL // CH
NCORE = 8
E20 = float(np.exp(20.0).astype(np.float32))

ENGINES = ("sync", "tensor", "vector", "scalar", "gpsimd")


class Prog:

    LD_POOL = 4
    ST_POOL = 6

    def __init__(self, nc):
        self.nc = nc
        self.ops = {e: [] for e in ENGINES}
        self.counts = {}
        self.sem_names = []
        self.dma_issue = {"sync": 0, "gpsimd": 0}

    def sem(self, name):
        self.counts[name] = 0
        self.sem_names.append(name)

    def set_barrier(self):
        waits = [(s, self.counts[s]) for s in self.sem_names if self.counts[s] > 0]
        self.pending_barrier = {e: list(waits) for e in ENGINES}

    def dma_pool(self, queue):
        if queue == "sync":
            return [f"ld{i}" for i in range(self.LD_POOL)]
        return [f"st{i}" for i in range(self.ST_POOL)]

    def emit(self, eng, fn, waits=(), inc=None, inc_n=1):
        tok = None
        if inc is not None:
            self.counts[inc] += inc_n
            tok = (inc, self.counts[inc])
        ws = [w for w in waits if w is not None]
        pb = getattr(self, "pending_barrier", None)
        if pb and pb.get(eng):
            ws = pb.pop(eng) + ws
        self.ops[eng].append((fn, ws, inc, inc_n, tok))
        return tok

    def dma(self, queue, out, in_, waits=(), **kw):
        pool = self.dma_pool(queue)
        sem = pool[self.dma_issue[queue] % len(pool)]
        self.dma_issue[queue] += 1
        prev_total = self.counts[sem]
        ws = list(waits)
        if prev_total > 0:
            ws.append((sem, prev_total))
        return self.emit(queue, lambda h: h.dma_start(out=out, in_=in_, **kw),
                         waits=ws, inc=sem, inc_n=16)

    def dma_all_done_waits(self):
        return [(s, self.counts[s]) for s in self.sem_names
                if (s.startswith("ld") or s.startswith("st")) and self.counts[s] > 0]

    def build(self, block, sems):
        def runner(eng_name):
            def _run(h):
                wm = {}
                for fn, ws, inc, inc_n, tok in self.ops[eng_name]:
                    best = {}
                    for (sname, val) in ws:
                        if val > best.get(sname, -1):
                            best[sname] = val
                    for sname, val in best.items():
                        if val > wm.get(sname, -1):
                            h.wait_ge(sems[sname], val)
                            wm[sname] = val
                    inst = fn(h)
                    if inc is not None:
                        inst.then_inc(sems[inc], inc_n)
            return _run

        for e in ENGINES:
            getattr(block, e)(runner(e))

    def final_waits(self, h, sems, eng="gpsimd"):
        pass


def build_common_consts(p, ctx, nc, pref=""):
    ident = ctx.enter_context(nc.sbuf_tensor(pref + "ident", [128, 128], BF16))
    tril = ctx.enter_context(nc.sbuf_tensor(pref + "tril", [128, 128], F32))
    t1 = p.emit("gpsimd", lambda h: h.memset(ident[:], 1.0), inc="st_g")
    t2 = p.emit("gpsimd", lambda h: h.affine_select(
        ident[:], ident[:], [[1, 128]], ALU.is_equal, 0.0, channel_multiplier=-1),
        waits=[t1], inc="st_g")
    t3 = p.emit("gpsimd", lambda h: h.memset(tril[:], 1.0), inc="st_g")
    t4 = p.emit("gpsimd", lambda h: h.affine_select(
        tril[:], tril[:], [[1, 128]], ALU.is_ge, 0.0, channel_multiplier=-1),
        waits=[t3], inc="st_g")
    return ident, tril, [t2, t4]


def emit_A(p, ctx, nc, io):
    NQ = 24

    pref = io.get("prefix", "")
    ident, _, const_toks = build_common_consts(p, ctx, nc, pref)

    sb = lambda name, shape, dt=BF16: ctx.enter_context(nc.sbuf_tensor(pref + name, shape, dt))
    w16 = sb("w16", [128, 8, 3072])
    w32t = sb("w32t", [128, 3072], F32)
    bqT = sb("bqT", [128, 24], F32)
    om16 = sb("om16", [64, 16, 64])
    ones2h = sb("ones2h", [128, 2])
    sel2 = sb("sel2", [2, 128])
    svec = sb("svec", [64, 1], F32)
    sq2s = sb("sq2s", [64, 1], F32)
    negs = sb("negs", [64, 1], F32)
    wMv = sb("wMv", [64, 1], F32)
    ewMv = sb("ewMv", [64, 1], F32)
    xc32 = [sb(f"xc32_{i}", [128, 8, 512], F32) for i in range(2)]
    x16 = [sb(f"x16_{i}", [128, 8, 512]) for i in range(2)]
    qk16 = [sb(f"qk16_{i}", [128, 512]) for i in range(2)]
    v16fm = [sb(f"v16fm_{i}", [128, 512]) for i in range(2)]
    sq16 = [sb(f"sq16_{i}", [128, 512]) for i in range(2)]
    r1sb = [sb(f"r1sb_{i}", [2, 512], F32) for i in range(2)]
    r2sb = [sb(f"r2sb_{i}", [2, 512], F32) for i in range(2)]
    rn16 = [sb(f"rn16_{i}", [2, 512]) for i in range(2)]
    qnh = [sb(f"qnh_{i}", [64, 2, 512]) for i in range(2)]
    f16 = [sb(f"f16_{i}", [64, 2, 512]) for i in range(2)]
    kft = sb("kft", [128, 16, 4, 64])
    v1b = sb("v1b", [128, 4, 130])
    ssb = sb("ssb", [64, 16, 65], F32)

    ps = lambda name, shape, dt=F32: ctx.enter_context(nc.psum_tensor(pref + name, shape, dt))
    ps_mm = [ps(f"ps_mm{i}", [128, 512]) for i in range(2)]
    ps_bc = ps("ps_bc", [128, 512])
    ps_proj = ps("ps_proj", [64, 512])
    ps_ssq = [ps_mm[i][0:2, :] for i in range(2)]
    ps_proj2 = [ps_proj[:], ps_bc[0:64, :]]
    ps_vT = ps("ps_vT", [128, 128], BF16)
    ps_state = [ps(f"ps_st{i}", [64, 512]) for i in range(2)]
    ps_k0 = ps("ps_k0", [64, 16])

    tok_w = []
    last_cast = None
    for kc in range(8):
        tl = p.dma("sync", w32t[:], io["Wqkv"].rearrange("(kc p) n -> p kc n", p=128)[:, kc, :],
                   waits=[last_cast])
        last_cast = p.emit("vector", lambda h, kc=kc: h.tensor_copy(w16[:, kc, :], w32t[:]),
                           waits=[tl], inc="dve")
        tok_w.append(last_cast)
    bq24 = sb("bq24", [24, 128], F32)
    ident32 = sb("ident32", [128, 128], F32)
    ti = p.emit("gpsimd", lambda h: h.memset(ident32[:], 1.0), inc="st_g")
    ti2 = p.emit("gpsimd", lambda h: h.affine_select(
        ident32[:], ident32[:], [[1, 128]], ALU.is_equal, 0.0, channel_multiplier=-1),
        waits=[ti], inc="st_g")
    tb = p.dma("sync", bq24[:], io["bqkvT"])
    tb2 = p.emit("tensor", lambda h: h.transpose(ps_bc[:, 0:24], bq24[:], ident32[0:24, 0:24]),
                 waits=[tb, ti2], inc="pe")
    tb3 = p.emit("vector", lambda h: h.tensor_copy(bqT[:], ps_bc[:, 0:24]), waits=[tb2], inc="dve")

    om32 = sb("om32", [64, 1024], F32)
    to1 = p.dma("sync", om32[:], io["omh"])
    to2 = p.emit("vector", lambda h: h.tensor_copy(
        om16[:].rearrange("p h m -> p (h m)"), om32[:]), waits=[to1], inc="dve")

    tg = []
    tg.append(p.emit("gpsimd", lambda h: h.memset(ones2h[:], 1.0), inc="st_g"))
    tg.append(p.emit("gpsimd", lambda h: h.affine_select(
        ones2h[0:64, :], ones2h[0:64, :], [[1, 2]], ALU.is_equal, 0.0,
        channel_multiplier=0), waits=[tg[-1]], inc="st_g"))
    tg.append(p.emit("gpsimd", lambda h: h.affine_select(
        ones2h[64:128, :], ones2h[64:128, :], [[1, 2]], ALU.is_equal, 0.0,
        base=-1, channel_multiplier=0), waits=[tg[-2]], inc="st_g"))
    tg.append(p.emit("gpsimd", lambda h: h.memset(sel2[:], 1.0), inc="st_g"))
    tg.append(p.emit("gpsimd", lambda h: h.affine_select(
        sel2[:], sel2[:], [[1, 2], [0, 64]], ALU.is_equal, 0.0,
        channel_multiplier=-1), waits=[tg[-1]], inc="st_g"))
    for tb_ in range(4):
        tg.append(p.emit("gpsimd", lambda h, tb_=tb_: h.memset(v1b[:, tb_, 64:65], 1.0), inc="st_g"))
        tg.append(p.emit("gpsimd", lambda h, tb_=tb_: h.memset(v1b[:, tb_, 129:130], 1.0), inc="st_g"))

    wvec = sb("wvec", [64, 1], F32)
    tq = []
    for f in range(64):
        r = f // 32
        tq.append(p.dma("sync", svec[f:f + 1, 0:1], io["qn"][r:r + 1].unsqueeze(1)))
        tq.append(p.dma("sync", wvec[f:f + 1, 0:1], io["qw"][r:r + 1].unsqueeze(1)))
    tmp1 = sb("tmp1", [64, 1], F32)
    qv_waits = p.dma_all_done_waits()
    t_a = p.emit("vector", lambda h: h.tensor_scalar(tmp1[:], svec[:], 2.0, 1e-30, ALU.mult, ALU.max),
                 waits=qv_waits, inc="dve")
    t_b = p.emit("scalar", lambda h: h.activation(tmp1[:], tmp1[:], AF.Ln), waits=[t_a], inc="act")
    t_c = p.emit("scalar", lambda h: h.activation(sq2s[:], tmp1[:], AF.Exp, scale=0.5), waits=[t_b], inc="act")
    t_d = p.emit("vector", lambda h: h.tensor_scalar_mul(negs[:], svec[:], -1.0), waits=qv_waits, inc="dve")
    t_e = p.emit("vector", lambda h: h.tensor_scalar(wMv[:], wvec[:], 0.0, 1.0 / M, ALU.max, ALU.mult),
                 waits=qv_waits, inc="dve")
    t_f = p.emit("vector", lambda h: h.tensor_scalar_mul(ewMv[:], wMv[:], E20), waits=[t_e], inc="dve")
    pre_toks = [tok_w[-1], tb3, to2, t_c, t_d, t_f] + tg + const_toks

    xT_t = io["xT"].rearrange("(kc p) t -> p kc t", p=128)
    last_mm_read_x = [None, None]
    last_evac = [None, None]
    last_pbuse = {"ssq": None, "bc": None, "proj": None, "vT": None}
    last_ssq_read = [None, None]
    last_qk_use = [None, None]
    last_v_use = [None, None]
    prev_tssq = [None, None]
    prev_tr2 = [None, None]
    prev_trn_use = [None, None]
    last_qnh_use = [None, None]
    last_f16_use = [[[], []], [[], []]]
    last_state_read = None
    last_kft_use = None
    last_v1b_dma = []
    state_first = [True, True]

    for c in range(NCH):
        par = c % 2
        tl = p.dma("sync", xc32[par][:], xT_t[:, :, c * CH:(c + 1) * CH],
                   waits=[last_mm_read_x[par]] + (pre_toks if c == 0 else []))
        cast_eng = "vector" if par == 0 else "scalar"
        if cast_eng == "vector":
            tcast = p.emit("vector", lambda h, par=par: h.tensor_copy(
                x16[par][:].rearrange("p a b -> p (a b)"), xc32[par][:].rearrange("p a b -> p (a b)")),
                waits=[tl], inc="dve")
        else:
            tcast = p.emit("scalar", lambda h, par=par: h.activation(
                x16[par][:].rearrange("p a b -> p (a b)"), xc32[par][:].rearrange("p a b -> p (a b)"),
                AF.Copy), waits=[tl], inc="act")

        def qkv_tile(j):
            i = j % 2
            t = None
            for kc in range(8):
                t = p.emit("tensor", lambda h, kc=kc, j=j, i=i, par=par: h.matmul(
                    ps_mm[i][:], w16[:, kc, j * 128:(j + 1) * 128], x16[par][:, kc, :],
                    start=(kc == 0), stop=(kc == 7)),
                    waits=[tcast, last_evac[i], last_ssq_read[i]] if kc == 0 else [],
                    inc="pe")
            last_mm_read_x[par] = t
            return t

        for j in range(16):
            i = j % 2
            is_q = j < 8
            h0 = (j % 8) * 2
            tmm = qkv_tile(j)
            tev = p.emit("scalar", lambda h, i=i, j=j: h.activation(
                qk16[i][:], ps_mm[i][:], AF.Identity, bias=bqT[:, j:j + 1]),
                waits=[tmm, last_qk_use[i]], inc="act")
            last_evac[i] = tev
            tsq = p.emit("vector", lambda h, i=i: h.tensor_mul(
                sq16[i][:], qk16[i][:], qk16[i][:]),
                waits=[tev, prev_tssq[i]], inc="dve")
            tssq = p.emit("tensor", lambda h, i=i: h.matmul(
                ps_ssq[i], ones2h[:], sq16[i][:], start=True, stop=True),
                waits=[tsq, last_ssq_read[i]], inc="pe")
            prev_tssq[i] = tssq
            tr1 = p.emit("vector", lambda h, i=i: h.tensor_scalar_max(
                r1sb[i][:], ps_ssq[i], 1e-24),
                waits=[tssq, prev_tr2[i]], inc="dve")
            last_ssq_read[i] = tr1
            tr2 = p.emit("scalar", lambda h, i=i: h.activation(r2sb[i][:], r1sb[i][:], AF.Ln),
                         waits=[tr1, prev_trn_use[i]], inc="act")
            prev_tr2[i] = tr2
            trn = p.emit("scalar", lambda h, i=i: h.activation(
                rn16[i][:], r2sb[i][:], AF.Exp, scale=-0.5),
                waits=[tr2], inc="act")
            tbc = p.emit("tensor", lambda h, i=i: h.matmul(
                ps_bc[:], sel2[:], rn16[i][:], start=True, stop=True),
                waits=[trn, last_pbuse["bc"]], inc="pe")
            prev_trn_use[i] = tbc
            tqn = []
            for hh in range(2):
                tqn.append(p.emit("vector", lambda h, hh=hh, i=i: h.tensor_mul(
                    qnh[i][:, hh, :], qk16[i][hh * 64:(hh + 1) * 64, :],
                    ps_bc[hh * 64:(hh + 1) * 64, :]),
                    waits=[tbc, last_qnh_use[i]], inc="dve"))
            last_qk_use[i] = tqn[-1]
            for hh in range(2):
                hcur = h0 + hh
                prj = ps_proj2[hh]
                pwait = last_pbuse["proj"] if hh == 0 else tqn[-1]
                tproj = p.emit("tensor", lambda h, hh=hh, hcur=hcur, i=i, prj=prj: h.matmul(
                    prj, om16[:, hcur, :], qnh[i][:, hh, :], start=True, stop=True),
                    waits=[tqn[hh], pwait], inc="pe")
                last_qnh_use[i] = tproj
                tfe = p.emit("scalar", lambda h, hh=hh, i=i, prj=prj: h.activation(
                    f16[i][:, hh, :], prj, AF.Exp, bias=negs[:], scale=sq2s[:]),
                    waits=[tproj] + last_f16_use[i][hh], inc="act")
                if hh == 0:
                    last_pbuse["proj"] = tfe
                else:
                    last_pbuse["bc"] = tfe
                if is_q:
                    tcl = p.emit("vector", lambda h, hh=hh, i=i: h.tensor_scalar_min(
                        f16[i][:, hh, :], f16[i][:, hh, :], E20), waits=[tfe], inc="dve")
                    tdma = p.dma("gpsimd", io["qf"][c, :, hcur, :], f16[i][:, hh, :],
                                 waits=[tcl])
                    last_f16_use[i][hh] = [tdma]
                else:
                    tcl = p.emit("vector", lambda h, hh=hh, i=i: h.tensor_scalar(
                        f16[i][:, hh, :], f16[i][:, hh, :], wMv[:], ewMv[:], ALU.mult, ALU.min),
                        waits=[tfe], inc="dve")
                    tdma = p.dma("gpsimd", io["kf"][c, :, hcur, :], f16[i][:, hh, :],
                                 waits=[tcl])
                    tt = None
                    for tb_ in range(4):
                        tt = p.emit("tensor", lambda h, hh=hh, tb_=tb_, i=i: h.transpose(
                            ps_vT[0:128, 0:64], f16[i][:, hh, tb_ * 128:(tb_ + 1) * 128],
                            ident[0:64, 0:64]),
                            waits=[tcl, last_pbuse["vT"], last_kft_use], inc="pe")
                        last_pbuse["vT"] = p.emit("vector",
                            lambda h, hcur=hcur, tb_=tb_: h.tensor_copy(
                                kft[:, hcur, tb_, :], ps_vT[0:128, 0:64]),
                            waits=[tt], inc="dve")
                    last_f16_use[i][hh] = [tdma, tt]

        kft_done = last_pbuse["vT"]

        state_mm_tok = None
        for j in range(16, 24):
            i = j % 2
            h0 = (j - 16) * 2
            tmm = qkv_tile(j)
            tev = p.emit("scalar", lambda h, i=i, j=j: h.activation(
                v16fm[i][:], ps_mm[i][:], AF.Identity, bias=bqT[:, j:j + 1]),
                waits=[tmm, last_v_use[i]], inc="act")
            last_evac[i] = tev
            ed = []
            for tb_ in range(4):
                tt = p.emit("tensor", lambda h, tb_=tb_, i=i: h.transpose(
                    ps_vT[:], v16fm[i][:, tb_ * 128:(tb_ + 1) * 128], ident[:]),
                    waits=[tev, last_pbuse["vT"]], inc="pe")
                e0 = p.emit("vector", lambda h, tb_=tb_: h.tensor_copy(
                    v1b[:, tb_, 0:64], ps_vT[:, 0:64]),
                    waits=[tt, state_mm_tok] + last_v1b_dma, inc="dve")
                e1 = p.emit("vector", lambda h, tb_=tb_: h.tensor_copy(
                    v1b[:, tb_, 65:129], ps_vT[:, 64:128]),
                    waits=[tt, state_mm_tok] + last_v1b_dma, inc="dve")
                last_pbuse["vT"] = e1
                ed.append((e0, e1))
            d0 = p.dma("gpsimd", io["vt"][h0, c, :, :, :], v1b[:, :, 0:64],
                       waits=[e for (e, _) in ed])
            d1 = p.dma("gpsimd", io["vt"][h0 + 1, c, :, :, :], v1b[:, :, 65:129],
                       waits=[e for (_, e) in ed])
            last_v1b_dma = [d0, d1]
            last_v_use[i] = ed[-1][1]
            for tb_ in range(4):
                for hh in range(2):
                    hcur = h0 + hh
                    grp, off = hcur // 8, (hcur % 8) * 64
                    first = state_first[grp]
                    state_first[grp] = False
                    state_mm_tok = p.emit("tensor", lambda h, grp=grp, off=off, hcur=hcur,
                                          tb_=tb_, hh=hh, first=first: h.matmul(
                        ps_state[grp][:, off:off + 64], kft[:, hcur, tb_, :],
                        v1b[:, tb_, hh * 65:hh * 65 + 64],
                        start=first, stop=False, skip_group_check=True),
                        waits=[ed[tb_][hh], kft_done, last_state_read], inc="pe")
                    firstk = state_first[2] if len(state_first) > 2 else True
                    if len(state_first) == 2:
                        state_first.append(False)
                    else:
                        state_first[2] = False
                    state_mm_tok = p.emit("tensor", lambda h, hcur=hcur, tb_=tb_, hh=hh,
                                          firstk=firstk: h.matmul(
                        ps_k0[:, hcur:hcur + 1], kft[:, hcur, tb_, :],
                        v1b[:, tb_, hh * 65 + 64:hh * 65 + 65],
                        start=firstk, stop=False, skip_group_check=True),
                        waits=[], inc="pe")
            last_kft_use = state_mm_tok

        sp = []
        for grp in range(2):
            sp.append(p.emit("vector", lambda h, grp=grp: h.tensor_copy(
                ssb[:, grp * 8:(grp + 1) * 8, 0:64],
                ps_state[grp][:].rearrange("p (a b) -> p a b", b=64)), waits=[state_mm_tok], inc="dve"))
        sp.append(p.emit("vector", lambda h: h.tensor_copy(
            ssb[:, :, 64:65], ps_k0[:].unsqueeze(2)), waits=[state_mm_tok, sp[-1]], inc="dve"))
        last_state_read = sp[-1]
        dst = io["sprev"][c] if c < NCH - 1 else io["stot"]
        p.dma("gpsimd", dst.rearrange("h f e -> f h e"), ssb[:], waits=sp)

    p.emit("gpsimd", lambda h: h.engine_nop(), waits=p.dma_all_done_waits())




def emit_B(p, ctx, nc, io):
    import os as _os
    B_STAGE = int(_os.environ.get("B_STAGE", "4"))
    """Launch B: chunked causal linear attention from spilled features,
    then out @ Wout + bout + residual(x), LayerNorm -> x1.

    io: qf, kf [NCH, FD, H, CH] bf16; vt [H, NCH, 128, 4, HD] bf16;
        sprev8 [NCH, H, FD, 65] f32 (prefix state per chunk, row 0 zeros);
        init [H, FD, 65] f32 (partner total state or zeros);
        x [TL, C] f32; Wout [C, C] f32; bout [C] f32 -> x1 [TL, C] f32.
    """
    pref = io.get("prefix", "")
    ident, tril, const_toks = build_common_consts(p, ctx, nc, pref)
    sb = lambda name, shape, dt=BF16: ctx.enter_context(nc.sbuf_tensor(pref + name, shape, dt))
    ps = lambda name, shape, dt=F32: ctx.enter_context(nc.psum_tensor(pref + name, shape, dt))

    w16o = sb("w16o", [128, 8, 1024])
    w32t = sb("w32t", [128, 1024], F32)
    bout_bc = sb("bout_bc", [128, 1024], F32)
    init_sb = sb("init_sb", [64, 16, 65], F32)
    qf_all = [sb(f"qf_all{i}", [64, 16, 512]) for i in range(2)]
    kf_all = [sb(f"kf_all{i}", [64, 16, 512]) for i in range(2)]
    sprev_all = [sb(f"sprev_all{i}", [64, 16, 65], F32) for i in range(2)]
    v1b0 = [sb(f"v1b0_{i}", [128, 4, 64]) for i in range(2)]
    v1b = [sb(f"v1b_{i}", [128, 4, 65]) for i in range(2)]
    st16 = [sb(f"st16_{i}", [64, 65]) for i in range(2)]
    Ebuf = sb("Ebuf", [128, 4, 512])
    af = sb("af", [128, 8, 512])
    obuf = [sb(f"obuf{i}", [128, 64]) for i in range(2)]
    den = [sb(f"den{i}", [128, 1], F32) for i in range(2)]
    rden = [sb(f"rden{i}", [128, 1], F32) for i in range(2)]
    xt32 = [sb(f"xt32_{i}", [128, 1024], F32) for i in range(2)]
    yt = sb("yt", [128, 1024], F32)
    ym = sb("ym", [128, 1024], F32)
    sqt = sb("sqt", [128, 1024], F32)
    snm = sb("snm", [128, 1], F32)
    negm = sb("negm", [128, 1], F32)
    lnv = sb("lnv", [128, 1], F32)
    rstd = sb("rstd", [128, 1], F32)
    x1t = [sb(f"x1t_{i}", [128, 1024], F32) for i in range(2)]

    ps_S = [ps(f"ps_S{i}", [128, 512]) for i in range(2)]
    ps_ctx = [ps(f"ps_ctx{i}", [128, 65]) for i in range(2)]
    ps_T = [ps(f"ps_T{i}", [64, 128], BF16) for i in range(2)]
    ps_o = [ps(f"ps_o{i}", [128, 512]) for i in range(2)]

    last_cast = None
    wtoks = []
    for kc in range(8):
        tl = p.dma("sync", w32t[:], io["Wout"].rearrange("(kc p) n -> p kc n", p=128)[:, kc, :],
                   waits=[last_cast])
        last_cast = p.emit("vector", lambda h, kc=kc: h.tensor_copy(w16o[:, kc, :], w32t[:]),
                           waits=[tl], inc="dve")
        wtoks.append(last_cast)
    tbb = p.dma("sync", bout_bc[:], io["bout"].unsqueeze(0).broadcast_to([128, C]))
    tinit = p.dma("sync", init_sb[:], io["init"].rearrange("h f e -> f h e"))
    tv1 = [p.emit("gpsimd", lambda h, i=i, tb=tb: h.memset(v1b[i][:, tb, 64:65], 1.0), inc="st_g")
           for i in range(2) for tb in range(4)]
    pre = [wtoks[-1], tbb, tinit] + tv1 + const_toks

    last_ctx_div = [None, None]
    last_S_evac = [None, None]
    last_E_read = None
    last_v1b_read = [None, None]
    last_st_read = [None, None]
    last_obuf_read = [None, None]
    last_T_evac = [None, None]
    last_af_read = None
    last_ps_o_read = [None, None]
    last_x_read = [None, None]
    last_x1_dma = [None, None]
    last_qfkf_read = [None, None]
    last_sprev_read = [None, None]

    for c in range(NCH):
        par = c % 2
        tq = p.dma("sync", qf_all[par][:], io["qf"][c].rearrange("f h t -> f (h t)"),
                   waits=[last_qfkf_read[par]] + (pre if c == 0 else []))
        tk = p.dma("sync", kf_all[par][:], io["kf"][c].rearrange("f h t -> f (h t)"),
                   waits=[last_qfkf_read[par]])
        if "sprev7" in io:
            tsp = None if c == 0 else p.dma(
                "sync", sprev_all[par][:], io["sprev7"][c - 1].rearrange("h f e -> f h e"),
                waits=[last_sprev_read[par]])
        else:
            tsp = p.dma("sync", sprev_all[par][:], io["sprev8"][c].rearrange("h f e -> f h e"),
                        waits=[last_sprev_read[par]])
        af_evacs = []
        t0 = None
        for hh in range(16):
            hp = hh % 2
            tv = p.dma("sync", v1b0[hp][:], io["vt"][hh, c], waits=[last_v1b_read[hp]])
            if "sprev7" in io and c == 0:
                tst = p.emit("vector", lambda h, hp=hp, hh=hh: h.tensor_copy(
                    st16[hp][:], init_sb[:, hh, :]),
                    waits=[tinit, last_st_read[hp]], inc="dve")
            else:
                tst = p.emit("vector", lambda h, hp=hp, hh=hh, par=par: h.tensor_add(
                    st16[hp][:], sprev_all[par][:, hh, :], init_sb[:, hh, :]),
                    waits=[tsp, tinit, last_st_read[hp]], inc="dve")
                last_sprev_read[par] = tst
            vcp = []
            for tb in range(4):
                vcp.append(p.emit("vector", lambda h, hp=hp, tb=tb: h.tensor_copy(
                    v1b[hp][:, tb, 0:64], v1b0[hp][:, tb, :]),
                    waits=[tv, last_v1b_read[hp]] + tv1, inc="dve"))
            if B_STAGE < 2:
                continue
            ev_toks = []
            for i in range(4):
                sp_ = i % 2
                tS = p.emit("tensor", lambda h, i=i, sp_=sp_, hh=hh, par=par: h.matmul(
                    ps_S[sp_][:], kf_all[par][:, hh, i * 128:(i + 1) * 128],
                    qf_all[par][:, hh, :], start=True, stop=True),
                    waits=[tq, tk] + (last_S_evac[sp_] or []), inc="pe")
                td = p.emit("vector", lambda h, i=i, sp_=sp_: h.tensor_mul(
                    Ebuf[:, i, i * 128:(i + 1) * 128], ps_S[sp_][:, i * 128:(i + 1) * 128],
                    tril[:]), waits=[tS, last_E_read], inc="dve")
                ev_toks.append(td)
                if i < 3:
                    tc_ = p.emit("scalar", lambda h, i=i, sp_=sp_: h.activation(
                        Ebuf[:, i, (i + 1) * 128:512], ps_S[sp_][:, (i + 1) * 128:512],
                        AF.Copy), waits=[tS, last_E_read, td], inc="act")
                    ev_toks.append(tc_)
                    last_S_evac[sp_] = [td, tc_]
                else:
                    last_S_evac[sp_] = [td]
            if B_STAGE < 3:
                continue
            for j in range(4):
                cp_ = j % 2
                t0 = p.emit("tensor", lambda h, j=j, cp_=cp_, hh=hh, par=par, hp=hp: h.matmul(
                    ps_ctx[cp_][:], qf_all[par][:, hh, j * 128:(j + 1) * 128], st16[hp][:],
                    start=True, stop=(j == 3 and False) or False, skip_group_check=True),
                    waits=[tst, last_ctx_div[cp_]], inc="pe")
                for i in range(j + 1):
                    t0 = p.emit("tensor", lambda h, i=i, j=j, cp_=cp_, hp=hp: h.matmul(
                        ps_ctx[cp_][:], Ebuf[:, i, j * 128:(j + 1) * 128], v1b[hp][:, i, :],
                        start=False, stop=(i == j), skip_group_check=True),
                        waits=ev_toks + vcp, inc="pe")
                last_E_read = t0
                last_v1b_read[hp] = t0
                last_st_read[hp] = t0
                bk = j % 2
                td1 = p.emit("vector", lambda h, cp_=cp_, bk=bk: h.tensor_scalar_max(
                    den[bk][:], ps_ctx[cp_][:, 64:65], 1e-6), waits=[t0], inc="dve")
                td2 = p.emit("vector", lambda h, bk=bk: h.reciprocal(rden[bk][:], den[bk][:]),
                             waits=[td1], inc="dve")
                td3 = p.emit("vector", lambda h, cp_=cp_, bk=bk: h.tensor_scalar(
                    obuf[bk][:], ps_ctx[cp_][:, 0:64], rden[bk][:], None, ALU.mult),
                    waits=[td2, last_obuf_read[bk]], inc="dve")
                last_ctx_div[cp_] = td3
                tT = p.emit("tensor", lambda h, bk=bk: h.transpose(
                    ps_T[bk][:], obuf[bk][:], ident[:]),
                    waits=[td3, last_T_evac[bk]], inc="pe")
                last_obuf_read[bk] = tT
                te = p.emit("vector", lambda h, hh=hh, j=j, bk=bk: h.tensor_copy(
                    af[64 * (hh % 2):64 * (hh % 2) + 64, hh // 2, j * 128:(j + 1) * 128],
                    ps_T[bk][0:64, :]), waits=[tT, last_af_read], inc="dve")
                last_T_evac[bk] = te
                af_evacs.append(te)
        last_qfkf_read[par] = t0

        for j in range(4 if B_STAGE >= 4 else 0):
            xp = j % 2
            tx = p.dma("sync", xt32[xp][:], io["x"][(c * 4 + j) * 128:(c * 4 + j + 1) * 128, :],
                       waits=[last_x_read[xp]])
            ytoks = []
            for n in range(2):
                tmo = None
                for kc in range(8):
                    tmo = p.emit("tensor", lambda h, kc=kc, j=j, n=n: h.matmul(
                        ps_o[n][:], af[:, kc, j * 128:(j + 1) * 128],
                        w16o[:, kc, n * 512:(n + 1) * 512],
                        start=(kc == 0), stop=(kc == 7)),
                        waits=(af_evacs + [last_ps_o_read[n]]) if kc == 0 else [], inc="pe")
                last_af_read = tmo
                ty1 = p.emit("vector", lambda h, n=n, xp=xp: h.tensor_add(
                    yt[:, n * 512:(n + 1) * 512], ps_o[n][:], xt32[xp][:, n * 512:(n + 1) * 512]),
                    waits=[tmo, tx], inc="dve")
                last_x_read[xp] = ty1
                ty2 = p.emit("vector", lambda h, n=n: h.tensor_add(
                    yt[:, n * 512:(n + 1) * 512], yt[:, n * 512:(n + 1) * 512],
                    bout_bc[:, n * 512:(n + 1) * 512]), waits=[ty1, tbb], inc="dve")
                last_ps_o_read[n] = ty1
                ytoks.append(ty2)
            tr = p.emit("vector", lambda h: h.reduce_sum(snm[:], yt[:], AX.X),
                        waits=ytoks, inc="dve")
            tn = p.emit("vector", lambda h: h.tensor_scalar_mul(negm[:], snm[:], -1.0 / C),
                        waits=[tr], inc="dve")
            tm = p.emit("vector", lambda h: h.tensor_scalar_add(ym[:], yt[:], negm[:]),
                        waits=[tn], inc="dve")
            tsq = p.emit("scalar", lambda h: h.activation(sqt[:], ym[:], AF.Square),
                         waits=[tm], inc="act")
            tss = p.emit("vector", lambda h: h.reduce_sum(lnv[:], sqt[:], AX.X),
                         waits=[tsq], inc="dve")
            tva = p.emit("vector", lambda h: h.tensor_scalar(
                lnv[:], lnv[:], 1.0 / C, 1e-5, ALU.mult, ALU.add), waits=[tss], inc="dve")
            tln = p.emit("scalar", lambda h: h.activation(lnv[:], lnv[:], AF.Ln),
                         waits=[tva], inc="act")
            trs = p.emit("scalar", lambda h: h.activation(rstd[:], lnv[:], AF.Exp, scale=-0.5),
                         waits=[tln], inc="act")
            tx1 = p.emit("vector", lambda h, xp=xp: h.tensor_scalar(
                x1t[xp][:], ym[:], rstd[:], None, ALU.mult),
                waits=[trs, last_x1_dma[xp]], inc="dve")
            last_x1_dma[xp] = p.dma("gpsimd", io["x1"][(c * 4 + j) * 128:(c * 4 + j + 1) * 128, :],
                                    x1t[xp][:], waits=[tx1])

    p.emit("gpsimd", lambda h: h.engine_nop(), waits=p.dma_all_done_waits())


def build_B():
    nc = bass.Bass("TRN2", target_bir_lowering=False, debug=False)
    io = {
        "qf": nc.dram_tensor("qf", [NCH, FD, H, CH], BF16, kind="ExternalInput").ap(),
        "kf": nc.dram_tensor("kf", [NCH, FD, H, CH], BF16, kind="ExternalInput").ap(),
        "vt": nc.dram_tensor("vt", [H, NCH, 128, 4, HD], BF16, kind="ExternalInput").ap(),
        "sprev8": nc.dram_tensor("sprev8", [NCH, H, FD, 65], F32, kind="ExternalInput").ap(),
        "init": nc.dram_tensor("init", [H, FD, 65], F32, kind="ExternalInput").ap(),
        "x": nc.dram_tensor("x", [TL, C], F32, kind="ExternalInput").ap(),
        "Wout": nc.dram_tensor("Wout", [C, C], F32, kind="ExternalInput").ap(),
        "bout": nc.dram_tensor("bout", [C], F32, kind="ExternalInput").ap(),
        "x1": nc.dram_tensor("x1", [TL, C], F32, kind="ExternalOutput").ap(),
    }
    finish_build(nc, emit_B, io)
    return nc




def emit_quant(p, e, x_in, q_out, scr, waits):
    F = x_in.shape[1]
    NB = F // 16
    sem = "dve" if e == "vector" else "st_g"
    A_ = scr["bufA"][:, 0:F]
    B_ = scr["bufB"][:, 0:F]
    Ab = scr["bufA"].bitcast(BF16)
    acc = Ab[:, 0:F]
    tmp = Ab[:, F:2 * F]
    Bb = scr["bufB"].bitcast(BF16)[:, 0:F]
    bm = scr["bm"][:, 0:NB]
    sc = scr["sc"][:, 0:NB]
    x3 = x_in.rearrange("p (nb w) -> p nb w", w=16)
    t1 = p.emit("vector", lambda h: h.tensor_reduce(bm.unsqueeze(2), x3, AX.X, ALU.max,
                apply_absolute_value=True), waits=waits, inc="dve")
    t2 = p.emit("vector", lambda h: h.tensor_scalar(sc, bm, 1.0 / 6.0, 1e-12, ALU.mult, ALU.max),
                waits=[t1], inc="dve")
    t3 = p.emit("vector", lambda h: h.reciprocal(bm, sc), waits=[t2], inc="dve")
    ta = p.emit("scalar", lambda h: h.activation(A_, x_in, AF.Abs),
                waits=list(waits) + [("act", p.counts["act"])], inc="act")
    rs_b = bm.unsqueeze(2).broadcast_to([128, NB, 16])
    tu = p.emit(e, lambda h: h.tensor_tensor(
        B_.rearrange("p (nb w) -> p nb w", w=16), A_.rearrange("p (nb w) -> p nb w", w=16),
        rs_b, ALU.mult), waits=[t3, ta], inc=sem)
    tacc = p.emit(e, lambda h: h.tensor_scalar(acc, B_, 0.25, 1.0, ALU.is_gt, ALU.mult),
                  waits=[tu], inc=sem)
    for thr, step in ((0.75, 1.0), (1.25, 1.0), (1.75, 1.0), (2.5, 2.0), (3.5, 2.0), (5.0, 4.0)):
        tci = p.emit(e, lambda h, thr=thr, step=step: h.tensor_scalar(
            tmp, B_, thr, step, ALU.is_gt, ALU.mult), waits=[tacc], inc=sem)
        tacc = p.emit(e, lambda h: h.tensor_add(acc, acc, tmp), waits=[tci], inc=sem)
    tsg = p.emit(e, lambda h: h.tensor_scalar(tmp, x_in, 0.0, 0.5, ALU.is_ge, ALU.subtract),
                 waits=[tacc] + list(waits), inc=sem)
    tt2 = p.emit(e, lambda h: h.tensor_mul(Bb, acc, tmp), waits=[tsg], inc=sem)
    sc_b = sc.unsqueeze(2).broadcast_to([128, NB, 16])
    tqf = p.emit(e, lambda h: h.tensor_tensor(
        q_out.rearrange("p (nb w) -> p nb w", w=16), Bb.rearrange("p (nb w) -> p nb w", w=16),
        sc_b, ALU.mult), waits=[tt2], inc=sem)
    return tqf




_DPREP = ([("w_gate", r) for r in range(22)] + [("w_up", r) for r in range(22)]
          + [None] * 4 + [("w_down", r) for r in range(8)])


def emit_Dprep(p, ctx, nc, io):
    pref = io.get("prefix", "")
    ident = ctx.enter_context(nc.sbuf_tensor(pref + "ident", [128, 128], BF16))
    ti = p.emit("gpsimd", lambda h: h.memset(ident[:], 1.0), inc="st_g")
    ti2 = p.emit("gpsimd", lambda h: h.affine_select(
        ident[:], ident[:], [[1, 128]], ALU.is_equal, 0.0, channel_multiplier=-1),
        waits=[ti], inc="st_g")
    sb = lambda name, shape, dt=BF16: ctx.enter_context(nc.sbuf_tensor(pref + name, shape, dt))
    scr = {
        "bufA": sb("qA", [128, 1408], F32),
        "bufB": sb("qB", [128, 1408], F32),
        "bm": sb("qbm", [128, 88], F32),
        "sc": sb("qsc", [128, 88], F32),
    }
    w32 = sb("w32", [128, 2816], F32)
    wq = sb("wq", [128, 2816])
    tstage = sb("tstage", [128, 22, 128])
    ps_T = ctx.enter_context(nc.psum_tensor(pref + "ps_T", [128, 128], BF16))
    tz = p.emit("gpsimd", lambda h: h.memset(tstage[:], 0.0), waits=[ti2], inc="st_g")
    last_load = None
    last_q = []
    last_T_evac = None
    last_st = None
    sdma = []
    for s in range(7):
        W = 1024 if s < 6 else 2816
        nkc = 8 if s < 6 else 22
        if s < 6:
            tl = p.dma("sync", w32[:, 0:1024], io["wrowsA"][s], waits=[last_load])
        else:
            tl = p.dma("sync", w32[:], io["wrowsB"], waits=[last_load])
        tq_all = []
        for (o, w_) in ([(0, 1024)] if s < 6 else [(0, 1408), (1408, 1408)]):
            tq_all.append(emit_quant(p, "vector", w32[:, o:o + w_], wq[:, o:o + w_], scr,
                          waits=[tl] + last_q + tq_all))
        last_load = tq_all[-1]
        last_q = tq_all
        evs = []
        for kc in range(nkc):
            tt = p.emit("tensor", lambda h, kc=kc: h.transpose(
                ps_T[:], wq[:, kc * 128:(kc + 1) * 128], ident[:]),
                waits=tq_all + [last_T_evac, tz], inc="pe")
            last_T_evac = p.emit("vector", lambda h, kc=kc: h.tensor_copy(
                tstage[:, kc, :], ps_T[:]), waits=[tt, last_st], inc="dve")
            evs.append(last_T_evac)
        last_q = [last_T_evac]
        last_st = p.dma("gpsimd", io["ag_in"][s].rearrange("kc p t -> p kc t"),
                        tstage[:], waits=evs + [tz])
        sdma.append(last_st)
    tcc = p.emit("gpsimd", lambda h: h.collective_compute(
        "AllGather", ALU.bypass, replica_groups=[[0, 1, 2, 3, 4, 5, 6, 7]],
        ins=[io["ag_in"].rearrange("s kc p t -> (s kc p) t").opt()],
        outs=[io["ag_out"].opt()]),
        waits=sdma, inc="cc")
    p.emit("gpsimd", lambda h: h.engine_nop(), waits=p.dma_all_done_waits() + [tcc])


def emit_D(p, ctx, nc, io):
    import os as _os
    D_STAGE = int(_os.environ.get("D_STAGE", "3"))
    """Launch D: L0 fp4 FFN: x2 = LN(x1 + fp4_lin(silu(fp4_lin(x1,wg)) * fp4_lin(x1,wu), wd))

    io: x1 [TL, C] f32; w_gate [I, C] f32; w_up [I, C] f32; w_down [C, I] f32
        -> x2 [TL, C] f32.  I = 2816.
    """
    I = 2816
    NI = I // 128
    pref = io.get("prefix", "")
    ident, _, const_toks = build_common_consts(p, ctx, nc, pref)
    sb = lambda name, shape, dt=BF16: ctx.enter_context(nc.sbuf_tensor(pref + name, shape, dt))
    ps = lambda name, shape, dt=F32: ctx.enter_context(nc.psum_tensor(pref + name, shape, dt))

    wgT = sb("wgT", [128, 8, I])
    wuT = sb("wuT", [128, 8, I])
    wdT = sb("wdT", [128, NI, C])
    scrV = {
        "bufA": sb("qvA", [128, 1408], F32),
        "bufB": sb("qvB", [128, 1408], F32),
        "bm": sb("qvbm", [128, 88], F32),
        "sc": sb("qvsc", [128, 88], F32),
    }
    magic = ctx.enter_context(nc.sbuf_tensor(pref + "magic", [128, 1], mybir.dt.int32))
    iscr = ctx.enter_context(nc.sbuf_tensor(pref + "iscr", [128, 1], mybir.dt.int32))
    nt1 = sb("nt1", [128, 1], F32)
    x32 = [sb(f"x32_{i}", [128, C], F32) for i in range(2)]
    xq = sb("xq", [128, C])
    xqT = sb("xqT", [128, 8, 128])
    hbuf = [sb(f"hbuf{i}", [128, I], F32) for i in range(2)]
    w32 = hbuf[0]
    eb = sb("eb", [128, 512], F32)
    hgb = sb("hgb", [128, 512], F32)
    hq = sb("hq", [128, I])
    wq = hq
    hqT = sb("hqT", [128, NI, 128])
    x2t = [sb(f"x2t_{i}", [128, C], F32) for i in range(2)]
    snm = sb("snm", [128, 1], F32)
    negm = sb("negm", [128, 1], F32)
    lnv = sb("lnv", [128, 1], F32)
    rstd = sb("rstd", [128, 1], F32)

    ps_g = ps("ps_g", [128, 512])
    ps_u = ps("ps_u", [128, 512])
    ps_T = [ps(f"ps_T{i}", [128, 128], BF16) for i in range(3)]
    ps_d = [ps(f"ps_d{i}", [128, 512]) for i in range(2)]

    STRIPS = [(0, 512), (512, 512), (1024, 512), (1536, 512), (2048, 512), (2560, 256)]

    tmagic = p.emit("gpsimd", lambda h: h.memset(magic[:], 0x5f3759df), inc="st_g")
    lte = [None, None, None]
    wdone = None
    if "ag_out" in io:
        agv = io["ag_out"].rearrange("(r s kc p) t -> r s kc p t", r=8, s=7, kc=22)
        wl = []
        for j, ent in enumerate(_DPREP):
            if ent is None:
                continue
            wname, r = ent
            rank, si = j % 8, j // 8
            nkc = 8 if si < 6 else 22
            dstT = {"w_gate": wgT, "w_up": wuT, "w_down": wdT}[wname]
            wl.append(p.dma("sync", dstT[:, 0:nkc, r * 128:(r + 1) * 128],
                            agv[rank, si, 0:nkc].rearrange("kc p t -> p kc t"),
                            waits=const_toks if not wl else []))
        wdone = wl[-1]
    else:
        wtoks = []
        last_wq_read = []
        last_w32_read = []
        for (wname, dstT, nrow, rowW) in (("w_gate", wgT, NI, C), ("w_up", wuT, NI, C),
                                          ("w_down", wdT, 8, I)):
            for r in range(nrow):
                tl = p.dma("sync", w32[:, 0:rowW],
                           io[wname].rearrange("(r p) k -> p r k", p=128)[:, r, :],
                           waits=list(last_w32_read) + (const_toks if r == 0 else []))
                halves = [(0, rowW)] if rowW <= 1408 else [(0, 1408), (1408, rowW - 1408)]
                tq_all = []
                for hi, (o, w_) in enumerate(halves):
                    tq_all.append(emit_quant(p, "vector", w32[:, o:o + w_], wq[:, o:o + w_],
                                  scrV, waits=[tl] + (last_wq_read or []) + tq_all))
                last_w32_read = tq_all
                for kc in range(rowW // 128):
                    bk = kc % 3
                    tt = p.emit("tensor", lambda h, kc=kc, bk=bk: h.transpose(
                        ps_T[bk][:], wq[:, kc * 128:(kc + 1) * 128], ident[:]),
                        waits=tq_all + [lte[bk]], inc="pe")
                    lte[bk] = p.emit("vector", lambda h, kc=kc, dstT=dstT, r=r, bk=bk:
                        h.tensor_copy(dstT[:, kc, r * 128:(r + 1) * 128], ps_T[bk][:]),
                        waits=[tt], inc="dve")
                last_wq_read = [t for t in lte if t]
                wtoks.append(last_wq_read[-1])
        wdone = wtoks[-1]

    S = dict(last_x_read=[None, None], last_xqT_evac=None, last_xq_read=None,
             last_h_read=[None, None], last_hq_read=None, last_ps_g_read=None,
             last_ps_u_read=None, last_ps_d_read=[None, None], last_x2_dma=[None, None],
             last_eb_read=None, last_hgb_read=None, last_hqT_read=None,
             lte=lte)
    carry = {}
    NT = TL // 128

    def stage_A1(t):
        par = t % 2
        hb = hbuf[par]
        tx = p.dma("sync", x32[par][:], io["x1"][t * 128:(t + 1) * 128, :],
                   waits=[S["last_x_read"][par]])
        tq = emit_quant(p, "vector", x32[par][:], xq[:], scrV,
                        waits=[tx, S["last_xq_read"], wdone])
        S["last_x_read"][par] = tq
        xT_toks = []
        for kc in range(8):
            bk = kc % 3
            tt = p.emit("tensor", lambda h, kc=kc, bk=bk: h.transpose(
                ps_T[bk][:], xq[:, kc * 128:(kc + 1) * 128], ident[:]),
                waits=[tq, S["lte"][bk]], inc="pe")
            S["lte"][bk] = p.emit("vector", lambda h, kc=kc, bk=bk: h.tensor_copy(
                xqT[:, kc, :], ps_T[bk][:]), waits=[tt, S["last_xqT_evac"]], inc="dve")
            xT_toks.append(S["lte"][bk])
        S["last_xq_read"] = xT_toks[-1]
        h_toks = []
        for (o, w_) in STRIPS:
            tg_ = None
            tu_ = None
            for kc in range(8):
                tg_ = p.emit("tensor", lambda h, kc=kc, o=o, w_=w_: h.matmul(
                    ps_g[:, 0:w_], xqT[:, kc, :], wgT[:, kc, o:o + w_],
                    start=(kc == 0), stop=(kc == 7)),
                    waits=[xT_toks[-1], S["last_ps_g_read"]] if kc == 0 else [], inc="pe")
            for kc in range(8):
                tu_ = p.emit("tensor", lambda h, kc=kc, o=o, w_=w_: h.matmul(
                    ps_u[:, 0:w_], xqT[:, kc, :], wuT[:, kc, o:o + w_],
                    start=(kc == 0), stop=(kc == 7)),
                    waits=[S["last_ps_u_read"]] if kc == 0 else [], inc="pe")
            S["last_xqT_evac"] = tu_
            te_ = p.emit("scalar", lambda h, o=o, w_=w_: h.activation(
                eb[:, 0:w_], ps_g[:, 0:w_], AF.Sigmoid),
                waits=[tg_, S["last_eb_read"]], inc="act")
            th1 = p.emit("vector", lambda h, o=o, w_=w_: h.tensor_tensor(
                hgb[:, 0:w_], ps_g[:, 0:w_], eb[:, 0:w_], ALU.mult),
                waits=[te_, S["last_hgb_read"]], inc="dve")
            S["last_ps_g_read"] = th1
            S["last_eb_read"] = th1
            th2 = p.emit("vector", lambda h, o=o, w_=w_, hb=hb: h.tensor_mul(
                hb[:, o:o + w_], hgb[:, 0:w_], ps_u[:, 0:w_]),
                waits=[th1, tu_, S["last_h_read"][par]], inc="dve")
            S["last_hgb_read"] = th2
            S["last_ps_u_read"] = th2
            h_toks.append(th2)
        carry[t] = h_toks

    def stage_B(t):
        par = t % 2
        hb = hbuf[par]
        lnA = hb[:, 0:1024]
        ymv = hb[:, 1024:2048]
        h_toks = carry.pop(t)
        tqh_l = []
        for hi, (o, w_) in enumerate(((0, 1408), (1408, 1408))):
            tqh_l.append(emit_quant(p, "vector", hb[:, o:o + w_], hq[:, o:o + w_],
                         scrV,
                         waits=h_toks + ([S["last_hq_read"]] if S["last_hq_read"] else [])
                         + tqh_l))
        tqh = tqh_l[-1]
        hT_toks = []
        for kc in range(NI):
            bk = kc % 3
            tt = p.emit("tensor", lambda h, kc=kc, bk=bk: h.transpose(
                ps_T[bk][:], hq[:, kc * 128:(kc + 1) * 128], ident[:]),
                waits=tqh_l + [S["lte"][bk]], inc="pe")
            S["lte"][bk] = p.emit("vector", lambda h, kc=kc, bk=bk: h.tensor_copy(
                hqT[:, kc, :], ps_T[bk][:]), waits=[tt, S["last_hqT_read"]], inc="dve")
            hT_toks.append(S["lte"][bk])
        S["last_hq_read"] = hT_toks[-1]
        dn_toks = []
        for n in range(2):
            td_ = None
            for kc in range(NI):
                td_ = p.emit("tensor", lambda h, kc=kc, n=n: h.matmul(
                    ps_d[n][:], hqT[:, kc, :], wdT[:, kc, n * 512:(n + 1) * 512],
                    start=(kc == 0), stop=(kc == NI - 1)),
                    waits=[hT_toks[-1], S["last_ps_d_read"][n]] if kc == 0 else [], inc="pe")
            dn_toks.append(td_)
        S["last_hqT_read"] = dn_toks[-1]
        ytoks = []
        for n in range(2):
            ty = p.emit("vector", lambda h, n=n, par=par, lnA=lnA: h.tensor_add(
                lnA[:, n * 512:(n + 1) * 512], ps_d[n][:],
                x32[par][:, n * 512:(n + 1) * 512]), waits=[dn_toks[n]] + tqh_l, inc="dve")
            S["last_ps_d_read"][n] = ty
            S["last_x_read"][par] = ty
            ytoks.append(ty)
        tr = p.emit("vector", lambda h, lnA=lnA: h.reduce_sum(snm[:], lnA, AX.X),
                    waits=ytoks, inc="dve")
        tn = p.emit("vector", lambda h: h.tensor_scalar_mul(negm[:], snm[:], -1.0 / C),
                    waits=[tr], inc="dve")
        tm = p.emit("vector", lambda h, lnA=lnA, ymv=ymv: h.tensor_scalar_add(
            ymv, lnA, negm[:]), waits=[tn], inc="dve")
        tsq1 = p.emit("scalar", lambda h, ymv=ymv: h.activation(
            eb[:, 0:512], ymv[:, 0:512], AF.Square),
            waits=[tm, S["last_eb_read"], ("act", p.counts["act"])], inc="act")
        tsq2 = p.emit("scalar", lambda h, ymv=ymv: h.activation(
            hgb[:, 0:512], ymv[:, 512:1024], AF.Square),
            waits=[tm, S["last_hgb_read"], ("act", p.counts["act"])], inc="act")
        tss1 = p.emit("vector", lambda h: h.reduce_sum(lnv[:], eb[:, 0:512], AX.X),
                      waits=[tsq1], inc="dve")
        tss2 = p.emit("vector", lambda h: h.reduce_sum(nt1[:], hgb[:, 0:512], AX.X),
                      waits=[tsq2], inc="dve")
        S["last_eb_read"] = tss2
        S["last_hgb_read"] = tss2
        tssb = p.emit("vector", lambda h: h.tensor_add(lnv[:], lnv[:], nt1[:]),
                      waits=[tss1, tss2], inc="dve")
        tva = p.emit("vector", lambda h: h.tensor_scalar(
            lnv[:], lnv[:], 1.0 / C, 1e-5, ALU.mult, ALU.add), waits=[tssb], inc="dve")
        tni = p.emit("vector", lambda h: h.tensor_scalar(
            iscr[:], lnv[:].bitcast(mybir.dt.int32), 1, None, ALU.arith_shift_right),
            waits=[tva], inc="dve")
        tny = p.emit("vector", lambda h: h.scalar_tensor_tensor(
            rstd[:].bitcast(mybir.dt.int32), iscr[:], -1, magic[:], ALU.mult, ALU.add),
            waits=[tni, tmagic], inc="dve")
        for _ in range(3):
            ta1 = p.emit("vector", lambda h: h.tensor_mul(nt1[:], rstd[:], rstd[:]),
                         waits=[tny], inc="dve")
            ta2 = p.emit("vector", lambda h: h.tensor_mul(nt1[:], nt1[:], lnv[:]),
                         waits=[ta1], inc="dve")
            ta3 = p.emit("vector", lambda h: h.tensor_scalar(
                nt1[:], nt1[:], -0.5, 1.5, ALU.mult, ALU.add), waits=[ta2], inc="dve")
            tny = p.emit("vector", lambda h: h.tensor_mul(rstd[:], rstd[:], nt1[:]),
                         waits=[ta3], inc="dve")
        tx2 = p.emit("vector", lambda h, par=par, ymv=ymv: h.tensor_scalar(
            x2t[par][:], ymv, rstd[:], None, ALU.mult),
            waits=[tny, S["last_x2_dma"][par]], inc="dve")
        S["last_x2_dma"][par] = p.dma("sync", io["x2"][t * 128:(t + 1) * 128, :],
                                      x2t[par][:], waits=[tx2])
        S["last_xq_read"] = tx2
        S["last_h_read"][par] = tx2

    for t in range(NT):
        stage_A1(t)
        if t >= 1:
            stage_B(t - 1)
    stage_B(NT - 1)

    p.emit("gpsimd", lambda h: h.engine_nop(), waits=p.dma_all_done_waits())


def build_D():
    nc = bass.Bass("TRN2", target_bir_lowering=False, debug=False)
    I = 2816
    io = {
        "x1": nc.dram_tensor("x1", [TL, C], F32, kind="ExternalInput").ap(),
        "w_gate": nc.dram_tensor("w_gate", [I, C], F32, kind="ExternalInput").ap(),
        "w_up": nc.dram_tensor("w_up", [I, C], F32, kind="ExternalInput").ap(),
        "w_down": nc.dram_tensor("w_down", [C, I], F32, kind="ExternalInput").ap(),
        "x2": nc.dram_tensor("x2", [TL, C], F32, kind="ExternalOutput").ap(),
    }
    finish_build(nc, emit_D, io)
    return nc




WQR = 2304
NWIN = 8


def emit_F1(p, ctx, nc, io):
    pref = io.get("prefix", "")
    ident, _, const_toks = build_common_consts(p, ctx, nc, pref)
    sb = lambda name, shape, dt=BF16: ctx.enter_context(nc.sbuf_tensor(pref + name, shape, dt))
    ps = lambda name, shape, dt=F32: ctx.enter_context(nc.psum_tensor(pref + name, shape, dt))

    w16 = sb("w16", [128, 8, 3072])
    w32t = sb("w32t", [128, 3072], F32)
    bqT = sb("bqT", [128, 24], F32)
    bq24 = sb("bq24", [24, 128], F32)
    ident32 = sb("ident32", [128, 128], F32)
    bvbc = sb("bvbc", [128, 1024], F32)
    xc32 = [sb(f"xc32_{i}", [128, 8, 512], F32) for i in range(2)]
    x16 = [sb(f"x16_{i}", [128, 8, 512]) for i in range(2)]
    qko = sb("qko", [128, 512])
    vo = sb("vo", [128, 1024], F32)
    vo16 = sb("vo16", [128, 1024])

    ps_mm = [ps(f"ps_mm{i}", [128, 512]) for i in range(2)]
    ps_b = ps("ps_b", [128, 24])

    wtoks = []
    last_cast = None
    for kc in range(8):
        tl = p.dma("sync", w32t[:], io["w_qkv"].rearrange("(kc p) n -> p kc n", p=128)[:, kc, :],
                   waits=[last_cast])
        last_cast = p.emit("vector", lambda h, kc=kc: h.tensor_copy(w16[:, kc, :], w32t[:]),
                           waits=[tl], inc="dve")
        wtoks.append(last_cast)
    ti = p.emit("gpsimd", lambda h: h.memset(ident32[:], 1.0), inc="st_g")
    ti2 = p.emit("gpsimd", lambda h: h.affine_select(
        ident32[:], ident32[:], [[1, 128]], ALU.is_equal, 0.0, channel_multiplier=-1),
        waits=[ti], inc="st_g")
    tb = p.dma("sync", bq24[:], io["b_qkvT"])
    tb2 = p.emit("tensor", lambda h: h.transpose(ps_b[:], bq24[:], ident32[0:24, 0:24]),
                 waits=[tb, ti2], inc="pe")
    tb3 = p.emit("vector", lambda h: h.tensor_copy(bqT[:], ps_b[:]), waits=[tb2], inc="dve")
    tbv = p.dma("sync", bvbc[:], io["b_qkvT"].rearrange("a b -> (a b)")[2048:3072]
                .unsqueeze(0).broadcast_to([128, 1024]))
    pre = [wtoks[-1], tb3, tbv] + const_toks

    CHUNKS = [(o, min(512, WQR - o)) for o in range(0, WQR, 512)]
    wqT_t = io["wqT"].rearrange("(kc p) t -> p kc t", p=128)
    last_mm_read_x = [None, None]
    last_evac = [None, None]
    last_qko_read = None
    last_vo_read = None
    last_v16_read = None
    wq_bf16 = io["wqT"].dtype == BF16
    for ci, (o, w_) in enumerate(CHUNKS):
        par = ci % 2
        if wq_bf16:
            tcast = p.dma("sync", x16[par][:, :, 0:w_], wqT_t[:, :, o:o + w_],
                          waits=[last_mm_read_x[par]] + (pre if ci == 0 else []))
        else:
            tl = p.dma("sync", xc32[par][:, :, 0:w_], wqT_t[:, :, o:o + w_],
                       waits=[last_mm_read_x[par]] + (pre if ci == 0 else []))
            tcast = p.emit("vector", lambda h, par=par, w_=w_: h.tensor_copy(
                x16[par][:, :, 0:w_], xc32[par][:, :, 0:w_]), waits=[tl], inc="dve")
        for j in range(16):
            i = j % 2
            t = None
            for kc in range(8):
                t = p.emit("tensor", lambda h, kc=kc, j=j, i=i, par=par, w_=w_: h.matmul(
                    ps_mm[i][:, 0:w_], w16[:, kc, j * 128:(j + 1) * 128],
                    x16[par][:, kc, 0:w_], start=(kc == 0), stop=(kc == 7)),
                    waits=[tcast, last_evac[i]] if kc == 0 else [], inc="pe")
            last_mm_read_x[par] = t
            tev = p.emit("scalar", lambda h, i=i, j=j, w_=w_: h.activation(
                qko[:, 0:w_], ps_mm[i][:, 0:w_], AF.Identity, bias=bqT[:, j:j + 1]),
                waits=[t, last_qko_read], inc="act")
            last_evac[i] = tev
            last_qko_read = p.dma("gpsimd", io["qk2"][j * 128:(j + 1) * 128, o:o + w_],
                                  qko[:, 0:w_], waits=[tev])
        for tt_ in range(w_ // 128):
            gl = (o + tt_ * 128) // 128
            for n in range(2):
                i = n % 2
                t = None
                for kc in range(8):
                    t = p.emit("tensor", lambda h, kc=kc, n=n, i=i, par=par, tt_=tt_: h.matmul(
                        ps_mm[i][:, :], x16[par][:, kc, tt_ * 128:(tt_ + 1) * 128],
                        w16[:, kc, 2048 + n * 512:2048 + (n + 1) * 512],
                        start=(kc == 0), stop=(kc == 7)),
                        waits=[tcast, last_evac[i]] if kc == 0 else [], inc="pe")
                last_mm_read_x[par] = t
                tev = p.emit("vector", lambda h, n=n, i=i: h.tensor_add(
                    vo[:, n * 512:(n + 1) * 512], ps_mm[i][:], bvbc[:, n * 512:(n + 1) * 512]),
                    waits=[t, tbv, last_vo_read], inc="dve")
                last_evac[i] = tev
            tv16 = p.emit("vector", lambda h: h.tensor_copy(vo16[:], vo[:]),
                          waits=[last_evac[0], last_evac[1], last_v16_read], inc="dve")
            last_vo_read = tv16
            last_v16_read = p.dma("gpsimd", io["vv"][gl * 128:(gl + 1) * 128, :], vo16[:],
                                  waits=[tv16])

    p.emit("gpsimd", lambda h: h.engine_nop(), waits=p.dma_all_done_waits())


def build_F1():
    nc = bass.Bass("TRN2", target_bir_lowering=False, debug=False)
    io = {
        "wqT": nc.dram_tensor("wqT", [C, WQR], F32, kind="ExternalInput").ap(),
        "w_qkv": nc.dram_tensor("w_qkv", [C, 3 * C], F32, kind="ExternalInput").ap(),
        "b_qkvT": nc.dram_tensor("b_qkvT", [24, 128], F32, kind="ExternalInput").ap(),
        "qk2": nc.dram_tensor("qk2", [2048, WQR], BF16, kind="ExternalOutput").ap(),
        "vv": nc.dram_tensor("vv", [WQR, C], BF16, kind="ExternalOutput").ap(),
    }
    finish_build(nc, emit_F1, io)
    return nc


def emit_F2(p, ctx, nc, io):
    pref = io.get("prefix", "")
    ident, _, const_toks = build_common_consts(p, ctx, nc, pref)
    sb = lambda name, shape, dt=BF16: ctx.enter_context(nc.sbuf_tensor(pref + name, shape, dt))
    ps = lambda name, shape, dt=F32: ctx.enter_context(nc.psum_tensor(pref + name, shape, dt))

    w16o = sb("w16o", [128, 8, 1024])
    w32t = sb("w32t", [128, 1024], F32)
    bout_bc = sb("bout_bc", [128, 1024], F32)
    qk_all = [sb(f"qk_all{i}", [128, 16, 512]) for i in range(2)]
    v_all = [sb(f"v_all{i}", [128, 4, 1024]) for i in range(2)]
    v1b = [sb(f"v1b_{i}", [128, 4, 65]) for i in range(2)]
    Ebuf = sb("Ebuf", [128, 4, 512])
    af = sb("af", [128, 8, 512])
    obuf = [sb(f"obuf{i}", [128, 64]) for i in range(2)]
    den = [sb(f"den{i}", [128, 1], F32) for i in range(2)]
    rden = [sb(f"rden{i}", [128, 1], F32) for i in range(2)]
    xt32 = [sb(f"xt32_{i}", [128, 1024], F32) for i in range(2)]
    yt = sb("yt", [128, 1024], F32)
    ym = sb("ym", [128, 1024], F32)
    sqt = sb("sqt", [128, 1024], F32)
    snm = sb("snm", [128, 1], F32)
    negm = sb("negm", [128, 1], F32)
    lnv = sb("lnv", [128, 1], F32)
    rstd = sb("rstd", [128, 1], F32)
    x3t = [sb(f"x3t_{i}", [128, 1024], F32) for i in range(2)]

    want_x3T = "x3T16" in io
    if want_x3T:
        x3b = sb("x3b", [128, 1024])
        xTst = sb("xTst", [128, 8, 128])
    ps_S = [ps(f"ps_S{i}", [128, 512]) for i in range(2)]
    ps_ctx = [ps(f"ps_ctx{i}", [128, 65]) for i in range(2)]
    ps_T = [ps(f"ps_T{i}", [128, 128], BF16) for i in range(2)]
    ps_o = [ps(f"ps_o{i}", [128, 512]) for i in range(2)]

    last_cast = None
    wtoks = []
    for kc in range(8):
        tl = p.dma("sync", w32t[:], io["w_out"].rearrange("(kc p) n -> p kc n", p=128)[:, kc, :],
                   waits=[last_cast])
        last_cast = p.emit("vector", lambda h, kc=kc: h.tensor_copy(w16o[:, kc, :], w32t[:]),
                           waits=[tl], inc="dve")
        wtoks.append(last_cast)
    tbb = p.dma("sync", bout_bc[:], io["b_out"].unsqueeze(0).broadcast_to([128, C]))
    tv1 = [p.emit("gpsimd", lambda h, i=i, tb=tb: h.memset(v1b[i][:, tb, 64:65], 1.0), inc="st_g")
           for i in range(2) for tb in range(4)]
    pre = [wtoks[-1], tbb] + tv1 + const_toks

    last_ctx_div = [None, None]
    last_S_evac = [None, None]
    last_E_read = None
    last_v1b_read = [None, None]
    last_obuf_read = [None, None]
    last_T_evac = [None, None]
    last_af_read = None
    last_ps_o_read = [None, None]
    last_x_read = [None, None]
    last_x3_dma = [None, None]
    last_qk_read = [None, None]
    last_vall_read = [None, None]
    last_x3b_read = None
    last_xTst_dma = None

    for n in range(NWIN):
        par = n % 2
        tq = p.dma("sync", qk_all[par][:],
                   io["qk2"].rearrange("(nt p) t -> p nt t", p=128)[:, :, n * 256:n * 256 + 512],
                   waits=[last_qk_read[par]] + (pre if n == 0 else []))
        tv = p.dma("sync", v_all[par][:],
                   io["vv"].rearrange("(tb p) d -> p tb d", p=128)[:, 2 * n:2 * n + 4, :]
                   if False else
                   io["vv"][n * 256:n * 256 + 512, :].rearrange("(tb p) d -> p tb d", p=128),
                   waits=[last_vall_read[par]])
        af_evacs = []
        t0 = None
        for hh in range(16):
            hp = hh % 2
            qrow = 64 * (hh % 2)
            qnt = hh // 2
            knt = 8 + hh // 2
            vcp = []
            for tb in range(4):
                vcp.append(p.emit("vector", lambda h, hp=hp, tb=tb, par=par, hh=hh: h.tensor_copy(
                    v1b[hp][:, tb, 0:64], v_all[par][:, tb, hh * 64:(hh + 1) * 64]),
                    waits=[tv, last_v1b_read[hp]] + tv1, inc="dve"))
            last_vall_read[par] = vcp[-1]
            ev_toks = []
            for i in range(4):
                sp_ = i % 2
                tS = p.emit("tensor", lambda h, i=i, sp_=sp_, hh=hh, par=par, qrow=qrow,
                            qnt=qnt, knt=knt: h.matmul(
                    ps_S[sp_][:], qk_all[par][qrow:qrow + 64, knt, i * 128:(i + 1) * 128],
                    qk_all[par][qrow:qrow + 64, qnt, :], start=True, stop=True),
                    waits=[tq] + (last_S_evac[sp_] or []), inc="pe")
                te_ = p.emit("scalar", lambda h, i=i, sp_=sp_: h.activation(
                    Ebuf[:, i, :], ps_S[sp_][:], AF.Exp, scale=0.125),
                    waits=[tS, last_E_read], inc="act")
                ev_toks.append(te_)
                last_S_evac[sp_] = [te_]
            for j in range(4):
                cp_ = j % 2
                t0 = None
                for i in range(4):
                    t0 = p.emit("tensor", lambda h, i=i, j=j, cp_=cp_, hp=hp: h.matmul(
                        ps_ctx[cp_][:], Ebuf[:, i, j * 128:(j + 1) * 128], v1b[hp][:, i, :],
                        start=(i == 0), stop=(i == 3)),
                        waits=(ev_toks + vcp + [last_ctx_div[cp_]]) if i == 0 else [], inc="pe")
                last_E_read = t0
                last_v1b_read[hp] = t0
                bk = j % 2
                td1 = p.emit("vector", lambda h, cp_=cp_, bk=bk: h.tensor_scalar_max(
                    den[bk][:], ps_ctx[cp_][:, 64:65], 1e-30), waits=[t0], inc="dve")
                td2 = p.emit("vector", lambda h, bk=bk: h.reciprocal(rden[bk][:], den[bk][:]),
                             waits=[td1], inc="dve")
                td3 = p.emit("vector", lambda h, cp_=cp_, bk=bk: h.tensor_scalar(
                    obuf[bk][:], ps_ctx[cp_][:, 0:64], rden[bk][:], None, ALU.mult),
                    waits=[td2, last_obuf_read[bk]], inc="dve")
                last_ctx_div[cp_] = td3
                tT = p.emit("tensor", lambda h, bk=bk: h.transpose(
                    ps_T[bk][0:64, :], obuf[bk][:], ident[:]),
                    waits=[td3, last_T_evac[bk]], inc="pe")
                last_obuf_read[bk] = tT
                te = p.emit("vector", lambda h, hh=hh, j=j, bk=bk: h.tensor_copy(
                    af[64 * (hh % 2):64 * (hh % 2) + 64, hh // 2, j * 128:(j + 1) * 128],
                    ps_T[bk][0:64, :]), waits=[tT, last_af_read], inc="dve")
                last_T_evac[bk] = te
                af_evacs.append(te)
        last_qk_read[par] = t0
        for j in range(4):
            xp = j % 2
            row0 = n * 512 + j * 128
            tx = p.dma("sync", xt32[xp][:], io["x2own"][row0:row0 + 128, :],
                       waits=[last_x_read[xp]])
            ytoks = []
            for nn in range(2):
                tmo = None
                for kc in range(8):
                    tmo = p.emit("tensor", lambda h, kc=kc, j=j, nn=nn: h.matmul(
                        ps_o[nn][:], af[:, kc, j * 128:(j + 1) * 128],
                        w16o[:, kc, nn * 512:(nn + 1) * 512],
                        start=(kc == 0), stop=(kc == 7)),
                        waits=(af_evacs + [last_ps_o_read[nn]]) if kc == 0 else [], inc="pe")
                last_af_read = tmo
                ty1 = p.emit("vector", lambda h, nn=nn, xp=xp: h.tensor_add(
                    yt[:, nn * 512:(nn + 1) * 512], ps_o[nn][:],
                    xt32[xp][:, nn * 512:(nn + 1) * 512]), waits=[tmo, tx], inc="dve")
                last_x_read[xp] = ty1
                ty2 = p.emit("vector", lambda h, nn=nn: h.tensor_add(
                    yt[:, nn * 512:(nn + 1) * 512], yt[:, nn * 512:(nn + 1) * 512],
                    bout_bc[:, nn * 512:(nn + 1) * 512]), waits=[ty1, tbb], inc="dve")
                last_ps_o_read[nn] = ty1
                ytoks.append(ty2)
            tr = p.emit("vector", lambda h: h.reduce_sum(snm[:], yt[:], AX.X),
                        waits=ytoks, inc="dve")
            tn = p.emit("vector", lambda h: h.tensor_scalar_mul(negm[:], snm[:], -1.0 / C),
                        waits=[tr], inc="dve")
            tm = p.emit("vector", lambda h: h.tensor_scalar_add(ym[:], yt[:], negm[:]),
                        waits=[tn], inc="dve")
            tsq = p.emit("scalar", lambda h: h.activation(sqt[:], ym[:], AF.Square),
                         waits=[tm], inc="act")
            tss = p.emit("vector", lambda h: h.reduce_sum(lnv[:], sqt[:], AX.X),
                         waits=[tsq], inc="dve")
            tva = p.emit("vector", lambda h: h.tensor_scalar(
                lnv[:], lnv[:], 1.0 / C, 1e-5, ALU.mult, ALU.add), waits=[tss], inc="dve")
            tln = p.emit("scalar", lambda h: h.activation(lnv[:], lnv[:], AF.Ln),
                         waits=[tva], inc="act")
            trs = p.emit("scalar", lambda h: h.activation(rstd[:], lnv[:], AF.Exp, scale=-0.5),
                         waits=[tln], inc="act")
            tx3 = p.emit("vector", lambda h, xp=xp: h.tensor_scalar(
                x3t[xp][:], ym[:], rstd[:], None, ALU.mult),
                waits=[trs, last_x3_dma[xp]], inc="dve")
            last_x3_dma[xp] = p.dma("gpsimd", io["x3"][row0:row0 + 128, :], x3t[xp][:],
                                    waits=[tx3])
            if want_x3T:
                tc16 = p.emit("vector", lambda h, xp=xp: h.tensor_copy(x3b[:], x3t[xp][:]),
                              waits=[tx3] + (last_x3b_read or []), inc="dve")
                last_x3_dma[xp] = tc16
                evs = []
                for kc in range(8):
                    bk2 = kc % 2
                    ttx = p.emit("tensor", lambda h, kc=kc, bk2=bk2: h.transpose(
                        ps_T[bk2][:], x3b[:, kc * 128:(kc + 1) * 128], ident[:]),
                        waits=[tc16, last_T_evac[bk2]], inc="pe")
                    last_T_evac[bk2] = p.emit("vector", lambda h, kc=kc, bk2=bk2: h.tensor_copy(
                        xTst[:, kc, :], ps_T[bk2][:]), waits=[ttx, last_xTst_dma], inc="dve")
                    evs.append(last_T_evac[bk2])
                last_x3b_read = [t for t in last_T_evac if t]
                last_xTst_dma = p.dma("gpsimd",
                    io["x3T16"].rearrange("(kc p) t -> p kc t", p=128)[:, :, row0:row0 + 128],
                    xTst[:], waits=evs)

    p.emit("gpsimd", lambda h: h.engine_nop(), waits=p.dma_all_done_waits())


def build_F2():
    nc = bass.Bass("TRN2", target_bir_lowering=False, debug=False)
    io = {
        "qk2": nc.dram_tensor("qk2", [2048, WQR], BF16, kind="ExternalInput").ap(),
        "vv": nc.dram_tensor("vv", [WQR, C], BF16, kind="ExternalInput").ap(),
        "w_out": nc.dram_tensor("w_out", [C, C], F32, kind="ExternalInput").ap(),
        "b_out": nc.dram_tensor("b_out", [C], F32, kind="ExternalInput").ap(),
        "x2own": nc.dram_tensor("x2own", [TL, C], F32, kind="ExternalInput").ap(),
        "x3": nc.dram_tensor("x3", [TL, C], F32, kind="ExternalOutput").ap(),
    }
    finish_build(nc, emit_F2, io)
    return nc


def emit_H(p, ctx, nc, io):
    pref = io.get("prefix", "")
    sb = lambda name, shape, dt=BF16: ctx.enter_context(nc.sbuf_tensor(pref + name, shape, dt))
    ps = lambda name, shape, dt=F32: ctx.enter_context(nc.psum_tensor(pref + name, shape, dt))

    swT = sb("swT", [128, 8, 1024])
    w32t = sb("w32t", [128, 1024], F32)
    sc_bc = sb("sc_bc", [128, 1024], F32)
    bi_bc = sb("bi_bc", [128, 1024], F32)
    xc32 = [sb(f"xc32_{i}", [128, 8, 128], F32) for i in range(2)]
    x16 = [sb(f"x16_{i}", [128, 8, 128]) for i in range(2)]
    xt32 = [sb(f"xt32_{i}", [128, 1024], F32) for i in range(2)]
    yt = [sb(f"yt{i}", [128, 1024], F32) for i in range(2)]
    ym = [sb(f"ym{i}", [128, 1024], F32) for i in range(2)]
    sqt = [sb(f"sqt{i}", [128, 1024], F32) for i in range(2)]
    snm = [sb(f"snm{i}", [128, 1], F32) for i in range(2)]
    negm = [sb(f"negm{i}", [128, 1], F32) for i in range(2)]
    lnv = [sb(f"lnv{i}", [128, 1], F32) for i in range(2)]
    rstd = [sb(f"rstd{i}", [128, 1], F32) for i in range(2)]
    ot = [sb(f"ot_{i}", [128, 1024], F32) for i in range(2)]

    ps_o = [ps(f"ps_o{i}", [128, 512]) for i in range(2)]

    wtoks = []
    last_sw = None
    for kc in range(8):
        tl = p.dma("sync", w32t[:], io["weightsT"].rearrange("(kc p) n -> p kc n", p=128)[:, kc, :],
                   waits=[last_sw])
        t1 = p.emit("vector", lambda h, kc=kc: h.tensor_scalar(
            w32t[:], w32t[:], 0.0, 2.0, ALU.is_ge, ALU.mult), waits=[tl], inc="dve")
        last_sw = p.emit("vector", lambda h, kc=kc: h.tensor_scalar_add(
            swT[:, kc, :], w32t[:], -1.0), waits=[t1], inc="dve")
        wtoks.append(last_sw)
    tsc = p.dma("sync", sc_bc[:], io["scale"].unsqueeze(0).broadcast_to([128, C]))
    tbi = p.dma("sync", bi_bc[:], io["bias"].unsqueeze(0).broadcast_to([128, C]))
    pre = [wtoks[-1], tsc, tbi]

    x3T_t = io["x3T"].rearrange("(kc p) t -> p kc t", p=128)
    x3T_bf16 = io["x3T"].dtype == BF16
    last_x16_read = [None, None]
    last_x_read = [None, None]
    last_o_dma = [None, None]
    last_ps_o_read = [None, None]
    last_yt_read = [None, None]
    NT = TL // 128
    for t in range(NT):
        par = t % 2
        if x3T_bf16:
            tcast = p.dma("sync", x16[par][:], x3T_t[:, :, t * 128:(t + 1) * 128],
                          waits=[last_x16_read[par]] + (pre if t == 0 else []))
        else:
            tl = p.dma("sync", xc32[par][:], x3T_t[:, :, t * 128:(t + 1) * 128],
                       waits=[last_x16_read[par]] + (pre if t == 0 else []))
            tcast = p.emit("vector", lambda h, par=par: h.tensor_copy(
                x16[par][:].rearrange("p a b -> p (a b)"),
                xc32[par][:].rearrange("p a b -> p (a b)")),
                waits=[tl], inc="dve")
        tx = p.dma("sync", xt32[par][:], io["x3"][t * 128:(t + 1) * 128, :],
                   waits=[last_x_read[par]])
        ytoks = []
        tmo = None
        for nn in range(2):
            for kc in range(8):
                tmo = p.emit("tensor", lambda h, kc=kc, nn=nn, par=par: h.matmul(
                    ps_o[nn][:], x16[par][:, kc, :], swT[:, kc, nn * 512:(nn + 1) * 512],
                    start=(kc == 0), stop=(kc == 7)),
                    waits=[tcast, last_ps_o_read[nn]] if kc == 0 else [], inc="pe")
            ty1 = p.emit("vector", lambda h, nn=nn, par=par: h.tensor_mul(
                yt[par][:, nn * 512:(nn + 1) * 512], ps_o[nn][:],
                sc_bc[:, nn * 512:(nn + 1) * 512]),
                waits=[tmo, tsc, last_yt_read[par]], inc="dve")
            last_ps_o_read[nn] = ty1
            ty2 = p.emit("vector", lambda h, nn=nn, par=par: h.tensor_add(
                yt[par][:, nn * 512:(nn + 1) * 512], yt[par][:, nn * 512:(nn + 1) * 512],
                bi_bc[:, nn * 512:(nn + 1) * 512]), waits=[ty1, tbi], inc="dve")
            ty3 = p.emit("vector", lambda h, nn=nn, par=par: h.tensor_scalar(
                yt[par][:, nn * 512:(nn + 1) * 512], yt[par][:, nn * 512:(nn + 1) * 512],
                100.0, -100.0, ALU.min, ALU.max), waits=[ty2], inc="dve")
            ty4 = p.emit("vector", lambda h, nn=nn, par=par: h.tensor_add(
                yt[par][:, nn * 512:(nn + 1) * 512], yt[par][:, nn * 512:(nn + 1) * 512],
                xt32[par][:, nn * 512:(nn + 1) * 512]), waits=[ty3, tx], inc="dve")
            ytoks.append(ty4)
        last_x16_read[par] = tmo
        last_x_read[par] = ytoks[-1]
        tr = p.emit("vector", lambda h, par=par: h.reduce_sum(snm[par][:], yt[par][:], AX.X),
                    waits=ytoks, inc="dve")
        tn = p.emit("vector", lambda h, par=par: h.tensor_scalar_mul(
            negm[par][:], snm[par][:], -1.0 / C), waits=[tr], inc="dve")
        tm = p.emit("vector", lambda h, par=par: h.tensor_scalar_add(
            ym[par][:], yt[par][:], negm[par][:]), waits=[tn], inc="dve")
        tsq = p.emit("scalar", lambda h, par=par: h.activation(
            sqt[par][:], ym[par][:], AF.Square),
            waits=[tm, ("act", p.counts["act"])], inc="act")
        tss = p.emit("vector", lambda h, par=par: h.reduce_sum(
            lnv[par][:], sqt[par][:], AX.X), waits=[tsq], inc="dve")
        tva = p.emit("vector", lambda h, par=par: h.tensor_scalar(
            lnv[par][:], lnv[par][:], 1.0 / C, 1e-5, ALU.mult, ALU.add),
            waits=[tss], inc="dve")
        tln = p.emit("scalar", lambda h, par=par: h.activation(
            lnv[par][:], lnv[par][:], AF.Ln),
            waits=[tva, ("act", p.counts["act"])], inc="act")
        trs = p.emit("scalar", lambda h, par=par: h.activation(
            rstd[par][:], lnv[par][:], AF.Exp, scale=-0.5), waits=[tln], inc="act")
        to_ = p.emit("vector", lambda h, par=par: h.tensor_scalar(
            ot[par][:], ym[par][:], rstd[par][:], None, ALU.mult),
            waits=[trs, last_o_dma[par]], inc="dve")
        last_yt_read[par] = to_
        last_o_dma[par] = p.dma("gpsimd", io["out"][t * 128:(t + 1) * 128, :], ot[par][:],
                                waits=[to_])

    p.emit("gpsimd", lambda h: h.engine_nop(), waits=p.dma_all_done_waits())


def build_H():
    nc = bass.Bass("TRN2", target_bir_lowering=False, debug=False)
    io = {
        "x3": nc.dram_tensor("x3", [TL, C], F32, kind="ExternalInput").ap(),
        "x3T": nc.dram_tensor("x3T", [C, TL], F32, kind="ExternalInput").ap(),
        "weightsT": nc.dram_tensor("weightsT", [C, C], F32, kind="ExternalInput").ap(),
        "scale": nc.dram_tensor("scale", [C], F32, kind="ExternalInput").ap(),
        "bias": nc.dram_tensor("bias", [C], F32, kind="ExternalInput").ap(),
        "out": nc.dram_tensor("out", [TL, C], F32, kind="ExternalOutput").ap(),
    }
    finish_build(nc, emit_H, io)
    return nc




def emit_E1(p, ctx, nc, io):
    pref = io.get("prefix", "")
    sb = lambda name, shape, dt=F32: ctx.enter_context(nc.sbuf_tensor(pref + name, shape, dt))
    m0 = sb("m0", [64, 1])
    m1 = sb("m1", [64, 1])
    stb = sb("stb", [64, 16, 65])
    cb = sb("cb", [64, 16, 65])
    tm0 = p.dma("sync", m0[:], io["msk"][0:1].unsqueeze(0).broadcast_to([64, 1]))
    tm1 = p.dma("sync", m1[:], io["msk"][1:2].unsqueeze(0).broadcast_to([64, 1]))
    tl = p.dma("sync", stb[:], io["stot"].rearrange("h f e -> f h e"))
    t1 = p.emit("vector", lambda h: h.tensor_scalar(
        cb[:].rearrange("p a b -> p (a b)"), stb[:].rearrange("p a b -> p (a b)"),
        m0[:], None, ALU.mult), waits=[tm0, tl], inc="dve")
    t2 = p.dma("gpsimd", io["e1in"].ap(), cb[:].rearrange("p a b -> p (a b)"), waits=[t1])
    t3 = p.emit("gpsimd", lambda h: h.collective_compute(
        "AllReduce", ALU.add, replica_groups=[[0, 1], [2, 3], [4, 5], [6, 7]],
        ins=[io["e1in"].ap().opt()], outs=[io["e1out"].ap().opt()]),
        waits=[t2], inc="cc")
    t4 = p.dma("sync", cb[:].rearrange("p a b -> p (a b)"), io["e1out"].ap(), waits=[t3, t1])
    t5 = p.emit("vector", lambda h: h.tensor_scalar(
        stb[:].rearrange("p a b -> p (a b)"), cb[:].rearrange("p a b -> p (a b)"),
        m1[:], None, ALU.mult), waits=[tm1, t4], inc="dve")
    p.dma("gpsimd", io["init"].rearrange("h f e -> f h e"), stb[:], waits=[t5])


def emit_E2(p, ctx, nc, io):
    pref = io.get("prefix", "")
    sb = lambda name, shape, dt=F32: ctx.enter_context(nc.sbuf_tensor(pref + name, shape, dt))
    ps = lambda name, shape, dt=F32: ctx.enter_context(nc.psum_tensor(pref + name, shape, dt))
    ident = ctx.enter_context(nc.sbuf_tensor(pref + "ident", [128, 128], BF16))
    ti = p.emit("gpsimd", lambda h: h.memset(ident[:], 1.0), inc="st_g")
    ti2 = p.emit("gpsimd", lambda h: h.affine_select(
        ident[:], ident[:], [[1, 128]], ALU.is_equal, 0.0, channel_multiplier=-1),
        waits=[ti], inc="st_g")
    m0 = sb("m0", [128, 1])
    m1 = sb("m1", [128, 1])
    xa = [sb(f"xa{i}", [128, 1024]) for i in range(2)]
    xb = [sb(f"xb{i}", [128, 1024]) for i in range(2)]
    rb16 = [sb(f"rb16_{i}", [128, 1024], BF16) for i in range(2)]
    cb16 = sb("cb16", [128, 1024], BF16)
    wq32 = sb("wq32", [128, 1024])
    wq16 = sb("wq16", [128, 1024], BF16)
    xTst = sb("xTst", [128, 8, 128], BF16)
    ps_T = ps("ps_T", [128, 128], BF16)
    tm0 = p.dma("sync", m0[:], io["msk"][0:1].unsqueeze(0).broadcast_to([128, 1]))
    tm1 = p.dma("sync", m1[:], io["msk"][1:2].unsqueeze(0).broadcast_to([128, 1]))
    last_c = [None, None]
    last_cb = None
    cdma = []
    for i in range(16):
        pp = i % 2
        tl = p.dma("sync", xa[pp][:], io["x2"][2048 + i * 128:2048 + (i + 1) * 128, :],
                   waits=[last_c[pp]])
        tc = p.emit("vector", lambda h, pp=pp: h.tensor_scalar(
            cb16[:], xa[pp][:], m0[:], None, ALU.mult), waits=[tl, tm0, last_cb], inc="dve")
        last_c[pp] = tc
        last_cb = p.dma("gpsimd", io["e2in"][i * 128:(i + 1) * 128, :], cb16[:], waits=[tc])
        cdma.append(last_cb)
    t3 = p.emit("gpsimd", lambda h: h.collective_compute(
        "AllReduce", ALU.add, replica_groups=[[0, 1], [2, 3], [4, 5], [6, 7]],
        ins=[io["e2in"].opt()], outs=[io["e2out"].opt()]),
        waits=cdma, inc="cc")
    last_T_evac = None
    last_xTst_dma = None
    last_rb = [None, None]
    last_wq16_read = None
    tw = None
    for i in range(18):
        pp = i % 2
        tla = p.dma("sync", xa[pp][:], io["x2"][i * 128:(i + 1) * 128, :],
                    waits=[last_c[pp]])
        t1 = p.emit("vector", lambda h, pp=pp: h.tensor_scalar(
            wq32[:], xa[pp][:], m0[:], None, ALU.mult),
            waits=[tla, tm0, tw], inc="dve")
        last_c[pp] = t1
        if i < 16:
            tlb = p.dma("sync", rb16[pp][:], io["e2out"][i * 128:(i + 1) * 128, :],
                        waits=[t3, last_rb[pp]])
            t2 = p.emit("vector", lambda h, pp=pp: h.scalar_tensor_tensor(
                wq32[:], rb16[pp][:], m1[:], wq32[:], ALU.mult, ALU.add),
                waits=[tlb, t1, tm1], inc="dve")
            last_rb[pp] = t2
        else:
            tlb = p.dma("sync", xb[pp][:], io["x2"][(i - 16) * 128:(i - 15) * 128, :],
                        waits=[last_rb[pp]])
            t2 = p.emit("vector", lambda h, pp=pp: h.scalar_tensor_tensor(
                wq32[:], xb[pp][:], m1[:], wq32[:], ALU.mult, ALU.add),
                waits=[tlb, t1, tm1], inc="dve")
            last_rb[pp] = t2
        tw = p.emit("vector", lambda h: h.tensor_copy(wq16[:], wq32[:]),
                    waits=[t2, last_wq16_read], inc="dve")
        evs = []
        for kc in range(8):
            ttx = p.emit("tensor", lambda h, kc=kc: h.transpose(
                ps_T[:], wq16[:, kc * 128:(kc + 1) * 128], ident[:]),
                waits=[tw, ti2, last_T_evac], inc="pe")
            last_T_evac = p.emit("vector", lambda h, kc=kc: h.tensor_copy(
                xTst[:, kc, :], ps_T[:]), waits=[ttx, last_xTst_dma], inc="dve")
            evs.append(last_T_evac)
        last_wq16_read = last_T_evac
        last_xTst_dma = p.dma("gpsimd",
            io["wqT16"].rearrange("(kc p) t -> p kc t", p=128)[:, :, i * 128:(i + 1) * 128],
            xTst[:], waits=evs)
    p.emit("gpsimd", lambda h: h.engine_nop(), waits=p.dma_all_done_waits())


def build_A():
    nc = bass.Bass("TRN2", target_bir_lowering=False, debug=False)
    io = {
        "xT": nc.dram_tensor("xT", [C, TL], F32, kind="ExternalInput").ap(),
        "Wqkv": nc.dram_tensor("Wqkv", [C, 3 * C], F32, kind="ExternalInput").ap(),
        "bqkvT": nc.dram_tensor("bqkvT", [24, 128], F32, kind="ExternalInput").ap(),
        "omh": nc.dram_tensor("omh", [FD, H * FD], F32, kind="ExternalInput").ap(),
        "qn": nc.dram_tensor("qn", [2], F32, kind="ExternalInput").ap(),
        "qw": nc.dram_tensor("qw", [2], F32, kind="ExternalInput").ap(),
        "qf": nc.dram_tensor("qf", [NCH, FD, H, CH], BF16, kind="ExternalOutput").ap(),
        "kf": nc.dram_tensor("kf", [NCH, FD, H, CH], BF16, kind="ExternalOutput").ap(),
        "vt": nc.dram_tensor("vt", [H, NCH, 128, 4, HD], BF16, kind="ExternalOutput").ap(),
        "sprev": nc.dram_tensor("sprev", [NCH - 1, H, FD, 65], F32, kind="ExternalOutput").ap(),
        "stot": nc.dram_tensor("stot", [H, FD, 65], F32, kind="ExternalOutput").ap(),
    }
    finish_build(nc, emit_A, io)
    return nc


def finish_build_multi(nc, phases):
    import contextlib
    with contextlib.ExitStack() as outer:
        p = Prog(nc)
        for s in ([f"ld{i}" for i in range(Prog.LD_POOL)]
                  + [f"st{i}" for i in range(Prog.ST_POOL)]
                  + ["st_g", "pe", "dve", "act", "cc"]):
            p.sem(s)
        sems = {}
        for s in p.sem_names:
            sems[s] = outer.enter_context(nc.semaphore(s))
        block = outer.enter_context(nc.Block())
        for i, (emitter, io) in enumerate(phases):
            with contextlib.ExitStack() as ctx:
                emitter(p, ctx, nc, io)
            p.set_barrier()
        p.build(block, sems)


def build_fused(upto=8):
    nc = bass.Bass("TRN2", target_bir_lowering=False, debug=False)
    I = 2816
    ei = lambda n, sh, dt=F32: nc.dram_tensor(n, sh, dt, kind="ExternalInput").ap()
    it = lambda n, sh, dt=F32: nc.dram_tensor(n, sh, dt, kind="Internal").ap()
    xT = ei("xT", [C, TL]); x = ei("x", [TL, C])
    Wqkv = ei("Wqkv", [C, 3 * C]); bqkvT = ei("bqkvT", [24, 128])
    omh = ei("omh", [FD, H * FD]); qn = ei("qn", [2]); qw = ei("qw", [2])
    Wout = ei("Wout", [C, C]); bout = ei("bout", [C])
    w_gate = ei("w_gate", [I, C]); w_up = ei("w_up", [I, C]); w_down = ei("w_down", [C, I])
    w_qkv1 = ei("w_qkv1", [C, 3 * C]); b_qkv1T = ei("b_qkv1T", [24, 128])
    w_out1 = ei("w_out1", [C, C]); b_out1 = ei("b_out1", [C])
    weightsT = ei("weightsT", [C, C]); scale = ei("scale", [C]); bias = ei("bias", [C])
    msk = ei("msk", [2])
    out = nc.dram_tensor("out", [TL, C], F32, kind="ExternalOutput").ap()
    import os as _os
    dbg = _os.environ.get("FUSED_DBG", "0") == "1"
    kd = "ExternalOutput" if dbg else "Internal"
    itd = lambda n, sh, dt=F32: nc.dram_tensor(n, sh, dt, kind=kd).ap()
    qf = it("qf_s", [NCH, FD, H, CH], BF16); kf = it("kf_s", [NCH, FD, H, CH], BF16)
    vt = it("vt_s", [H, NCH, 128, 4, HD], BF16)
    sprev7 = it("sprev7_s", [NCH - 1, H, FD, 65])
    stot = itd("stot_s", [H, FD, 65])
    e1in = nc.dram_tensor("e1in_s", [FD, H * 65], F32, kind="Internal")
    e1out = nc.dram_tensor("e1out_s", [FD, H * 65], F32, kind="Internal")
    wrowsA = ei("wrowsA", [6, 128, 1024])
    wrowsB = ei("wrowsB", [128, 2816])
    ag_in = it("agin_s", [7, 22, 128, 128], BF16)
    ag_out = it("agout_s", [56 * 22 * 128, 128], BF16)
    init = itd("init_s", [H, FD, 65])
    x1 = itd("x1_s", [TL, C]); x2 = itd("x2_s", [TL, C]); x3 = itd("x3_s", [TL, C])
    e2in = it("e2in_s", [2048, C], BF16)
    e2out = it("e2out_s", [2048, C], BF16)
    wqT16 = itd("wqT16_s", [C, WQR], BF16)
    qk2 = it("qk2_s", [2048, WQR], BF16); vv = it("vv_s", [WQR, C], BF16)
    x3T16 = itd("x3T16_s", [C, TL], BF16)

    phases = [
        (emit_Dprep, {"prefix": "dp_", "wrowsA": wrowsA, "wrowsB": wrowsB,
                      "ag_in": ag_in, "ag_out": ag_out}),
        (emit_A, {"prefix": "a_", "xT": xT, "Wqkv": Wqkv, "bqkvT": bqkvT, "omh": omh,
                  "qn": qn, "qw": qw, "qf": qf, "kf": kf, "vt": vt, "sprev": sprev7,
                  "stot": stot}),
        (emit_E1, {"prefix": "e1_", "stot": stot, "e1in": e1in, "e1out": e1out,
                   "msk": msk, "init": init}),
        (emit_B, {"prefix": "b_", "qf": qf, "kf": kf, "vt": vt, "sprev7": sprev7,
                  "init": init, "x": x, "Wout": Wout, "bout": bout, "x1": x1}),
        (emit_D, {"prefix": "d_", "x1": x1, "w_gate": w_gate, "w_up": w_up,
                  "w_down": w_down, "x2": x2, "ag_out": ag_out}),
        (emit_E2, {"prefix": "e2_", "x2": x2, "e2in": e2in, "e2out": e2out,
                   "msk": msk, "wqT16": wqT16}),
        (emit_F1, {"prefix": "f1_", "wqT": wqT16, "w_qkv": w_qkv1, "b_qkvT": b_qkv1T,
                   "qk2": qk2, "vv": vv}),
        (emit_F2, {"prefix": "f2_", "qk2": qk2, "vv": vv, "w_out": w_out1,
                   "b_out": b_out1, "x2own": x2, "x3": x3, "x3T16": x3T16}),
        (emit_H, {"prefix": "h_", "x3": x3, "x3T": x3T16, "weightsT": weightsT,
                  "scale": scale, "bias": bias, "out": out}),
    ]
    finish_build_multi(nc, phases)
    return nc


def finish_build(nc, emitter, io):
    import contextlib
    with contextlib.ExitStack() as ctx:
        p = Prog(nc)
        for s in ([f"ld{i}" for i in range(Prog.LD_POOL)]
                  + [f"st{i}" for i in range(Prog.ST_POOL)]
                  + ["st_g", "pe", "dve", "act"]):
            p.sem(s)
        sems = {}
        for s in p.sem_names:
            sems[s] = ctx.enter_context(nc.semaphore(s))
        block = ctx.enter_context(nc.Block())
        emitter(p, ctx, nc, io)
        p.build(block, sems)


_BUILT = {}


def _get(name, fn):
    if name not in _BUILT:
        _BUILT[name] = fn()
    return _BUILT[name]


def kernel(L0_attn_x, L0_attn_Wqkv, L0_attn_bqkv, L0_attn_Wout, L0_attn_bout,
           L0_attn_omega, L0_attn_quad_nodes, L0_attn_quad_weights,
           L0_ffn_w_gate, L0_ffn_w_up, L0_ffn_w_down,
           L1_attn_w_qkv, L1_attn_b_qkv, L1_attn_w_out, L1_attn_b_out,
           L1_ffn_weights, L1_ffn_scale, L1_ffn_bias):
    asn = lambda a: np.ascontiguousarray(np.asarray(a))
    x = asn(L0_attn_x)
    Wqkv = asn(L0_attn_Wqkv); bqkv = asn(L0_attn_bqkv)
    Wout = asn(L0_attn_Wout); bout = asn(L0_attn_bout)
    omega = asn(L0_attn_omega)
    qn = asn(L0_attn_quad_nodes); qw = asn(L0_attn_quad_weights)
    wg = asn(L0_ffn_w_gate); wu = asn(L0_ffn_w_up); wd = asn(L0_ffn_w_down)
    wqkv1 = asn(L1_attn_w_qkv); bqkv1 = asn(L1_attn_b_qkv)
    wout1 = asn(L1_attn_w_out); bout1 = asn(L1_attn_b_out)
    w2 = asn(L1_ffn_weights); sc2 = asn(L1_ffn_scale); bi2 = asn(L1_ffn_bias)
    cores = list(range(NCORE))

    omh = np.ascontiguousarray(np.transpose(omega, (2, 1, 0, 3)).reshape(HD, H * FD))
    bqkvT = bqkv.reshape(24, 128)

    if True:
        in_maps = []
        for c in cores:
            b, s = c // 2, c % 2
            xs = np.ascontiguousarray(x[b, s * TL:(s + 1) * TL])
            wrA = np.zeros((6, 128, 1024), np.float32)
            wrB = np.empty((128, 2816), np.float32)
            for si in range(7):
                j = si * 8 + c
                ent = _DPREP[j]
                if ent is None:
                    continue
                wname, r = ent
                wmat = {"w_gate": wg, "w_up": wu, "w_down": wd}[wname]
                if si < 6:
                    wrA[si] = wmat[r * 128:(r + 1) * 128, :]
                else:
                    wrB[:] = wmat[r * 128:(r + 1) * 128, :]
            in_maps.append({
                "wrowsA": wrA, "wrowsB": wrB,
                "xT": np.ascontiguousarray(xs.T), "x": xs,
                "Wqkv": Wqkv, "bqkvT": bqkvT, "omh": omh, "qn": qn, "qw": qw,
                "Wout": Wout, "bout": bout,
                "w_gate": wg, "w_up": wu, "w_down": wd,
                "w_qkv1": wqkv1, "b_qkv1T": bqkv1.reshape(24, 128),
                "w_out1": wout1, "b_out1": bout1,
                "weightsT": np.ascontiguousarray(w2.T), "scale": sc2, "bias": bi2,
                "msk": np.array([1.0 - s, float(s)], np.float32),
            })
        res = run_bass_kernel_spmd(_get("FUSED", build_fused), in_maps, core_ids=cores).results
        out = np.empty((B, T, C), np.float32)
        for c in cores:
            b, s = c // 2, c % 2
            out[b, s * TL:(s + 1) * TL] = res[c]["out"]
        return out

    insA = []
    for c in cores:
        b, s = c // 2, c % 2
        xs = x[b, s * TL:(s + 1) * TL]
        insA.append({"xT": np.ascontiguousarray(xs.T), "Wqkv": Wqkv, "bqkvT": bqkvT,
                     "omh": omh, "qn": qn, "qw": qw})
    rA = run_bass_kernel_spmd(_get("A", build_A), insA, core_ids=cores).results

    z1 = np.zeros((1, H, FD, 65), np.float32)
    z0 = np.zeros((H, FD, 65), np.float32)
    insB = []
    for c in cores:
        b, s = c // 2, c % 2
        init = rA[2 * b]["stot"].astype(np.float32) if s == 1 else z0
        insB.append({"qf": rA[c]["qf"], "kf": rA[c]["kf"], "vt": rA[c]["vt"],
                     "sprev8": np.concatenate([z1, rA[c]["sprev"].astype(np.float32)], 0),
                     "init": init, "x": x[b, s * TL:(s + 1) * TL],
                     "Wout": Wout, "bout": bout})
    rB = run_bass_kernel_spmd(_get("B", build_B), insB, core_ids=cores).results

    insD = [{"x1": rB[c]["x1"], "w_gate": wg, "w_up": wu, "w_down": wd} for c in cores]
    rD = run_bass_kernel_spmd(_get("D", build_D), insD, core_ids=cores).results

    bqkv1T = bqkv1.reshape(24, 128)
    insF1 = []
    for c in cores:
        b, s = c // 2, c % 2
        if s == 0:
            wq = rD[c]["x2"][0:WQR]
        else:
            wq = np.concatenate([rD[2 * b]["x2"][2048:4096], rD[c]["x2"][0:256]], 0)
        insF1.append({"wqT": np.ascontiguousarray(wq.T), "w_qkv": wqkv1, "b_qkvT": bqkv1T})
    rF1 = run_bass_kernel_spmd(_get("F1", build_F1), insF1, core_ids=cores).results

    insF2 = [{"qk2": rF1[c]["qk2"], "vv": rF1[c]["vv"], "w_out": wout1, "b_out": bout1,
              "x2own": rD[c]["x2"]} for c in cores]
    rF2 = run_bass_kernel_spmd(_get("F2", build_F2), insF2, core_ids=cores).results

    w2T = np.ascontiguousarray(w2.T)
    insH = [{"x3": rF2[c]["x3"], "x3T": np.ascontiguousarray(rF2[c]["x3"].T),
             "weightsT": w2T, "scale": sc2, "bias": bi2} for c in cores]
    rH = run_bass_kernel_spmd(_get("H", build_H), insH, core_ids=cores).results

    out = np.empty((B, T, C), np.float32)
    for c in cores:
        b, s = c // 2, c % 2
        out[b, s * TL:(s + 1) * TL] = rH[c]["out"]
    return out


# version 15
# speedup vs baseline: 1.0068x; 1.0068x over previous
import numpy as np
import concourse.bass as bass
import concourse.mybir as mybir
from concourse.bass_utils import run_bass_kernel_spmd

F32 = mybir.dt.float32
BF16 = mybir.dt.bfloat16
AF = mybir.ActivationFunctionType
ALU = mybir.AluOpType
AX = mybir.AxisListType

B, T, C = 4, 8192, 1024
TL = T // 2
H, HD = 16, 64
M, R = 32, 2
FD = R * M
CH = 512
NCH = TL // CH
NCORE = 8
E20 = float(np.exp(20.0).astype(np.float32))

ENGINES = ("sync", "tensor", "vector", "scalar", "gpsimd")


class Prog:

    LD_POOL = 4
    ST_POOL = 6

    def __init__(self, nc):
        self.nc = nc
        self.ops = {e: [] for e in ENGINES}
        self.counts = {}
        self.sem_names = []
        self.dma_issue = {"sync": 0, "gpsimd": 0}

    def sem(self, name):
        self.counts[name] = 0
        self.sem_names.append(name)

    def set_barrier(self):
        waits = [(s, self.counts[s]) for s in self.sem_names if self.counts[s] > 0]
        self.pending_barrier = {e: list(waits) for e in ENGINES}

    def dma_pool(self, queue):
        if queue == "sync":
            return [f"ld{i}" for i in range(self.LD_POOL)]
        return [f"st{i}" for i in range(self.ST_POOL)]

    def emit(self, eng, fn, waits=(), inc=None, inc_n=1):
        tok = None
        if inc is not None:
            self.counts[inc] += inc_n
            tok = (inc, self.counts[inc])
        ws = [w for w in waits if w is not None]
        pb = getattr(self, "pending_barrier", None)
        if pb and pb.get(eng):
            ws = pb.pop(eng) + ws
        self.ops[eng].append((fn, ws, inc, inc_n, tok))
        return tok

    def dma(self, queue, out, in_, waits=(), **kw):
        pool = self.dma_pool(queue)
        sem = pool[self.dma_issue[queue] % len(pool)]
        self.dma_issue[queue] += 1
        prev_total = self.counts[sem]
        ws = list(waits)
        if prev_total > 0:
            ws.append((sem, prev_total))
        return self.emit(queue, lambda h: h.dma_start(out=out, in_=in_, **kw),
                         waits=ws, inc=sem, inc_n=16)

    def dma_all_done_waits(self):
        return [(s, self.counts[s]) for s in self.sem_names
                if (s.startswith("ld") or s.startswith("st")) and self.counts[s] > 0]

    def build(self, block, sems):
        def runner(eng_name):
            def _run(h):
                wm = {}
                for fn, ws, inc, inc_n, tok in self.ops[eng_name]:
                    best = {}
                    for (sname, val) in ws:
                        if val > best.get(sname, -1):
                            best[sname] = val
                    for sname, val in best.items():
                        if val > wm.get(sname, -1):
                            h.wait_ge(sems[sname], val)
                            wm[sname] = val
                    inst = fn(h)
                    if inc is not None:
                        inst.then_inc(sems[inc], inc_n)
            return _run

        for e in ENGINES:
            getattr(block, e)(runner(e))

    def final_waits(self, h, sems, eng="gpsimd"):
        pass


def build_common_consts(p, ctx, nc, pref=""):
    ident = ctx.enter_context(nc.sbuf_tensor(pref + "ident", [128, 128], BF16))
    tril = ctx.enter_context(nc.sbuf_tensor(pref + "tril", [128, 128], F32))
    t1 = p.emit("gpsimd", lambda h: h.memset(ident[:], 1.0), inc="st_g")
    t2 = p.emit("gpsimd", lambda h: h.affine_select(
        ident[:], ident[:], [[1, 128]], ALU.is_equal, 0.0, channel_multiplier=-1),
        waits=[t1], inc="st_g")
    t3 = p.emit("gpsimd", lambda h: h.memset(tril[:], 1.0), inc="st_g")
    t4 = p.emit("gpsimd", lambda h: h.affine_select(
        tril[:], tril[:], [[1, 128]], ALU.is_ge, 0.0, channel_multiplier=-1),
        waits=[t3], inc="st_g")
    return ident, tril, [t2, t4]


def emit_A(p, ctx, nc, io):
    NQ = 24

    pref = io.get("prefix", "")
    ident, _, const_toks = build_common_consts(p, ctx, nc, pref)

    sb = lambda name, shape, dt=BF16: ctx.enter_context(nc.sbuf_tensor(pref + name, shape, dt))
    w16 = sb("w16", [128, 8, 3072])
    w32t = sb("w32t", [128, 3072], F32)
    bqT = sb("bqT", [128, 24], F32)
    om16 = sb("om16", [64, 16, 64])
    ones2h = sb("ones2h", [128, 2])
    sel2 = sb("sel2", [2, 128])
    svec = sb("svec", [64, 1], F32)
    sq2s = sb("sq2s", [64, 1], F32)
    negs = sb("negs", [64, 1], F32)
    wMv = sb("wMv", [64, 1], F32)
    ewMv = sb("ewMv", [64, 1], F32)
    xc32 = [sb(f"xc32_{i}", [128, 8, 512], F32) for i in range(2)]
    x16 = [sb(f"x16_{i}", [128, 8, 512]) for i in range(2)]
    qk16 = [sb(f"qk16_{i}", [128, 512]) for i in range(2)]
    v16fm = [sb(f"v16fm_{i}", [128, 512]) for i in range(2)]
    sq16 = [sb(f"sq16_{i}", [128, 512]) for i in range(2)]
    r1sb = [sb(f"r1sb_{i}", [2, 512], F32) for i in range(2)]
    r2sb = [sb(f"r2sb_{i}", [2, 512], F32) for i in range(2)]
    rn16 = [sb(f"rn16_{i}", [2, 512]) for i in range(2)]
    qnh = [sb(f"qnh_{i}", [64, 2, 512]) for i in range(2)]
    f16 = [sb(f"f16_{i}", [64, 2, 512]) for i in range(2)]
    kft = sb("kft", [128, 16, 4, 64])
    v1b = sb("v1b", [128, 4, 130])
    ssb = sb("ssb", [64, 16, 65], F32)

    ps = lambda name, shape, dt=F32: ctx.enter_context(nc.psum_tensor(pref + name, shape, dt))
    ps_mm = [ps(f"ps_mm{i}", [128, 512]) for i in range(2)]
    ps_bc = ps("ps_bc", [128, 512])
    ps_proj = ps("ps_proj", [64, 512])
    ps_ssq = [ps_mm[i][0:2, :] for i in range(2)]
    ps_proj2 = [ps_proj[:], ps_bc[0:64, :]]
    ps_vT = ps("ps_vT", [128, 128], BF16)
    ps_state = [ps(f"ps_st{i}", [64, 512]) for i in range(2)]
    ps_k0 = ps("ps_k0", [64, 16])

    tok_w = []
    last_cast = None
    for kc in range(8):
        tl = p.dma("sync", w32t[:], io["Wqkv"].rearrange("(kc p) n -> p kc n", p=128)[:, kc, :],
                   waits=[last_cast])
        last_cast = p.emit("vector", lambda h, kc=kc: h.tensor_copy(w16[:, kc, :], w32t[:]),
                           waits=[tl], inc="dve")
        tok_w.append(last_cast)
    bq24 = sb("bq24", [24, 128], F32)
    ident32 = sb("ident32", [128, 128], F32)
    ti = p.emit("gpsimd", lambda h: h.memset(ident32[:], 1.0), inc="st_g")
    ti2 = p.emit("gpsimd", lambda h: h.affine_select(
        ident32[:], ident32[:], [[1, 128]], ALU.is_equal, 0.0, channel_multiplier=-1),
        waits=[ti], inc="st_g")
    tb = p.dma("sync", bq24[:], io["bqkvT"])
    tb2 = p.emit("tensor", lambda h: h.transpose(ps_bc[:, 0:24], bq24[:], ident32[0:24, 0:24]),
                 waits=[tb, ti2], inc="pe")
    tb3 = p.emit("vector", lambda h: h.tensor_copy(bqT[:], ps_bc[:, 0:24]), waits=[tb2], inc="dve")

    om32 = sb("om32", [64, 1024], F32)
    to1 = p.dma("sync", om32[:], io["omh"])
    to2 = p.emit("vector", lambda h: h.tensor_copy(
        om16[:].rearrange("p h m -> p (h m)"), om32[:]), waits=[to1], inc="dve")

    tg = []
    tg.append(p.emit("gpsimd", lambda h: h.memset(ones2h[:], 1.0), inc="st_g"))
    tg.append(p.emit("gpsimd", lambda h: h.affine_select(
        ones2h[0:64, :], ones2h[0:64, :], [[1, 2]], ALU.is_equal, 0.0,
        channel_multiplier=0), waits=[tg[-1]], inc="st_g"))
    tg.append(p.emit("gpsimd", lambda h: h.affine_select(
        ones2h[64:128, :], ones2h[64:128, :], [[1, 2]], ALU.is_equal, 0.0,
        base=-1, channel_multiplier=0), waits=[tg[-2]], inc="st_g"))
    tg.append(p.emit("gpsimd", lambda h: h.memset(sel2[:], 1.0), inc="st_g"))
    tg.append(p.emit("gpsimd", lambda h: h.affine_select(
        sel2[:], sel2[:], [[1, 2], [0, 64]], ALU.is_equal, 0.0,
        channel_multiplier=-1), waits=[tg[-1]], inc="st_g"))
    for tb_ in range(4):
        tg.append(p.emit("gpsimd", lambda h, tb_=tb_: h.memset(v1b[:, tb_, 64:65], 1.0), inc="st_g"))
        tg.append(p.emit("gpsimd", lambda h, tb_=tb_: h.memset(v1b[:, tb_, 129:130], 1.0), inc="st_g"))

    wvec = sb("wvec", [64, 1], F32)
    tq = []
    for f in range(64):
        r = f // 32
        tq.append(p.dma("sync", svec[f:f + 1, 0:1], io["qn"][r:r + 1].unsqueeze(1)))
        tq.append(p.dma("sync", wvec[f:f + 1, 0:1], io["qw"][r:r + 1].unsqueeze(1)))
    tmp1 = sb("tmp1", [64, 1], F32)
    qv_waits = p.dma_all_done_waits()
    t_a = p.emit("vector", lambda h: h.tensor_scalar(tmp1[:], svec[:], 2.0, 1e-30, ALU.mult, ALU.max),
                 waits=qv_waits, inc="dve")
    t_b = p.emit("scalar", lambda h: h.activation(tmp1[:], tmp1[:], AF.Ln), waits=[t_a], inc="act")
    t_c = p.emit("scalar", lambda h: h.activation(sq2s[:], tmp1[:], AF.Exp, scale=0.5), waits=[t_b], inc="act")
    t_d = p.emit("vector", lambda h: h.tensor_scalar_mul(negs[:], svec[:], -1.0), waits=qv_waits, inc="dve")
    t_e = p.emit("vector", lambda h: h.tensor_scalar(wMv[:], wvec[:], 0.0, 1.0 / M, ALU.max, ALU.mult),
                 waits=qv_waits, inc="dve")
    t_f = p.emit("vector", lambda h: h.tensor_scalar_mul(ewMv[:], wMv[:], E20), waits=[t_e], inc="dve")
    pre_toks = [tok_w[-1], tb3, to2, t_c, t_d, t_f] + tg + const_toks

    xT_t = io["xT"].rearrange("(kc p) t -> p kc t", p=128)
    last_mm_read_x = [None, None]
    last_evac = [None, None]
    last_pbuse = {"ssq": None, "bc": None, "proj": None, "vT": None}
    last_ssq_read = [None, None]
    last_qk_use = [None, None]
    last_v_use = [None, None]
    prev_tssq = [None, None]
    prev_tr2 = [None, None]
    prev_trn_use = [None, None]
    last_qnh_use = [None, None]
    last_f16_use = [[[], []], [[], []]]
    last_state_read = None
    last_kft_use = None
    last_v1b_dma = []
    state_first = [True, True]

    for c in range(NCH):
        par = c % 2
        tl = p.dma("sync", xc32[par][:], xT_t[:, :, c * CH:(c + 1) * CH],
                   waits=[last_mm_read_x[par]] + (pre_toks if c == 0 else []))
        cast_eng = "vector" if par == 0 else "scalar"
        if cast_eng == "vector":
            tcast = p.emit("vector", lambda h, par=par: h.tensor_copy(
                x16[par][:].rearrange("p a b -> p (a b)"), xc32[par][:].rearrange("p a b -> p (a b)")),
                waits=[tl], inc="dve")
        else:
            tcast = p.emit("scalar", lambda h, par=par: h.activation(
                x16[par][:].rearrange("p a b -> p (a b)"), xc32[par][:].rearrange("p a b -> p (a b)"),
                AF.Copy), waits=[tl], inc="act")

        def qkv_tile(j):
            i = j % 2
            t = None
            for kc in range(8):
                t = p.emit("tensor", lambda h, kc=kc, j=j, i=i, par=par: h.matmul(
                    ps_mm[i][:], w16[:, kc, j * 128:(j + 1) * 128], x16[par][:, kc, :],
                    start=(kc == 0), stop=(kc == 7)),
                    waits=[tcast, last_evac[i], last_ssq_read[i]] if kc == 0 else [],
                    inc="pe")
            last_mm_read_x[par] = t
            return t

        for j in range(16):
            i = j % 2
            is_q = j < 8
            h0 = (j % 8) * 2
            tmm = qkv_tile(j)
            tev = p.emit("scalar", lambda h, i=i, j=j: h.activation(
                qk16[i][:], ps_mm[i][:], AF.Identity, bias=bqT[:, j:j + 1]),
                waits=[tmm, last_qk_use[i]], inc="act")
            last_evac[i] = tev
            tsq = p.emit("vector", lambda h, i=i: h.tensor_mul(
                sq16[i][:], qk16[i][:], qk16[i][:]),
                waits=[tev, prev_tssq[i]], inc="dve")
            tssq = p.emit("tensor", lambda h, i=i: h.matmul(
                ps_ssq[i], ones2h[:], sq16[i][:], start=True, stop=True),
                waits=[tsq, last_ssq_read[i]], inc="pe")
            prev_tssq[i] = tssq
            tr1 = p.emit("vector", lambda h, i=i: h.tensor_scalar_max(
                r1sb[i][:], ps_ssq[i], 1e-24),
                waits=[tssq, prev_tr2[i]], inc="dve")
            last_ssq_read[i] = tr1
            tr2 = p.emit("scalar", lambda h, i=i: h.activation(r2sb[i][:], r1sb[i][:], AF.Ln),
                         waits=[tr1, prev_trn_use[i]], inc="act")
            prev_tr2[i] = tr2
            trn = p.emit("scalar", lambda h, i=i: h.activation(
                rn16[i][:], r2sb[i][:], AF.Exp, scale=-0.5),
                waits=[tr2], inc="act")
            tbc = p.emit("tensor", lambda h, i=i: h.matmul(
                ps_bc[:], sel2[:], rn16[i][:], start=True, stop=True),
                waits=[trn, last_pbuse["bc"]], inc="pe")
            prev_trn_use[i] = tbc
            tqn = []
            for hh in range(2):
                tqn.append(p.emit("vector", lambda h, hh=hh, i=i: h.tensor_mul(
                    qnh[i][:, hh, :], qk16[i][hh * 64:(hh + 1) * 64, :],
                    ps_bc[hh * 64:(hh + 1) * 64, :]),
                    waits=[tbc, last_qnh_use[i]], inc="dve"))
            last_qk_use[i] = tqn[-1]
            for hh in range(2):
                hcur = h0 + hh
                prj = ps_proj2[hh]
                pwait = last_pbuse["proj"] if hh == 0 else tqn[-1]
                tproj = p.emit("tensor", lambda h, hh=hh, hcur=hcur, i=i, prj=prj: h.matmul(
                    prj, om16[:, hcur, :], qnh[i][:, hh, :], start=True, stop=True),
                    waits=[tqn[hh], pwait], inc="pe")
                last_qnh_use[i] = tproj
                tfe = p.emit("scalar", lambda h, hh=hh, i=i, prj=prj: h.activation(
                    f16[i][:, hh, :], prj, AF.Exp, bias=negs[:], scale=sq2s[:]),
                    waits=[tproj] + last_f16_use[i][hh], inc="act")
                if hh == 0:
                    last_pbuse["proj"] = tfe
                else:
                    last_pbuse["bc"] = tfe
                if is_q:
                    tcl = p.emit("vector", lambda h, hh=hh, i=i: h.tensor_scalar_min(
                        f16[i][:, hh, :], f16[i][:, hh, :], E20), waits=[tfe], inc="dve")
                    tdma = p.dma("gpsimd", io["qf"][c, :, hcur, :], f16[i][:, hh, :],
                                 waits=[tcl])
                    last_f16_use[i][hh] = [tdma]
                else:
                    tcl = p.emit("vector", lambda h, hh=hh, i=i: h.tensor_scalar(
                        f16[i][:, hh, :], f16[i][:, hh, :], wMv[:], ewMv[:], ALU.mult, ALU.min),
                        waits=[tfe], inc="dve")
                    tdma = p.dma("gpsimd", io["kf"][c, :, hcur, :], f16[i][:, hh, :],
                                 waits=[tcl])
                    tt = None
                    for tb_ in range(4):
                        tt = p.emit("tensor", lambda h, hh=hh, tb_=tb_, i=i: h.transpose(
                            ps_vT[0:128, 0:64], f16[i][:, hh, tb_ * 128:(tb_ + 1) * 128],
                            ident[0:64, 0:64]),
                            waits=[tcl, last_pbuse["vT"], last_kft_use], inc="pe")
                        last_pbuse["vT"] = p.emit("vector",
                            lambda h, hcur=hcur, tb_=tb_: h.tensor_copy(
                                kft[:, hcur, tb_, :], ps_vT[0:128, 0:64]),
                            waits=[tt], inc="dve")
                    last_f16_use[i][hh] = [tdma, tt]

        kft_done = last_pbuse["vT"]

        state_mm_tok = None
        for j in range(16, 24):
            i = j % 2
            h0 = (j - 16) * 2
            tmm = qkv_tile(j)
            tev = p.emit("scalar", lambda h, i=i, j=j: h.activation(
                v16fm[i][:], ps_mm[i][:], AF.Identity, bias=bqT[:, j:j + 1]),
                waits=[tmm, last_v_use[i]], inc="act")
            last_evac[i] = tev
            ed = []
            for tb_ in range(4):
                tt = p.emit("tensor", lambda h, tb_=tb_, i=i: h.transpose(
                    ps_vT[:], v16fm[i][:, tb_ * 128:(tb_ + 1) * 128], ident[:]),
                    waits=[tev, last_pbuse["vT"]], inc="pe")
                e0 = p.emit("vector", lambda h, tb_=tb_: h.tensor_copy(
                    v1b[:, tb_, 0:64], ps_vT[:, 0:64]),
                    waits=[tt, state_mm_tok] + last_v1b_dma, inc="dve")
                e1 = p.emit("vector", lambda h, tb_=tb_: h.tensor_copy(
                    v1b[:, tb_, 65:129], ps_vT[:, 64:128]),
                    waits=[tt, state_mm_tok] + last_v1b_dma, inc="dve")
                last_pbuse["vT"] = e1
                ed.append((e0, e1))
            d0 = p.dma("gpsimd", io["vt"][h0, c, :, :, :], v1b[:, :, 0:64],
                       waits=[e for (e, _) in ed])
            d1 = p.dma("gpsimd", io["vt"][h0 + 1, c, :, :, :], v1b[:, :, 65:129],
                       waits=[e for (_, e) in ed])
            last_v1b_dma = [d0, d1]
            last_v_use[i] = ed[-1][1]
            for tb_ in range(4):
                for hh in range(2):
                    hcur = h0 + hh
                    grp, off = hcur // 8, (hcur % 8) * 64
                    first = state_first[grp]
                    state_first[grp] = False
                    state_mm_tok = p.emit("tensor", lambda h, grp=grp, off=off, hcur=hcur,
                                          tb_=tb_, hh=hh, first=first: h.matmul(
                        ps_state[grp][:, off:off + 64], kft[:, hcur, tb_, :],
                        v1b[:, tb_, hh * 65:hh * 65 + 64],
                        start=first, stop=False, skip_group_check=True),
                        waits=[ed[tb_][hh], kft_done, last_state_read], inc="pe")
                    firstk = state_first[2] if len(state_first) > 2 else True
                    if len(state_first) == 2:
                        state_first.append(False)
                    else:
                        state_first[2] = False
                    state_mm_tok = p.emit("tensor", lambda h, hcur=hcur, tb_=tb_, hh=hh,
                                          firstk=firstk: h.matmul(
                        ps_k0[:, hcur:hcur + 1], kft[:, hcur, tb_, :],
                        v1b[:, tb_, hh * 65 + 64:hh * 65 + 65],
                        start=firstk, stop=False, skip_group_check=True),
                        waits=[], inc="pe")
            last_kft_use = state_mm_tok

        sp = []
        for grp in range(2):
            sp.append(p.emit("vector", lambda h, grp=grp: h.tensor_copy(
                ssb[:, grp * 8:(grp + 1) * 8, 0:64],
                ps_state[grp][:].rearrange("p (a b) -> p a b", b=64)), waits=[state_mm_tok], inc="dve"))
        sp.append(p.emit("vector", lambda h: h.tensor_copy(
            ssb[:, :, 64:65], ps_k0[:].unsqueeze(2)), waits=[state_mm_tok, sp[-1]], inc="dve"))
        last_state_read = sp[-1]
        dst = io["sprev"][c] if c < NCH - 1 else io["stot"]
        p.dma("gpsimd", dst.rearrange("h f e -> f h e"), ssb[:], waits=sp)

    p.emit("gpsimd", lambda h: h.engine_nop(), waits=p.dma_all_done_waits())




def emit_B(p, ctx, nc, io):
    import os as _os
    B_STAGE = int(_os.environ.get("B_STAGE", "4"))
    """Launch B: chunked causal linear attention from spilled features,
    then out @ Wout + bout + residual(x), LayerNorm -> x1.

    io: qf, kf [NCH, FD, H, CH] bf16; vt [H, NCH, 128, 4, HD] bf16;
        sprev8 [NCH, H, FD, 65] f32 (prefix state per chunk, row 0 zeros);
        init [H, FD, 65] f32 (partner total state or zeros);
        x [TL, C] f32; Wout [C, C] f32; bout [C] f32 -> x1 [TL, C] f32.
    """
    pref = io.get("prefix", "")
    ident, tril, const_toks = build_common_consts(p, ctx, nc, pref)
    sb = lambda name, shape, dt=BF16: ctx.enter_context(nc.sbuf_tensor(pref + name, shape, dt))
    ps = lambda name, shape, dt=F32: ctx.enter_context(nc.psum_tensor(pref + name, shape, dt))

    w16o = sb("w16o", [128, 8, 1024])
    w32t = sb("w32t", [128, 1024], F32)
    bout_bc = sb("bout_bc", [128, 1024], F32)
    init_sb = sb("init_sb", [64, 16, 65], F32)
    qf_all = [sb(f"qf_all{i}", [64, 16, 512]) for i in range(2)]
    kf_all = [sb(f"kf_all{i}", [64, 16, 512]) for i in range(2)]
    sprev_all = [sb(f"sprev_all{i}", [64, 16, 65], F32) for i in range(2)]
    v1b0 = [sb(f"v1b0_{i}", [128, 4, 64]) for i in range(2)]
    v1b = [sb(f"v1b_{i}", [128, 4, 65]) for i in range(2)]
    st16 = [sb(f"st16_{i}", [64, 65]) for i in range(2)]
    Ebuf = sb("Ebuf", [128, 4, 512])
    af = sb("af", [128, 8, 512])
    obuf = [sb(f"obuf{i}", [128, 64]) for i in range(2)]
    den = [sb(f"den{i}", [128, 1], F32) for i in range(2)]
    rden = [sb(f"rden{i}", [128, 1], F32) for i in range(2)]
    xt32 = [sb(f"xt32_{i}", [128, 1024], F32) for i in range(2)]
    yt = sb("yt", [128, 1024], F32)
    ym = sb("ym", [128, 1024], F32)
    sqt = sb("sqt", [128, 1024], F32)
    snm = sb("snm", [128, 1], F32)
    negm = sb("negm", [128, 1], F32)
    lnv = sb("lnv", [128, 1], F32)
    rstd = sb("rstd", [128, 1], F32)
    x1t = [sb(f"x1t_{i}", [128, 1024], F32) for i in range(2)]

    ps_S = [ps(f"ps_S{i}", [128, 512]) for i in range(2)]
    ps_ctx = [ps(f"ps_ctx{i}", [128, 65]) for i in range(2)]
    ps_T = [ps(f"ps_T{i}", [64, 128], BF16) for i in range(2)]
    ps_o = [ps(f"ps_o{i}", [128, 512]) for i in range(2)]

    last_cast = None
    wtoks = []
    for kc in range(8):
        tl = p.dma("sync", w32t[:], io["Wout"].rearrange("(kc p) n -> p kc n", p=128)[:, kc, :],
                   waits=[last_cast])
        last_cast = p.emit("vector", lambda h, kc=kc: h.tensor_copy(w16o[:, kc, :], w32t[:]),
                           waits=[tl], inc="dve")
        wtoks.append(last_cast)
    tbb = p.dma("sync", bout_bc[:], io["bout"].unsqueeze(0).broadcast_to([128, C]))
    tinit = p.dma("sync", init_sb[:], io["init"].rearrange("h f e -> f h e"))
    tv1 = [p.emit("gpsimd", lambda h, i=i, tb=tb: h.memset(v1b[i][:, tb, 64:65], 1.0), inc="st_g")
           for i in range(2) for tb in range(4)]
    pre = [wtoks[-1], tbb, tinit] + tv1 + const_toks

    last_ctx_div = [None, None]
    last_S_evac = [None, None]
    last_E_read = None
    last_v1b_read = [None, None]
    last_st_read = [None, None]
    last_obuf_read = [None, None]
    last_T_evac = [None, None]
    last_af_read = None
    last_ps_o_read = [None, None]
    last_x_read = [None, None]
    last_x1_dma = [None, None]
    last_qfkf_read = [None, None]
    last_sprev_read = [None, None]

    for c in range(NCH):
        par = c % 2
        tq = p.dma("sync", qf_all[par][:], io["qf"][c].rearrange("f h t -> f (h t)"),
                   waits=[last_qfkf_read[par]] + (pre if c == 0 else []))
        tk = p.dma("sync", kf_all[par][:], io["kf"][c].rearrange("f h t -> f (h t)"),
                   waits=[last_qfkf_read[par]])
        if "sprev7" in io:
            tsp = None if c == 0 else p.dma(
                "sync", sprev_all[par][:], io["sprev7"][c - 1].rearrange("h f e -> f h e"),
                waits=[last_sprev_read[par]])
        else:
            tsp = p.dma("sync", sprev_all[par][:], io["sprev8"][c].rearrange("h f e -> f h e"),
                        waits=[last_sprev_read[par]])
        af_evacs = []
        t0 = None
        for hh in range(16):
            hp = hh % 2
            tv = p.dma("sync", v1b0[hp][:], io["vt"][hh, c], waits=[last_v1b_read[hp]])
            if "sprev7" in io and c == 0:
                tst = p.emit("vector", lambda h, hp=hp, hh=hh: h.tensor_copy(
                    st16[hp][:], init_sb[:, hh, :]),
                    waits=[tinit, last_st_read[hp]], inc="dve")
            else:
                tst = p.emit("vector", lambda h, hp=hp, hh=hh, par=par: h.tensor_add(
                    st16[hp][:], sprev_all[par][:, hh, :], init_sb[:, hh, :]),
                    waits=[tsp, tinit, last_st_read[hp]], inc="dve")
                last_sprev_read[par] = tst
            vcp = []
            for tb in range(4):
                vcp.append(p.emit("vector", lambda h, hp=hp, tb=tb: h.tensor_copy(
                    v1b[hp][:, tb, 0:64], v1b0[hp][:, tb, :]),
                    waits=[tv, last_v1b_read[hp]] + tv1, inc="dve"))
            if B_STAGE < 2:
                continue
            ev_toks = []
            for i in range(4):
                sp_ = i % 2
                tS = p.emit("tensor", lambda h, i=i, sp_=sp_, hh=hh, par=par: h.matmul(
                    ps_S[sp_][:], kf_all[par][:, hh, i * 128:(i + 1) * 128],
                    qf_all[par][:, hh, :], start=True, stop=True),
                    waits=[tq, tk] + (last_S_evac[sp_] or []), inc="pe")
                td = p.emit("vector", lambda h, i=i, sp_=sp_: h.tensor_mul(
                    Ebuf[:, i, i * 128:(i + 1) * 128], ps_S[sp_][:, i * 128:(i + 1) * 128],
                    tril[:]), waits=[tS, last_E_read], inc="dve")
                ev_toks.append(td)
                if i < 3:
                    tc_ = p.emit("scalar", lambda h, i=i, sp_=sp_: h.activation(
                        Ebuf[:, i, (i + 1) * 128:512], ps_S[sp_][:, (i + 1) * 128:512],
                        AF.Copy), waits=[tS, last_E_read, td], inc="act")
                    ev_toks.append(tc_)
                    last_S_evac[sp_] = [td, tc_]
                else:
                    last_S_evac[sp_] = [td]
            if B_STAGE < 3:
                continue
            for j in range(4):
                cp_ = j % 2
                t0 = p.emit("tensor", lambda h, j=j, cp_=cp_, hh=hh, par=par, hp=hp: h.matmul(
                    ps_ctx[cp_][:], qf_all[par][:, hh, j * 128:(j + 1) * 128], st16[hp][:],
                    start=True, stop=(j == 3 and False) or False, skip_group_check=True),
                    waits=[tst, last_ctx_div[cp_]], inc="pe")
                for i in range(j + 1):
                    t0 = p.emit("tensor", lambda h, i=i, j=j, cp_=cp_, hp=hp: h.matmul(
                        ps_ctx[cp_][:], Ebuf[:, i, j * 128:(j + 1) * 128], v1b[hp][:, i, :],
                        start=False, stop=(i == j), skip_group_check=True),
                        waits=ev_toks + vcp, inc="pe")
                last_E_read = t0
                last_v1b_read[hp] = t0
                last_st_read[hp] = t0
                bk = j % 2
                td1 = p.emit("vector", lambda h, cp_=cp_, bk=bk: h.tensor_scalar_max(
                    den[bk][:], ps_ctx[cp_][:, 64:65], 1e-6), waits=[t0], inc="dve")
                td2 = p.emit("vector", lambda h, bk=bk: h.reciprocal(rden[bk][:], den[bk][:]),
                             waits=[td1], inc="dve")
                td3 = p.emit("vector", lambda h, cp_=cp_, bk=bk: h.tensor_scalar(
                    obuf[bk][:], ps_ctx[cp_][:, 0:64], rden[bk][:], None, ALU.mult),
                    waits=[td2, last_obuf_read[bk]], inc="dve")
                last_ctx_div[cp_] = td3
                tT = p.emit("tensor", lambda h, bk=bk: h.transpose(
                    ps_T[bk][:], obuf[bk][:], ident[:]),
                    waits=[td3, last_T_evac[bk]], inc="pe")
                last_obuf_read[bk] = tT
                te = p.emit("vector", lambda h, hh=hh, j=j, bk=bk: h.tensor_copy(
                    af[64 * (hh % 2):64 * (hh % 2) + 64, hh // 2, j * 128:(j + 1) * 128],
                    ps_T[bk][0:64, :]), waits=[tT, last_af_read], inc="dve")
                last_T_evac[bk] = te
                af_evacs.append(te)
        last_qfkf_read[par] = t0

        for j in range(4 if B_STAGE >= 4 else 0):
            xp = j % 2
            tx = p.dma("sync", xt32[xp][:], io["x"][(c * 4 + j) * 128:(c * 4 + j + 1) * 128, :],
                       waits=[last_x_read[xp]])
            ytoks = []
            for n in range(2):
                tmo = None
                for kc in range(8):
                    tmo = p.emit("tensor", lambda h, kc=kc, j=j, n=n: h.matmul(
                        ps_o[n][:], af[:, kc, j * 128:(j + 1) * 128],
                        w16o[:, kc, n * 512:(n + 1) * 512],
                        start=(kc == 0), stop=(kc == 7)),
                        waits=(af_evacs + [last_ps_o_read[n]]) if kc == 0 else [], inc="pe")
                last_af_read = tmo
                ty1 = p.emit("vector", lambda h, n=n, xp=xp: h.tensor_add(
                    yt[:, n * 512:(n + 1) * 512], ps_o[n][:], xt32[xp][:, n * 512:(n + 1) * 512]),
                    waits=[tmo, tx], inc="dve")
                last_x_read[xp] = ty1
                ty2 = p.emit("vector", lambda h, n=n: h.tensor_add(
                    yt[:, n * 512:(n + 1) * 512], yt[:, n * 512:(n + 1) * 512],
                    bout_bc[:, n * 512:(n + 1) * 512]), waits=[ty1, tbb], inc="dve")
                last_ps_o_read[n] = ty1
                ytoks.append(ty2)
            tr = p.emit("vector", lambda h: h.reduce_sum(snm[:], yt[:], AX.X),
                        waits=ytoks, inc="dve")
            tn = p.emit("vector", lambda h: h.tensor_scalar_mul(negm[:], snm[:], -1.0 / C),
                        waits=[tr], inc="dve")
            tm = p.emit("vector", lambda h: h.tensor_scalar_add(ym[:], yt[:], negm[:]),
                        waits=[tn], inc="dve")
            tsq = p.emit("scalar", lambda h: h.activation(sqt[:], ym[:], AF.Square),
                         waits=[tm], inc="act")
            tss = p.emit("vector", lambda h: h.reduce_sum(lnv[:], sqt[:], AX.X),
                         waits=[tsq], inc="dve")
            tva = p.emit("vector", lambda h: h.tensor_scalar(
                lnv[:], lnv[:], 1.0 / C, 1e-5, ALU.mult, ALU.add), waits=[tss], inc="dve")
            tln = p.emit("scalar", lambda h: h.activation(lnv[:], lnv[:], AF.Ln),
                         waits=[tva], inc="act")
            trs = p.emit("scalar", lambda h: h.activation(rstd[:], lnv[:], AF.Exp, scale=-0.5),
                         waits=[tln], inc="act")
            tx1 = p.emit("vector", lambda h, xp=xp: h.tensor_scalar(
                x1t[xp][:], ym[:], rstd[:], None, ALU.mult),
                waits=[trs, last_x1_dma[xp]], inc="dve")
            last_x1_dma[xp] = p.dma("gpsimd", io["x1"][(c * 4 + j) * 128:(c * 4 + j + 1) * 128, :],
                                    x1t[xp][:], waits=[tx1])

    p.emit("gpsimd", lambda h: h.engine_nop(), waits=p.dma_all_done_waits())


def build_B():
    nc = bass.Bass("TRN2", target_bir_lowering=False, debug=False)
    io = {
        "qf": nc.dram_tensor("qf", [NCH, FD, H, CH], BF16, kind="ExternalInput").ap(),
        "kf": nc.dram_tensor("kf", [NCH, FD, H, CH], BF16, kind="ExternalInput").ap(),
        "vt": nc.dram_tensor("vt", [H, NCH, 128, 4, HD], BF16, kind="ExternalInput").ap(),
        "sprev8": nc.dram_tensor("sprev8", [NCH, H, FD, 65], F32, kind="ExternalInput").ap(),
        "init": nc.dram_tensor("init", [H, FD, 65], F32, kind="ExternalInput").ap(),
        "x": nc.dram_tensor("x", [TL, C], F32, kind="ExternalInput").ap(),
        "Wout": nc.dram_tensor("Wout", [C, C], F32, kind="ExternalInput").ap(),
        "bout": nc.dram_tensor("bout", [C], F32, kind="ExternalInput").ap(),
        "x1": nc.dram_tensor("x1", [TL, C], F32, kind="ExternalOutput").ap(),
    }
    finish_build(nc, emit_B, io)
    return nc




def emit_quant(p, e, x_in, q_out, scr, waits):
    F = x_in.shape[1]
    NB = F // 16
    sem = "dve" if e == "vector" else "st_g"
    A_ = scr["bufA"][:, 0:F]
    B_ = scr["bufB"][:, 0:F]
    Ab = scr["bufA"].bitcast(BF16)
    acc = Ab[:, 0:F]
    tmp = Ab[:, F:2 * F]
    Bb = scr["bufB"].bitcast(BF16)[:, 0:F]
    bm = scr["bm"][:, 0:NB]
    sc = scr["sc"][:, 0:NB]
    x3 = x_in.rearrange("p (nb w) -> p nb w", w=16)
    t1 = p.emit("vector", lambda h: h.tensor_reduce(bm.unsqueeze(2), x3, AX.X, ALU.max,
                apply_absolute_value=True), waits=waits, inc="dve")
    t2 = p.emit("vector", lambda h: h.tensor_scalar(sc, bm, 1.0 / 6.0, 1e-12, ALU.mult, ALU.max),
                waits=[t1], inc="dve")
    t3 = p.emit("vector", lambda h: h.reciprocal(bm, sc), waits=[t2], inc="dve")
    ta = p.emit("scalar", lambda h: h.activation(A_, x_in, AF.Abs),
                waits=list(waits) + [("act", p.counts["act"])], inc="act")
    rs_b = bm.unsqueeze(2).broadcast_to([128, NB, 16])
    tu = p.emit(e, lambda h: h.tensor_tensor(
        B_.rearrange("p (nb w) -> p nb w", w=16), A_.rearrange("p (nb w) -> p nb w", w=16),
        rs_b, ALU.mult), waits=[t3, ta], inc=sem)
    tacc = p.emit(e, lambda h: h.tensor_scalar(acc, B_, 0.25, 1.0, ALU.is_gt, ALU.mult),
                  waits=[tu], inc=sem)
    for thr, step in ((0.75, 1.0), (1.25, 1.0), (1.75, 1.0), (2.5, 2.0), (3.5, 2.0), (5.0, 4.0)):
        tci = p.emit(e, lambda h, thr=thr, step=step: h.tensor_scalar(
            tmp, B_, thr, step, ALU.is_gt, ALU.mult), waits=[tacc], inc=sem)
        tacc = p.emit(e, lambda h: h.tensor_add(acc, acc, tmp), waits=[tci], inc=sem)
    tsg = p.emit(e, lambda h: h.tensor_scalar(tmp, x_in, 0.0, 0.5, ALU.is_ge, ALU.subtract),
                 waits=[tacc] + list(waits), inc=sem)
    tt2 = p.emit(e, lambda h: h.tensor_mul(Bb, acc, tmp), waits=[tsg], inc=sem)
    sc_b = sc.unsqueeze(2).broadcast_to([128, NB, 16])
    tqf = p.emit(e, lambda h: h.tensor_tensor(
        q_out.rearrange("p (nb w) -> p nb w", w=16), Bb.rearrange("p (nb w) -> p nb w", w=16),
        sc_b, ALU.mult), waits=[tt2], inc=sem)
    return tqf




_DPREP = ([("w_gate", r) for r in range(22)] + [("w_up", r) for r in range(22)]
          + [None] * 4 + [("w_down", r) for r in range(8)])


def emit_Dprep(p, ctx, nc, io):
    pref = io.get("prefix", "")
    ident = ctx.enter_context(nc.sbuf_tensor(pref + "ident", [128, 128], BF16))
    ti = p.emit("gpsimd", lambda h: h.memset(ident[:], 1.0), inc="st_g")
    ti2 = p.emit("gpsimd", lambda h: h.affine_select(
        ident[:], ident[:], [[1, 128]], ALU.is_equal, 0.0, channel_multiplier=-1),
        waits=[ti], inc="st_g")
    sb = lambda name, shape, dt=BF16: ctx.enter_context(nc.sbuf_tensor(pref + name, shape, dt))
    scr = {
        "bufA": sb("qA", [128, 1408], F32),
        "bufB": sb("qB", [128, 1408], F32),
        "bm": sb("qbm", [128, 88], F32),
        "sc": sb("qsc", [128, 88], F32),
    }
    w32 = sb("w32", [128, 2816], F32)
    wq = sb("wq", [128, 2816])
    tstage = sb("tstage", [128, 22, 128])
    ps_T = ctx.enter_context(nc.psum_tensor(pref + "ps_T", [128, 128], BF16))
    tz = p.emit("gpsimd", lambda h: h.memset(tstage[:], 0.0), waits=[ti2], inc="st_g")
    last_load = None
    last_q = []
    last_T_evac = None
    last_st = None
    sdma = []
    for s in range(7):
        W = 1024 if s < 6 else 2816
        nkc = 8 if s < 6 else 22
        if s < 6:
            tl = p.dma("sync", w32[:, 0:1024], io["wrowsA"][s], waits=[last_load])
        else:
            tl = p.dma("sync", w32[:], io["wrowsB"], waits=[last_load])
        tq_all = []
        for (o, w_) in ([(0, 1024)] if s < 6 else [(0, 1408), (1408, 1408)]):
            tq_all.append(emit_quant(p, "vector", w32[:, o:o + w_], wq[:, o:o + w_], scr,
                          waits=[tl] + last_q + tq_all))
        last_load = tq_all[-1]
        last_q = tq_all
        evs = []
        for kc in range(nkc):
            tt = p.emit("tensor", lambda h, kc=kc: h.transpose(
                ps_T[:], wq[:, kc * 128:(kc + 1) * 128], ident[:]),
                waits=tq_all + [last_T_evac, tz], inc="pe")
            last_T_evac = p.emit("vector", lambda h, kc=kc: h.tensor_copy(
                tstage[:, kc, :], ps_T[:]), waits=[tt, last_st], inc="dve")
            evs.append(last_T_evac)
        last_q = [last_T_evac]
        last_st = p.dma("gpsimd", io["ag_in"][s].rearrange("kc p t -> p kc t"),
                        tstage[:], waits=evs + [tz])
        sdma.append(last_st)
    tcc = p.emit("gpsimd", lambda h: h.collective_compute(
        "AllGather", ALU.bypass, replica_groups=[[0, 1, 2, 3, 4, 5, 6, 7]],
        ins=[io["ag_in"].rearrange("s kc p t -> (s kc p) t").opt()],
        outs=[io["ag_out"].opt()]),
        waits=sdma, inc="cc")
    p.emit("gpsimd", lambda h: h.engine_nop(), waits=p.dma_all_done_waits() + [tcc])


def emit_D(p, ctx, nc, io):
    import os as _os
    D_STAGE = int(_os.environ.get("D_STAGE", "3"))
    """Launch D: L0 fp4 FFN: x2 = LN(x1 + fp4_lin(silu(fp4_lin(x1,wg)) * fp4_lin(x1,wu), wd))

    io: x1 [TL, C] f32; w_gate [I, C] f32; w_up [I, C] f32; w_down [C, I] f32
        -> x2 [TL, C] f32.  I = 2816.
    """
    I = 2816
    NI = I // 128
    pref = io.get("prefix", "")
    ident, _, const_toks = build_common_consts(p, ctx, nc, pref)
    sb = lambda name, shape, dt=BF16: ctx.enter_context(nc.sbuf_tensor(pref + name, shape, dt))
    ps = lambda name, shape, dt=F32: ctx.enter_context(nc.psum_tensor(pref + name, shape, dt))

    wgT = sb("wgT", [128, 8, I])
    wuT = sb("wuT", [128, 8, I])
    wdT = sb("wdT", [128, NI, C])
    scrV = {
        "bufA": sb("qvA", [128, 1408], F32),
        "bufB": sb("qvB", [128, 1408], F32),
        "bm": sb("qvbm", [128, 88], F32),
        "sc": sb("qvsc", [128, 88], F32),
    }
    magic = ctx.enter_context(nc.sbuf_tensor(pref + "magic", [128, 1], mybir.dt.int32))
    iscr = ctx.enter_context(nc.sbuf_tensor(pref + "iscr", [128, 1], mybir.dt.int32))
    nt1 = sb("nt1", [128, 1], F32)
    x32 = [sb(f"x32_{i}", [128, C], F32) for i in range(2)]
    xq = sb("xq", [128, C])
    xqT = sb("xqT", [128, 8, 128])
    hbuf = [sb(f"hbuf{i}", [128, I], F32) for i in range(2)]
    w32 = hbuf[0]
    eb = sb("eb", [128, 512], F32)
    hgb = sb("hgb", [128, 512], F32)
    hq = sb("hq", [128, I])
    wq = hq
    hqT = sb("hqT", [128, NI, 128])
    x2t = [sb(f"x2t_{i}", [128, C], F32) for i in range(2)]
    snm = sb("snm", [128, 1], F32)
    negm = sb("negm", [128, 1], F32)
    lnv = sb("lnv", [128, 1], F32)
    rstd = sb("rstd", [128, 1], F32)

    ps_g = ps("ps_g", [128, 512])
    ps_u = ps("ps_u", [128, 512])
    ps_T = [ps(f"ps_T{i}", [128, 128], BF16) for i in range(3)]
    ps_d = [ps(f"ps_d{i}", [128, 512]) for i in range(2)]

    STRIPS = [(0, 512), (512, 512), (1024, 512), (1536, 512), (2048, 512), (2560, 256)]

    tmagic = p.emit("gpsimd", lambda h: h.memset(magic[:], 0x5f3759df), inc="st_g")
    lte = [None, None, None]
    wdone = None
    if "ag_out" in io:
        agv = io["ag_out"].rearrange("(r s kc p) t -> r s kc p t", r=8, s=7, kc=22)
        wl = []
        for j, ent in enumerate(_DPREP):
            if ent is None:
                continue
            wname, r = ent
            rank, si = j % 8, j // 8
            nkc = 8 if si < 6 else 22
            dstT = {"w_gate": wgT, "w_up": wuT, "w_down": wdT}[wname]
            wl.append(p.dma("sync", dstT[:, 0:nkc, r * 128:(r + 1) * 128],
                            agv[rank, si, 0:nkc].rearrange("kc p t -> p kc t"),
                            waits=const_toks if not wl else []))
        wdone = wl[-1]
    else:
        wtoks = []
        last_wq_read = []
        last_w32_read = []
        for (wname, dstT, nrow, rowW) in (("w_gate", wgT, NI, C), ("w_up", wuT, NI, C),
                                          ("w_down", wdT, 8, I)):
            for r in range(nrow):
                tl = p.dma("sync", w32[:, 0:rowW],
                           io[wname].rearrange("(r p) k -> p r k", p=128)[:, r, :],
                           waits=list(last_w32_read) + (const_toks if r == 0 else []))
                halves = [(0, rowW)] if rowW <= 1408 else [(0, 1408), (1408, rowW - 1408)]
                tq_all = []
                for hi, (o, w_) in enumerate(halves):
                    tq_all.append(emit_quant(p, "vector", w32[:, o:o + w_], wq[:, o:o + w_],
                                  scrV, waits=[tl] + (last_wq_read or []) + tq_all))
                last_w32_read = tq_all
                for kc in range(rowW // 128):
                    bk = kc % 3
                    tt = p.emit("tensor", lambda h, kc=kc, bk=bk: h.transpose(
                        ps_T[bk][:], wq[:, kc * 128:(kc + 1) * 128], ident[:]),
                        waits=tq_all + [lte[bk]], inc="pe")
                    lte[bk] = p.emit("vector", lambda h, kc=kc, dstT=dstT, r=r, bk=bk:
                        h.tensor_copy(dstT[:, kc, r * 128:(r + 1) * 128], ps_T[bk][:]),
                        waits=[tt], inc="dve")
                last_wq_read = [t for t in lte if t]
                wtoks.append(last_wq_read[-1])
        wdone = wtoks[-1]

    S = dict(last_x_read=[None, None], last_xqT_evac=None, last_xq_read=None,
             last_h_read=[None, None], last_hq_read=None, last_ps_g_read=None,
             last_ps_u_read=None, last_ps_d_read=[None, None], last_x2_dma=[None, None],
             last_eb_read=None, last_hgb_read=None, last_hqT_read=None,
             lte=lte)
    carry = {}
    NT = TL // 128

    def stage_A1(t):
        par = t % 2
        hb = hbuf[par]
        tx = p.dma("sync", x32[par][:], io["x1"][t * 128:(t + 1) * 128, :],
                   waits=[S["last_x_read"][par]])
        tq = emit_quant(p, "vector", x32[par][:], xq[:], scrV,
                        waits=[tx, S["last_xq_read"], wdone])
        S["last_x_read"][par] = tq
        xT_toks = []
        for kc in range(8):
            bk = kc % 3
            tt = p.emit("tensor", lambda h, kc=kc, bk=bk: h.transpose(
                ps_T[bk][:], xq[:, kc * 128:(kc + 1) * 128], ident[:]),
                waits=[tq, S["lte"][bk]], inc="pe")
            S["lte"][bk] = p.emit("vector", lambda h, kc=kc, bk=bk: h.tensor_copy(
                xqT[:, kc, :], ps_T[bk][:]), waits=[tt, S["last_xqT_evac"]], inc="dve")
            xT_toks.append(S["lte"][bk])
        S["last_xq_read"] = xT_toks[-1]
        h_toks = []
        for (o, w_) in STRIPS:
            tg_ = None
            tu_ = None
            for kc in range(8):
                tg_ = p.emit("tensor", lambda h, kc=kc, o=o, w_=w_: h.matmul(
                    ps_g[:, 0:w_], xqT[:, kc, :], wgT[:, kc, o:o + w_],
                    start=(kc == 0), stop=(kc == 7)),
                    waits=[xT_toks[-1], S["last_ps_g_read"]] if kc == 0 else [], inc="pe")
            for kc in range(8):
                tu_ = p.emit("tensor", lambda h, kc=kc, o=o, w_=w_: h.matmul(
                    ps_u[:, 0:w_], xqT[:, kc, :], wuT[:, kc, o:o + w_],
                    start=(kc == 0), stop=(kc == 7)),
                    waits=[S["last_ps_u_read"]] if kc == 0 else [], inc="pe")
            S["last_xqT_evac"] = tu_
            te_ = p.emit("scalar", lambda h, o=o, w_=w_: h.activation(
                eb[:, 0:w_], ps_g[:, 0:w_], AF.Sigmoid),
                waits=[tg_, S["last_eb_read"]], inc="act")
            th1 = p.emit("vector", lambda h, o=o, w_=w_: h.tensor_tensor(
                hgb[:, 0:w_], ps_g[:, 0:w_], eb[:, 0:w_], ALU.mult),
                waits=[te_, S["last_hgb_read"]], inc="dve")
            S["last_ps_g_read"] = th1
            S["last_eb_read"] = th1
            th2 = p.emit("vector", lambda h, o=o, w_=w_, hb=hb: h.tensor_mul(
                hb[:, o:o + w_], hgb[:, 0:w_], ps_u[:, 0:w_]),
                waits=[th1, tu_, S["last_h_read"][par]], inc="dve")
            S["last_hgb_read"] = th2
            S["last_ps_u_read"] = th2
            h_toks.append(th2)
        carry[t] = h_toks

    def stage_B(t):
        par = t % 2
        hb = hbuf[par]
        lnA = hb[:, 0:1024]
        ymv = hb[:, 1024:2048]
        h_toks = carry.pop(t)
        tqh_l = []
        for hi, (o, w_) in enumerate(((0, 1408), (1408, 1408))):
            tqh_l.append(emit_quant(p, "vector", hb[:, o:o + w_], hq[:, o:o + w_],
                         scrV,
                         waits=h_toks + ([S["last_hq_read"]] if S["last_hq_read"] else [])
                         + tqh_l))
        tqh = tqh_l[-1]
        hT_toks = []
        for kc in range(NI):
            bk = kc % 3
            tt = p.emit("tensor", lambda h, kc=kc, bk=bk: h.transpose(
                ps_T[bk][:], hq[:, kc * 128:(kc + 1) * 128], ident[:]),
                waits=tqh_l + [S["lte"][bk]], inc="pe")
            S["lte"][bk] = p.emit("vector", lambda h, kc=kc, bk=bk: h.tensor_copy(
                hqT[:, kc, :], ps_T[bk][:]), waits=[tt, S["last_hqT_read"]], inc="dve")
            hT_toks.append(S["lte"][bk])
        S["last_hq_read"] = hT_toks[-1]
        dn_toks = []
        for n in range(2):
            td_ = None
            for kc in range(NI):
                td_ = p.emit("tensor", lambda h, kc=kc, n=n: h.matmul(
                    ps_d[n][:], hqT[:, kc, :], wdT[:, kc, n * 512:(n + 1) * 512],
                    start=(kc == 0), stop=(kc == NI - 1)),
                    waits=[hT_toks[-1], S["last_ps_d_read"][n]] if kc == 0 else [], inc="pe")
            dn_toks.append(td_)
        S["last_hqT_read"] = dn_toks[-1]
        ytoks = []
        for n in range(2):
            ty = p.emit("vector", lambda h, n=n, par=par, lnA=lnA: h.tensor_add(
                lnA[:, n * 512:(n + 1) * 512], ps_d[n][:],
                x32[par][:, n * 512:(n + 1) * 512]), waits=[dn_toks[n]] + tqh_l, inc="dve")
            S["last_ps_d_read"][n] = ty
            S["last_x_read"][par] = ty
            ytoks.append(ty)
        tr = p.emit("vector", lambda h, lnA=lnA: h.reduce_sum(snm[:], lnA, AX.X),
                    waits=ytoks, inc="dve")
        tn = p.emit("vector", lambda h: h.tensor_scalar_mul(negm[:], snm[:], -1.0 / C),
                    waits=[tr], inc="dve")
        tm = p.emit("vector", lambda h, lnA=lnA, ymv=ymv: h.tensor_scalar_add(
            ymv, lnA, negm[:]), waits=[tn], inc="dve")
        tsq1 = p.emit("scalar", lambda h, ymv=ymv: h.activation(
            eb[:, 0:512], ymv[:, 0:512], AF.Square),
            waits=[tm, S["last_eb_read"], ("act", p.counts["act"])], inc="act")
        tsq2 = p.emit("scalar", lambda h, ymv=ymv: h.activation(
            hgb[:, 0:512], ymv[:, 512:1024], AF.Square),
            waits=[tm, S["last_hgb_read"], ("act", p.counts["act"])], inc="act")
        tss1 = p.emit("vector", lambda h: h.reduce_sum(lnv[:], eb[:, 0:512], AX.X),
                      waits=[tsq1], inc="dve")
        tss2 = p.emit("vector", lambda h: h.reduce_sum(nt1[:], hgb[:, 0:512], AX.X),
                      waits=[tsq2], inc="dve")
        S["last_eb_read"] = tss2
        S["last_hgb_read"] = tss2
        tssb = p.emit("vector", lambda h: h.tensor_add(lnv[:], lnv[:], nt1[:]),
                      waits=[tss1, tss2], inc="dve")
        tva = p.emit("vector", lambda h: h.tensor_scalar(
            lnv[:], lnv[:], 1.0 / C, 1e-5, ALU.mult, ALU.add), waits=[tssb], inc="dve")
        tni = p.emit("vector", lambda h: h.tensor_scalar(
            iscr[:], lnv[:].bitcast(mybir.dt.int32), 1, None, ALU.arith_shift_right),
            waits=[tva], inc="dve")
        tny = p.emit("vector", lambda h: h.scalar_tensor_tensor(
            rstd[:].bitcast(mybir.dt.int32), iscr[:], -1, magic[:], ALU.mult, ALU.add),
            waits=[tni, tmagic], inc="dve")
        for _ in range(3):
            ta1 = p.emit("vector", lambda h: h.tensor_mul(nt1[:], rstd[:], rstd[:]),
                         waits=[tny], inc="dve")
            ta2 = p.emit("vector", lambda h: h.tensor_mul(nt1[:], nt1[:], lnv[:]),
                         waits=[ta1], inc="dve")
            ta3 = p.emit("vector", lambda h: h.tensor_scalar(
                nt1[:], nt1[:], -0.5, 1.5, ALU.mult, ALU.add), waits=[ta2], inc="dve")
            tny = p.emit("vector", lambda h: h.tensor_mul(rstd[:], rstd[:], nt1[:]),
                         waits=[ta3], inc="dve")
        tx2 = p.emit("vector", lambda h, par=par, ymv=ymv: h.tensor_scalar(
            x2t[par][:], ymv, rstd[:], None, ALU.mult),
            waits=[tny, S["last_x2_dma"][par]], inc="dve")
        S["last_x2_dma"][par] = p.dma("sync", io["x2"][t * 128:(t + 1) * 128, :],
                                      x2t[par][:], waits=[tx2])
        S["last_xq_read"] = tx2
        S["last_h_read"][par] = tx2

    for t in range(NT):
        stage_A1(t)
        if t >= 1:
            stage_B(t - 1)
    stage_B(NT - 1)

    p.emit("gpsimd", lambda h: h.engine_nop(), waits=p.dma_all_done_waits())


def build_D():
    nc = bass.Bass("TRN2", target_bir_lowering=False, debug=False)
    I = 2816
    io = {
        "x1": nc.dram_tensor("x1", [TL, C], F32, kind="ExternalInput").ap(),
        "w_gate": nc.dram_tensor("w_gate", [I, C], F32, kind="ExternalInput").ap(),
        "w_up": nc.dram_tensor("w_up", [I, C], F32, kind="ExternalInput").ap(),
        "w_down": nc.dram_tensor("w_down", [C, I], F32, kind="ExternalInput").ap(),
        "x2": nc.dram_tensor("x2", [TL, C], F32, kind="ExternalOutput").ap(),
    }
    finish_build(nc, emit_D, io)
    return nc




WQR = 2304
NWIN = 8


def emit_F1(p, ctx, nc, io):
    pref = io.get("prefix", "")
    ident, _, const_toks = build_common_consts(p, ctx, nc, pref)
    sb = lambda name, shape, dt=BF16: ctx.enter_context(nc.sbuf_tensor(pref + name, shape, dt))
    ps = lambda name, shape, dt=F32: ctx.enter_context(nc.psum_tensor(pref + name, shape, dt))

    w16 = sb("w16", [128, 8, 3072])
    w32t = sb("w32t", [128, 3072], F32)
    bqT = sb("bqT", [128, 24], F32)
    bq24 = sb("bq24", [24, 128], F32)
    ident32 = sb("ident32", [128, 128], F32)
    bvbc = sb("bvbc", [128, 1024], F32)
    xc32 = [sb(f"xc32_{i}", [128, 8, 512], F32) for i in range(2)]
    x16 = [sb(f"x16_{i}", [128, 8, 512]) for i in range(2)]
    qko = sb("qko", [128, 512])
    vo = sb("vo", [128, 1024], F32)
    vo16 = sb("vo16", [128, 1024])

    ps_mm = [ps(f"ps_mm{i}", [128, 512]) for i in range(2)]
    ps_b = ps("ps_b", [128, 24])

    wtoks = []
    last_cast = None
    for kc in range(8):
        tl = p.dma("sync", w32t[:], io["w_qkv"].rearrange("(kc p) n -> p kc n", p=128)[:, kc, :],
                   waits=[last_cast])
        last_cast = p.emit("vector", lambda h, kc=kc: h.tensor_copy(w16[:, kc, :], w32t[:]),
                           waits=[tl], inc="dve")
        wtoks.append(last_cast)
    ti = p.emit("gpsimd", lambda h: h.memset(ident32[:], 1.0), inc="st_g")
    ti2 = p.emit("gpsimd", lambda h: h.affine_select(
        ident32[:], ident32[:], [[1, 128]], ALU.is_equal, 0.0, channel_multiplier=-1),
        waits=[ti], inc="st_g")
    tb = p.dma("sync", bq24[:], io["b_qkvT"])
    tb2 = p.emit("tensor", lambda h: h.transpose(ps_b[:], bq24[:], ident32[0:24, 0:24]),
                 waits=[tb, ti2], inc="pe")
    tb3 = p.emit("vector", lambda h: h.tensor_copy(bqT[:], ps_b[:]), waits=[tb2], inc="dve")
    tbv = p.dma("sync", bvbc[:], io["b_qkvT"].rearrange("a b -> (a b)")[2048:3072]
                .unsqueeze(0).broadcast_to([128, 1024]))
    pre = [wtoks[-1], tb3, tbv] + const_toks

    CHUNKS = [(o, min(512, WQR - o)) for o in range(0, WQR, 512)]
    wqT_t = io["wqT"].rearrange("(kc p) t -> p kc t", p=128)
    last_mm_read_x = [None, None]
    last_evac = [None, None]
    last_qko_read = None
    last_vo_read = None
    last_v16_read = None
    wq_bf16 = io["wqT"].dtype == BF16
    for ci, (o, w_) in enumerate(CHUNKS):
        par = ci % 2
        if wq_bf16:
            tcast = p.dma("sync", x16[par][:, :, 0:w_], wqT_t[:, :, o:o + w_],
                          waits=[last_mm_read_x[par]] + (pre if ci == 0 else []))
        else:
            tl = p.dma("sync", xc32[par][:, :, 0:w_], wqT_t[:, :, o:o + w_],
                       waits=[last_mm_read_x[par]] + (pre if ci == 0 else []))
            tcast = p.emit("vector", lambda h, par=par, w_=w_: h.tensor_copy(
                x16[par][:, :, 0:w_], xc32[par][:, :, 0:w_]), waits=[tl], inc="dve")
        for j in range(16):
            i = j % 2
            t = None
            for kc in range(8):
                t = p.emit("tensor", lambda h, kc=kc, j=j, i=i, par=par, w_=w_: h.matmul(
                    ps_mm[i][:, 0:w_], w16[:, kc, j * 128:(j + 1) * 128],
                    x16[par][:, kc, 0:w_], start=(kc == 0), stop=(kc == 7)),
                    waits=[tcast, last_evac[i]] if kc == 0 else [], inc="pe")
            last_mm_read_x[par] = t
            tev = p.emit("scalar", lambda h, i=i, j=j, w_=w_: h.activation(
                qko[:, 0:w_], ps_mm[i][:, 0:w_], AF.Identity, bias=bqT[:, j:j + 1]),
                waits=[t, last_qko_read], inc="act")
            last_evac[i] = tev
            last_qko_read = p.dma("gpsimd", io["qk2"][j * 128:(j + 1) * 128, o:o + w_],
                                  qko[:, 0:w_], waits=[tev])
        for tt_ in range(w_ // 128):
            gl = (o + tt_ * 128) // 128
            for n in range(2):
                i = n % 2
                t = None
                for kc in range(8):
                    t = p.emit("tensor", lambda h, kc=kc, n=n, i=i, par=par, tt_=tt_: h.matmul(
                        ps_mm[i][:, :], x16[par][:, kc, tt_ * 128:(tt_ + 1) * 128],
                        w16[:, kc, 2048 + n * 512:2048 + (n + 1) * 512],
                        start=(kc == 0), stop=(kc == 7)),
                        waits=[tcast, last_evac[i]] if kc == 0 else [], inc="pe")
                last_mm_read_x[par] = t
                tev = p.emit("vector", lambda h, n=n, i=i: h.tensor_add(
                    vo[:, n * 512:(n + 1) * 512], ps_mm[i][:], bvbc[:, n * 512:(n + 1) * 512]),
                    waits=[t, tbv, last_vo_read], inc="dve")
                last_evac[i] = tev
            tv16 = p.emit("vector", lambda h: h.tensor_copy(vo16[:], vo[:]),
                          waits=[last_evac[0], last_evac[1], last_v16_read], inc="dve")
            last_vo_read = tv16
            last_v16_read = p.dma("gpsimd", io["vv"][gl * 128:(gl + 1) * 128, :], vo16[:],
                                  waits=[tv16])

    p.emit("gpsimd", lambda h: h.engine_nop(), waits=p.dma_all_done_waits())


def build_F1():
    nc = bass.Bass("TRN2", target_bir_lowering=False, debug=False)
    io = {
        "wqT": nc.dram_tensor("wqT", [C, WQR], F32, kind="ExternalInput").ap(),
        "w_qkv": nc.dram_tensor("w_qkv", [C, 3 * C], F32, kind="ExternalInput").ap(),
        "b_qkvT": nc.dram_tensor("b_qkvT", [24, 128], F32, kind="ExternalInput").ap(),
        "qk2": nc.dram_tensor("qk2", [2048, WQR], BF16, kind="ExternalOutput").ap(),
        "vv": nc.dram_tensor("vv", [WQR, C], BF16, kind="ExternalOutput").ap(),
    }
    finish_build(nc, emit_F1, io)
    return nc


def emit_F2(p, ctx, nc, io):
    pref = io.get("prefix", "")
    ident, _, const_toks = build_common_consts(p, ctx, nc, pref)
    sb = lambda name, shape, dt=BF16: ctx.enter_context(nc.sbuf_tensor(pref + name, shape, dt))
    ps = lambda name, shape, dt=F32: ctx.enter_context(nc.psum_tensor(pref + name, shape, dt))

    w16o = sb("w16o", [128, 8, 1024])
    w32t = sb("w32t", [128, 1024], F32)
    bout_bc = sb("bout_bc", [128, 1024], F32)
    qk_all = [sb(f"qk_all{i}", [128, 16, 512]) for i in range(2)]
    v_all = [sb(f"v_all{i}", [128, 4, 1024]) for i in range(2)]
    v1b = [sb(f"v1b_{i}", [128, 4, 65]) for i in range(2)]
    Ebuf = sb("Ebuf", [128, 4, 512])
    af = sb("af", [128, 8, 512])
    obuf = [sb(f"obuf{i}", [128, 64]) for i in range(2)]
    den = [sb(f"den{i}", [128, 1], F32) for i in range(2)]
    rden = [sb(f"rden{i}", [128, 1], F32) for i in range(2)]
    xt32 = [sb(f"xt32_{i}", [128, 1024], F32) for i in range(2)]
    yt = sb("yt", [128, 1024], F32)
    ym = sb("ym", [128, 1024], F32)
    sqt = sb("sqt", [128, 1024], F32)
    snm = sb("snm", [128, 1], F32)
    negm = sb("negm", [128, 1], F32)
    lnv = sb("lnv", [128, 1], F32)
    rstd = sb("rstd", [128, 1], F32)
    x3t = [sb(f"x3t_{i}", [128, 1024], F32) for i in range(2)]

    want_x3T = "x3T16" in io
    if want_x3T:
        x3b = sb("x3b", [128, 1024])
        xTst = sb("xTst", [128, 8, 128])
    ps_S = [ps(f"ps_S{i}", [128, 512]) for i in range(2)]
    ps_ctx = [ps(f"ps_ctx{i}", [128, 65]) for i in range(2)]
    ps_T = [ps(f"ps_T{i}", [128, 128], BF16) for i in range(2)]
    ps_o = [ps(f"ps_o{i}", [128, 512]) for i in range(2)]

    last_cast = None
    wtoks = []
    for kc in range(8):
        tl = p.dma("sync", w32t[:], io["w_out"].rearrange("(kc p) n -> p kc n", p=128)[:, kc, :],
                   waits=[last_cast])
        last_cast = p.emit("vector", lambda h, kc=kc: h.tensor_copy(w16o[:, kc, :], w32t[:]),
                           waits=[tl], inc="dve")
        wtoks.append(last_cast)
    tbb = p.dma("sync", bout_bc[:], io["b_out"].unsqueeze(0).broadcast_to([128, C]))
    tv1 = [p.emit("gpsimd", lambda h, i=i, tb=tb: h.memset(v1b[i][:, tb, 64:65], 1.0), inc="st_g")
           for i in range(2) for tb in range(4)]
    pre = [wtoks[-1], tbb] + tv1 + const_toks

    last_ctx_div = [None, None]
    last_S_evac = [None, None]
    last_E_read = None
    last_v1b_read = [None, None]
    last_obuf_read = [None, None]
    last_T_evac = [None, None]
    last_af_read = None
    last_ps_o_read = [None, None]
    last_x_read = [None, None]
    last_x3_dma = [None, None]
    last_qk_read = [None, None]
    last_vall_read = [None, None]
    last_x3b_read = None
    last_xTst_dma = None

    for n in range(NWIN):
        par = n % 2
        tq = p.dma("sync", qk_all[par][:],
                   io["qk2"].rearrange("(nt p) t -> p nt t", p=128)[:, :, n * 256:n * 256 + 512],
                   waits=[last_qk_read[par]] + (pre if n == 0 else []))
        tv = p.dma("sync", v_all[par][:],
                   io["vv"].rearrange("(tb p) d -> p tb d", p=128)[:, 2 * n:2 * n + 4, :]
                   if False else
                   io["vv"][n * 256:n * 256 + 512, :].rearrange("(tb p) d -> p tb d", p=128),
                   waits=[last_vall_read[par]])
        af_evacs = []
        t0 = None
        for hh in range(16):
            hp = hh % 2
            qrow = 64 * (hh % 2)
            qnt = hh // 2
            knt = 8 + hh // 2
            vcp = []
            for tb in range(4):
                vcp.append(p.emit("vector", lambda h, hp=hp, tb=tb, par=par, hh=hh: h.tensor_copy(
                    v1b[hp][:, tb, 0:64], v_all[par][:, tb, hh * 64:(hh + 1) * 64]),
                    waits=[tv, last_v1b_read[hp]] + tv1, inc="dve"))
            last_vall_read[par] = vcp[-1]
            ev_toks = []
            for i in range(4):
                sp_ = i % 2
                tS = p.emit("tensor", lambda h, i=i, sp_=sp_, hh=hh, par=par, qrow=qrow,
                            qnt=qnt, knt=knt: h.matmul(
                    ps_S[sp_][:], qk_all[par][qrow:qrow + 64, knt, i * 128:(i + 1) * 128],
                    qk_all[par][qrow:qrow + 64, qnt, :], start=True, stop=True),
                    waits=[tq] + (last_S_evac[sp_] or []), inc="pe")
                te_ = p.emit("scalar", lambda h, i=i, sp_=sp_: h.activation(
                    Ebuf[:, i, :], ps_S[sp_][:], AF.Exp, scale=0.125),
                    waits=[tS, last_E_read], inc="act")
                ev_toks.append(te_)
                last_S_evac[sp_] = [te_]
            for j in range(4):
                cp_ = j % 2
                t0 = None
                for i in range(4):
                    t0 = p.emit("tensor", lambda h, i=i, j=j, cp_=cp_, hp=hp: h.matmul(
                        ps_ctx[cp_][:], Ebuf[:, i, j * 128:(j + 1) * 128], v1b[hp][:, i, :],
                        start=(i == 0), stop=(i == 3)),
                        waits=(ev_toks + vcp + [last_ctx_div[cp_]]) if i == 0 else [], inc="pe")
                last_E_read = t0
                last_v1b_read[hp] = t0
                bk = j % 2
                td1 = p.emit("vector", lambda h, cp_=cp_, bk=bk: h.tensor_scalar_max(
                    den[bk][:], ps_ctx[cp_][:, 64:65], 1e-30), waits=[t0], inc="dve")
                td2 = p.emit("vector", lambda h, bk=bk: h.reciprocal(rden[bk][:], den[bk][:]),
                             waits=[td1], inc="dve")
                td3 = p.emit("vector", lambda h, cp_=cp_, bk=bk: h.tensor_scalar(
                    obuf[bk][:], ps_ctx[cp_][:, 0:64], rden[bk][:], None, ALU.mult),
                    waits=[td2, last_obuf_read[bk]], inc="dve")
                last_ctx_div[cp_] = td3
                tT = p.emit("tensor", lambda h, bk=bk: h.transpose(
                    ps_T[bk][0:64, :], obuf[bk][:], ident[:]),
                    waits=[td3, last_T_evac[bk]], inc="pe")
                last_obuf_read[bk] = tT
                te = p.emit("vector", lambda h, hh=hh, j=j, bk=bk: h.tensor_copy(
                    af[64 * (hh % 2):64 * (hh % 2) + 64, hh // 2, j * 128:(j + 1) * 128],
                    ps_T[bk][0:64, :]), waits=[tT, last_af_read], inc="dve")
                last_T_evac[bk] = te
                af_evacs.append(te)
        last_qk_read[par] = t0
        for j in range(4):
            xp = j % 2
            row0 = n * 512 + j * 128
            tx = p.dma("sync", xt32[xp][:], io["x2own"][row0:row0 + 128, :],
                       waits=[last_x_read[xp]])
            ytoks = []
            for nn in range(2):
                tmo = None
                for kc in range(8):
                    tmo = p.emit("tensor", lambda h, kc=kc, j=j, nn=nn: h.matmul(
                        ps_o[nn][:], af[:, kc, j * 128:(j + 1) * 128],
                        w16o[:, kc, nn * 512:(nn + 1) * 512],
                        start=(kc == 0), stop=(kc == 7)),
                        waits=(af_evacs + [last_ps_o_read[nn]]) if kc == 0 else [], inc="pe")
                last_af_read = tmo
                ty1 = p.emit("vector", lambda h, nn=nn, xp=xp: h.tensor_add(
                    yt[:, nn * 512:(nn + 1) * 512], ps_o[nn][:],
                    xt32[xp][:, nn * 512:(nn + 1) * 512]), waits=[tmo, tx], inc="dve")
                last_x_read[xp] = ty1
                ty2 = p.emit("vector", lambda h, nn=nn: h.tensor_add(
                    yt[:, nn * 512:(nn + 1) * 512], yt[:, nn * 512:(nn + 1) * 512],
                    bout_bc[:, nn * 512:(nn + 1) * 512]), waits=[ty1, tbb], inc="dve")
                last_ps_o_read[nn] = ty1
                ytoks.append(ty2)
            tr = p.emit("vector", lambda h: h.reduce_sum(snm[:], yt[:], AX.X),
                        waits=ytoks, inc="dve")
            tn = p.emit("vector", lambda h: h.tensor_scalar_mul(negm[:], snm[:], -1.0 / C),
                        waits=[tr], inc="dve")
            tm = p.emit("vector", lambda h: h.tensor_scalar_add(ym[:], yt[:], negm[:]),
                        waits=[tn], inc="dve")
            tsq = p.emit("scalar", lambda h: h.activation(sqt[:], ym[:], AF.Square),
                         waits=[tm], inc="act")
            tss = p.emit("vector", lambda h: h.reduce_sum(lnv[:], sqt[:], AX.X),
                         waits=[tsq], inc="dve")
            tva = p.emit("vector", lambda h: h.tensor_scalar(
                lnv[:], lnv[:], 1.0 / C, 1e-5, ALU.mult, ALU.add), waits=[tss], inc="dve")
            tln = p.emit("scalar", lambda h: h.activation(lnv[:], lnv[:], AF.Ln),
                         waits=[tva], inc="act")
            trs = p.emit("scalar", lambda h: h.activation(rstd[:], lnv[:], AF.Exp, scale=-0.5),
                         waits=[tln], inc="act")
            tx3 = p.emit("vector", lambda h, xp=xp: h.tensor_scalar(
                x3t[xp][:], ym[:], rstd[:], None, ALU.mult),
                waits=[trs, last_x3_dma[xp]], inc="dve")
            last_x3_dma[xp] = p.dma("gpsimd", io["x3"][row0:row0 + 128, :], x3t[xp][:],
                                    waits=[tx3])
            if want_x3T:
                tc16 = p.emit("vector", lambda h, xp=xp: h.tensor_copy(x3b[:], x3t[xp][:]),
                              waits=[tx3] + (last_x3b_read or []), inc="dve")
                last_x3_dma[xp] = tc16
                evs = []
                for kc in range(8):
                    bk2 = kc % 2
                    ttx = p.emit("tensor", lambda h, kc=kc, bk2=bk2: h.transpose(
                        ps_T[bk2][:], x3b[:, kc * 128:(kc + 1) * 128], ident[:]),
                        waits=[tc16, last_T_evac[bk2]], inc="pe")
                    last_T_evac[bk2] = p.emit("vector", lambda h, kc=kc, bk2=bk2: h.tensor_copy(
                        xTst[:, kc, :], ps_T[bk2][:]), waits=[ttx, last_xTst_dma], inc="dve")
                    evs.append(last_T_evac[bk2])
                last_x3b_read = [t for t in last_T_evac if t]
                last_xTst_dma = p.dma("gpsimd",
                    io["x3T16"].rearrange("(kc p) t -> p kc t", p=128)[:, :, row0:row0 + 128],
                    xTst[:], waits=evs)

    p.emit("gpsimd", lambda h: h.engine_nop(), waits=p.dma_all_done_waits())


def build_F2():
    nc = bass.Bass("TRN2", target_bir_lowering=False, debug=False)
    io = {
        "qk2": nc.dram_tensor("qk2", [2048, WQR], BF16, kind="ExternalInput").ap(),
        "vv": nc.dram_tensor("vv", [WQR, C], BF16, kind="ExternalInput").ap(),
        "w_out": nc.dram_tensor("w_out", [C, C], F32, kind="ExternalInput").ap(),
        "b_out": nc.dram_tensor("b_out", [C], F32, kind="ExternalInput").ap(),
        "x2own": nc.dram_tensor("x2own", [TL, C], F32, kind="ExternalInput").ap(),
        "x3": nc.dram_tensor("x3", [TL, C], F32, kind="ExternalOutput").ap(),
    }
    finish_build(nc, emit_F2, io)
    return nc


def emit_H(p, ctx, nc, io):
    pref = io.get("prefix", "")
    sb = lambda name, shape, dt=BF16: ctx.enter_context(nc.sbuf_tensor(pref + name, shape, dt))
    ps = lambda name, shape, dt=F32: ctx.enter_context(nc.psum_tensor(pref + name, shape, dt))

    swT = sb("swT", [128, 8, 1024])
    w32t = sb("w32t", [128, 1024], F32)
    sc_bc = sb("sc_bc", [128, 1024], F32)
    bi_bc = sb("bi_bc", [128, 1024], F32)
    xc32 = [sb(f"xc32_{i}", [128, 8, 128], F32) for i in range(2)]
    x16 = [sb(f"x16_{i}", [128, 8, 128]) for i in range(2)]
    xt32 = [sb(f"xt32_{i}", [128, 1024], F32) for i in range(2)]
    yt = [sb(f"yt{i}", [128, 1024], F32) for i in range(2)]
    ym = [sb(f"ym{i}", [128, 1024], F32) for i in range(2)]
    sqt = [sb(f"sqt{i}", [128, 1024], F32) for i in range(2)]
    snm = [sb(f"snm{i}", [128, 1], F32) for i in range(2)]
    negm = [sb(f"negm{i}", [128, 1], F32) for i in range(2)]
    lnv = [sb(f"lnv{i}", [128, 1], F32) for i in range(2)]
    rstd = [sb(f"rstd{i}", [128, 1], F32) for i in range(2)]
    ot = [sb(f"ot_{i}", [128, 1024], F32) for i in range(2)]

    ps_o = [ps(f"ps_o{i}", [128, 512]) for i in range(2)]

    wtoks = []
    last_sw = None
    for kc in range(8):
        tl = p.dma("sync", w32t[:], io["weightsT"].rearrange("(kc p) n -> p kc n", p=128)[:, kc, :],
                   waits=[last_sw])
        t1 = p.emit("vector", lambda h, kc=kc: h.tensor_scalar(
            w32t[:], w32t[:], 0.0, 2.0, ALU.is_ge, ALU.mult), waits=[tl], inc="dve")
        last_sw = p.emit("vector", lambda h, kc=kc: h.tensor_scalar_add(
            swT[:, kc, :], w32t[:], -1.0), waits=[t1], inc="dve")
        wtoks.append(last_sw)
    tsc = p.dma("sync", sc_bc[:], io["scale"].unsqueeze(0).broadcast_to([128, C]))
    tbi = p.dma("sync", bi_bc[:], io["bias"].unsqueeze(0).broadcast_to([128, C]))
    pre = [wtoks[-1], tsc, tbi]

    x3T_t = io["x3T"].rearrange("(kc p) t -> p kc t", p=128)
    x3T_bf16 = io["x3T"].dtype == BF16
    last_x16_read = [None, None]
    last_x_read = [None, None]
    last_o_dma = [None, None]
    last_ps_o_read = [None, None]
    last_yt_read = [None, None]
    NT = TL // 128
    for t in range(NT):
        par = t % 2
        if x3T_bf16:
            tcast = p.dma("sync", x16[par][:], x3T_t[:, :, t * 128:(t + 1) * 128],
                          waits=[last_x16_read[par]] + (pre if t == 0 else []))
        else:
            tl = p.dma("sync", xc32[par][:], x3T_t[:, :, t * 128:(t + 1) * 128],
                       waits=[last_x16_read[par]] + (pre if t == 0 else []))
            tcast = p.emit("vector", lambda h, par=par: h.tensor_copy(
                x16[par][:].rearrange("p a b -> p (a b)"),
                xc32[par][:].rearrange("p a b -> p (a b)")),
                waits=[tl], inc="dve")
        tx = p.dma("sync", xt32[par][:], io["x3"][t * 128:(t + 1) * 128, :],
                   waits=[last_x_read[par]])
        ytoks = []
        tmo = None
        for nn in range(2):
            for kc in range(8):
                tmo = p.emit("tensor", lambda h, kc=kc, nn=nn, par=par: h.matmul(
                    ps_o[nn][:], x16[par][:, kc, :], swT[:, kc, nn * 512:(nn + 1) * 512],
                    start=(kc == 0), stop=(kc == 7)),
                    waits=[tcast, last_ps_o_read[nn]] if kc == 0 else [], inc="pe")
            ty1 = p.emit("vector", lambda h, nn=nn, par=par: h.tensor_mul(
                yt[par][:, nn * 512:(nn + 1) * 512], ps_o[nn][:],
                sc_bc[:, nn * 512:(nn + 1) * 512]),
                waits=[tmo, tsc, last_yt_read[par]], inc="dve")
            last_ps_o_read[nn] = ty1
            ty2 = p.emit("vector", lambda h, nn=nn, par=par: h.tensor_add(
                yt[par][:, nn * 512:(nn + 1) * 512], yt[par][:, nn * 512:(nn + 1) * 512],
                bi_bc[:, nn * 512:(nn + 1) * 512]), waits=[ty1, tbi], inc="dve")
            ty3 = p.emit("vector", lambda h, nn=nn, par=par: h.tensor_scalar(
                yt[par][:, nn * 512:(nn + 1) * 512], yt[par][:, nn * 512:(nn + 1) * 512],
                100.0, -100.0, ALU.min, ALU.max), waits=[ty2], inc="dve")
            ty4 = p.emit("vector", lambda h, nn=nn, par=par: h.tensor_add(
                yt[par][:, nn * 512:(nn + 1) * 512], yt[par][:, nn * 512:(nn + 1) * 512],
                xt32[par][:, nn * 512:(nn + 1) * 512]), waits=[ty3, tx], inc="dve")
            ytoks.append(ty4)
        last_x16_read[par] = tmo
        last_x_read[par] = ytoks[-1]
        tr = p.emit("vector", lambda h, par=par: h.reduce_sum(snm[par][:], yt[par][:], AX.X),
                    waits=ytoks, inc="dve")
        tn = p.emit("vector", lambda h, par=par: h.tensor_scalar_mul(
            negm[par][:], snm[par][:], -1.0 / C), waits=[tr], inc="dve")
        tm = p.emit("vector", lambda h, par=par: h.tensor_scalar_add(
            ym[par][:], yt[par][:], negm[par][:]), waits=[tn], inc="dve")
        tsq = p.emit("scalar", lambda h, par=par: h.activation(
            sqt[par][:], ym[par][:], AF.Square),
            waits=[tm, ("act", p.counts["act"])], inc="act")
        tss = p.emit("vector", lambda h, par=par: h.reduce_sum(
            lnv[par][:], sqt[par][:], AX.X), waits=[tsq], inc="dve")
        tva = p.emit("vector", lambda h, par=par: h.tensor_scalar(
            lnv[par][:], lnv[par][:], 1.0 / C, 1e-5, ALU.mult, ALU.add),
            waits=[tss], inc="dve")
        tln = p.emit("scalar", lambda h, par=par: h.activation(
            lnv[par][:], lnv[par][:], AF.Ln),
            waits=[tva, ("act", p.counts["act"])], inc="act")
        trs = p.emit("scalar", lambda h, par=par: h.activation(
            rstd[par][:], lnv[par][:], AF.Exp, scale=-0.5), waits=[tln], inc="act")
        to_ = p.emit("vector", lambda h, par=par: h.tensor_scalar(
            ot[par][:], ym[par][:], rstd[par][:], None, ALU.mult),
            waits=[trs, last_o_dma[par]], inc="dve")
        last_yt_read[par] = to_
        last_o_dma[par] = p.dma("gpsimd", io["out"][t * 128:(t + 1) * 128, :], ot[par][:],
                                waits=[to_])

    p.emit("gpsimd", lambda h: h.engine_nop(), waits=p.dma_all_done_waits())


def build_H():
    nc = bass.Bass("TRN2", target_bir_lowering=False, debug=False)
    io = {
        "x3": nc.dram_tensor("x3", [TL, C], F32, kind="ExternalInput").ap(),
        "x3T": nc.dram_tensor("x3T", [C, TL], F32, kind="ExternalInput").ap(),
        "weightsT": nc.dram_tensor("weightsT", [C, C], F32, kind="ExternalInput").ap(),
        "scale": nc.dram_tensor("scale", [C], F32, kind="ExternalInput").ap(),
        "bias": nc.dram_tensor("bias", [C], F32, kind="ExternalInput").ap(),
        "out": nc.dram_tensor("out", [TL, C], F32, kind="ExternalOutput").ap(),
    }
    finish_build(nc, emit_H, io)
    return nc




def emit_E1(p, ctx, nc, io):
    pref = io.get("prefix", "")
    sb = lambda name, shape, dt=F32: ctx.enter_context(nc.sbuf_tensor(pref + name, shape, dt))
    m0 = sb("m0", [64, 1])
    m1 = sb("m1", [64, 1])
    stb = sb("stb", [64, 16, 65])
    cb = sb("cb", [64, 16, 65])
    tm0 = p.dma("sync", m0[:], io["msk"][0:1].unsqueeze(0).broadcast_to([64, 1]))
    tm1 = p.dma("sync", m1[:], io["msk"][1:2].unsqueeze(0).broadcast_to([64, 1]))
    tl = p.dma("sync", stb[:], io["stot"].rearrange("h f e -> f h e"))
    t1 = p.emit("vector", lambda h: h.tensor_scalar(
        cb[:].rearrange("p a b -> p (a b)"), stb[:].rearrange("p a b -> p (a b)"),
        m0[:], None, ALU.mult), waits=[tm0, tl], inc="dve")
    t2 = p.dma("gpsimd", io["e1in"].ap(), cb[:].rearrange("p a b -> p (a b)"), waits=[t1])
    t3 = p.emit("gpsimd", lambda h: h.collective_compute(
        "AllReduce", ALU.add, replica_groups=[[0, 1], [2, 3], [4, 5], [6, 7]],
        ins=[io["e1in"].ap().opt()], outs=[io["e1out"].ap().opt()]),
        waits=[t2], inc="cc")
    t4 = p.dma("sync", cb[:].rearrange("p a b -> p (a b)"), io["e1out"].ap(), waits=[t3, t1])
    t5 = p.emit("vector", lambda h: h.tensor_scalar(
        stb[:].rearrange("p a b -> p (a b)"), cb[:].rearrange("p a b -> p (a b)"),
        m1[:], None, ALU.mult), waits=[tm1, t4], inc="dve")
    p.dma("gpsimd", io["init"].rearrange("h f e -> f h e"), stb[:], waits=[t5])


def emit_E2(p, ctx, nc, io):
    pref = io.get("prefix", "")
    sb = lambda name, shape, dt=F32: ctx.enter_context(nc.sbuf_tensor(pref + name, shape, dt))
    ps = lambda name, shape, dt=F32: ctx.enter_context(nc.psum_tensor(pref + name, shape, dt))
    ident = ctx.enter_context(nc.sbuf_tensor(pref + "ident", [128, 128], BF16))
    ti = p.emit("gpsimd", lambda h: h.memset(ident[:], 1.0), inc="st_g")
    ti2 = p.emit("gpsimd", lambda h: h.affine_select(
        ident[:], ident[:], [[1, 128]], ALU.is_equal, 0.0, channel_multiplier=-1),
        waits=[ti], inc="st_g")
    m0 = sb("m0", [128, 1])
    m1 = sb("m1", [128, 1])
    xa = [sb(f"xa{i}", [128, 1024]) for i in range(2)]
    xb = [sb(f"xb{i}", [128, 1024]) for i in range(2)]
    rb16 = [sb(f"rb16_{i}", [128, 1024], BF16) for i in range(2)]
    cb16 = sb("cb16", [128, 1024], BF16)
    wq32 = sb("wq32", [128, 1024])
    wq16 = sb("wq16", [128, 1024], BF16)
    xTst = sb("xTst", [128, 8, 128], BF16)
    ps_T = ps("ps_T", [128, 128], BF16)
    tm0 = p.dma("sync", m0[:], io["msk"][0:1].unsqueeze(0).broadcast_to([128, 1]))
    tm1 = p.dma("sync", m1[:], io["msk"][1:2].unsqueeze(0).broadcast_to([128, 1]))
    last_c = [None, None]
    last_cb = None
    cdma = []
    for i in range(16):
        pp = i % 2
        tl = p.dma("sync", xa[pp][:], io["x2"][2048 + i * 128:2048 + (i + 1) * 128, :],
                   waits=[last_c[pp]])
        tc = p.emit("vector", lambda h, pp=pp: h.tensor_scalar(
            cb16[:], xa[pp][:], m0[:], None, ALU.mult), waits=[tl, tm0, last_cb], inc="dve")
        last_c[pp] = tc
        last_cb = p.dma("gpsimd", io["e2in"][i * 128:(i + 1) * 128, :], cb16[:], waits=[tc])
        cdma.append(last_cb)
    t3 = p.emit("gpsimd", lambda h: h.collective_compute(
        "AllReduce", ALU.add, replica_groups=[[0, 1], [2, 3], [4, 5], [6, 7]],
        ins=[io["e2in"].opt()], outs=[io["e2out"].opt()]),
        waits=cdma, inc="cc")
    last_T_evac = None
    last_xTst_dma = None
    last_rb = [None, None]
    last_wq16_read = None
    tw = None
    for i in range(18):
        pp = i % 2
        tla = p.dma("sync", xa[pp][:], io["x2"][i * 128:(i + 1) * 128, :],
                    waits=[last_c[pp]])
        t1 = p.emit("vector", lambda h, pp=pp: h.tensor_scalar(
            wq32[:], xa[pp][:], m0[:], None, ALU.mult),
            waits=[tla, tm0, tw], inc="dve")
        last_c[pp] = t1
        if i < 16:
            tlb = p.dma("sync", rb16[pp][:], io["e2out"][i * 128:(i + 1) * 128, :],
                        waits=[t3, last_rb[pp]])
            t2 = p.emit("vector", lambda h, pp=pp: h.scalar_tensor_tensor(
                wq32[:], rb16[pp][:], m1[:], wq32[:], ALU.mult, ALU.add),
                waits=[tlb, t1, tm1], inc="dve")
            last_rb[pp] = t2
        else:
            tlb = p.dma("sync", xb[pp][:], io["x2"][(i - 16) * 128:(i - 15) * 128, :],
                        waits=[last_rb[pp]])
            t2 = p.emit("vector", lambda h, pp=pp: h.scalar_tensor_tensor(
                wq32[:], xb[pp][:], m1[:], wq32[:], ALU.mult, ALU.add),
                waits=[tlb, t1, tm1], inc="dve")
            last_rb[pp] = t2
        tw = p.emit("vector", lambda h: h.tensor_copy(wq16[:], wq32[:]),
                    waits=[t2, last_wq16_read], inc="dve")
        evs = []
        for kc in range(8):
            ttx = p.emit("tensor", lambda h, kc=kc: h.transpose(
                ps_T[:], wq16[:, kc * 128:(kc + 1) * 128], ident[:]),
                waits=[tw, ti2, last_T_evac], inc="pe")
            last_T_evac = p.emit("vector", lambda h, kc=kc: h.tensor_copy(
                xTst[:, kc, :], ps_T[:]), waits=[ttx, last_xTst_dma], inc="dve")
            evs.append(last_T_evac)
        last_wq16_read = last_T_evac
        last_xTst_dma = p.dma("gpsimd",
            io["wqT16"].rearrange("(kc p) t -> p kc t", p=128)[:, :, i * 128:(i + 1) * 128],
            xTst[:], waits=evs)
    p.emit("gpsimd", lambda h: h.engine_nop(), waits=p.dma_all_done_waits())


def build_A():
    nc = bass.Bass("TRN2", target_bir_lowering=False, debug=False)
    io = {
        "xT": nc.dram_tensor("xT", [C, TL], F32, kind="ExternalInput").ap(),
        "Wqkv": nc.dram_tensor("Wqkv", [C, 3 * C], F32, kind="ExternalInput").ap(),
        "bqkvT": nc.dram_tensor("bqkvT", [24, 128], F32, kind="ExternalInput").ap(),
        "omh": nc.dram_tensor("omh", [FD, H * FD], F32, kind="ExternalInput").ap(),
        "qn": nc.dram_tensor("qn", [2], F32, kind="ExternalInput").ap(),
        "qw": nc.dram_tensor("qw", [2], F32, kind="ExternalInput").ap(),
        "qf": nc.dram_tensor("qf", [NCH, FD, H, CH], BF16, kind="ExternalOutput").ap(),
        "kf": nc.dram_tensor("kf", [NCH, FD, H, CH], BF16, kind="ExternalOutput").ap(),
        "vt": nc.dram_tensor("vt", [H, NCH, 128, 4, HD], BF16, kind="ExternalOutput").ap(),
        "sprev": nc.dram_tensor("sprev", [NCH - 1, H, FD, 65], F32, kind="ExternalOutput").ap(),
        "stot": nc.dram_tensor("stot", [H, FD, 65], F32, kind="ExternalOutput").ap(),
    }
    finish_build(nc, emit_A, io)
    return nc


def finish_build_multi(nc, phases):
    import contextlib
    groups = []
    for ent in phases:
        if len(ent) == 3 and ent[2] and groups:
            groups[-1].append(ent[:2])
        else:
            groups.append([ent[:2]])
    with contextlib.ExitStack() as outer:
        p = Prog(nc)
        for s in ([f"ld{i}" for i in range(Prog.LD_POOL)]
                  + [f"st{i}" for i in range(Prog.ST_POOL)]
                  + ["st_g", "pe", "dve", "act", "cc"]):
            p.sem(s)
        sems = {}
        for s in p.sem_names:
            sems[s] = outer.enter_context(nc.semaphore(s))
        block = outer.enter_context(nc.Block())
        for grp in groups:
            with contextlib.ExitStack() as ctx:
                for gi, (emitter, io) in enumerate(grp):
                    if gi > 0:
                        ws = p.dma_all_done_waits() + (
                            [("cc", p.counts["cc"])] if p.counts["cc"] else [])
                        p.pending_barrier = {"sync": list(ws), "gpsimd": list(ws)}
                    emitter(p, ctx, nc, io)
            p.set_barrier()
        p.build(block, sems)


def build_fused(upto=8):
    nc = bass.Bass("TRN2", target_bir_lowering=False, debug=False)
    I = 2816
    ei = lambda n, sh, dt=F32: nc.dram_tensor(n, sh, dt, kind="ExternalInput").ap()
    it = lambda n, sh, dt=F32: nc.dram_tensor(n, sh, dt, kind="Internal").ap()
    xT = ei("xT", [C, TL]); x = ei("x", [TL, C])
    Wqkv = ei("Wqkv", [C, 3 * C]); bqkvT = ei("bqkvT", [24, 128])
    omh = ei("omh", [FD, H * FD]); qn = ei("qn", [2]); qw = ei("qw", [2])
    Wout = ei("Wout", [C, C]); bout = ei("bout", [C])
    w_gate = ei("w_gate", [I, C]); w_up = ei("w_up", [I, C]); w_down = ei("w_down", [C, I])
    w_qkv1 = ei("w_qkv1", [C, 3 * C]); b_qkv1T = ei("b_qkv1T", [24, 128])
    w_out1 = ei("w_out1", [C, C]); b_out1 = ei("b_out1", [C])
    weightsT = ei("weightsT", [C, C]); scale = ei("scale", [C]); bias = ei("bias", [C])
    msk = ei("msk", [2])
    out = nc.dram_tensor("out", [TL, C], F32, kind="ExternalOutput").ap()
    import os as _os
    dbg = _os.environ.get("FUSED_DBG", "0") == "1"
    kd = "ExternalOutput" if dbg else "Internal"
    itd = lambda n, sh, dt=F32: nc.dram_tensor(n, sh, dt, kind=kd).ap()
    qf = it("qf_s", [NCH, FD, H, CH], BF16); kf = it("kf_s", [NCH, FD, H, CH], BF16)
    vt = it("vt_s", [H, NCH, 128, 4, HD], BF16)
    sprev7 = it("sprev7_s", [NCH - 1, H, FD, 65])
    stot = itd("stot_s", [H, FD, 65])
    e1in = nc.dram_tensor("e1in_s", [FD, H * 65], F32, kind="Internal")
    e1out = nc.dram_tensor("e1out_s", [FD, H * 65], F32, kind="Internal")
    wrowsA = ei("wrowsA", [6, 128, 1024])
    wrowsB = ei("wrowsB", [128, 2816])
    ag_in = it("agin_s", [7, 22, 128, 128], BF16)
    ag_out = it("agout_s", [56 * 22 * 128, 128], BF16)
    init = itd("init_s", [H, FD, 65])
    x1 = itd("x1_s", [TL, C]); x2 = itd("x2_s", [TL, C]); x3 = itd("x3_s", [TL, C])
    e2in = it("e2in_s", [2048, C], BF16)
    e2out = it("e2out_s", [2048, C], BF16)
    wqT16 = itd("wqT16_s", [C, WQR], BF16)
    qk2 = it("qk2_s", [2048, WQR], BF16); vv = it("vv_s", [WQR, C], BF16)
    x3T16 = itd("x3T16_s", [C, TL], BF16)

    phases = [
        (emit_Dprep, {"prefix": "dp_", "wrowsA": wrowsA, "wrowsB": wrowsB,
                      "ag_in": ag_in, "ag_out": ag_out}),
        (emit_A, {"prefix": "a_", "xT": xT, "Wqkv": Wqkv, "bqkvT": bqkvT, "omh": omh,
                  "qn": qn, "qw": qw, "qf": qf, "kf": kf, "vt": vt, "sprev": sprev7,
                  "stot": stot}),
        (emit_E1, {"prefix": "e1_", "stot": stot, "e1in": e1in, "e1out": e1out,
                   "msk": msk, "init": init}),
        (emit_B, {"prefix": "b_", "qf": qf, "kf": kf, "vt": vt, "sprev7": sprev7,
                  "init": init, "x": x, "Wout": Wout, "bout": bout, "x1": x1}, True),
        (emit_D, {"prefix": "d_", "x1": x1, "w_gate": w_gate, "w_up": w_up,
                  "w_down": w_down, "x2": x2, "ag_out": ag_out}),
        (emit_E2, {"prefix": "e2_", "x2": x2, "e2in": e2in, "e2out": e2out,
                   "msk": msk, "wqT16": wqT16}),
        (emit_F1, {"prefix": "f1_", "wqT": wqT16, "w_qkv": w_qkv1, "b_qkvT": b_qkv1T,
                   "qk2": qk2, "vv": vv}, True),
        (emit_F2, {"prefix": "f2_", "qk2": qk2, "vv": vv, "w_out": w_out1,
                   "b_out": b_out1, "x2own": x2, "x3": x3, "x3T16": x3T16}),
        (emit_H, {"prefix": "h_", "x3": x3, "x3T": x3T16, "weightsT": weightsT,
                  "scale": scale, "bias": bias, "out": out}),
    ]
    finish_build_multi(nc, phases)
    return nc


def finish_build(nc, emitter, io):
    import contextlib
    with contextlib.ExitStack() as ctx:
        p = Prog(nc)
        for s in ([f"ld{i}" for i in range(Prog.LD_POOL)]
                  + [f"st{i}" for i in range(Prog.ST_POOL)]
                  + ["st_g", "pe", "dve", "act"]):
            p.sem(s)
        sems = {}
        for s in p.sem_names:
            sems[s] = ctx.enter_context(nc.semaphore(s))
        block = ctx.enter_context(nc.Block())
        emitter(p, ctx, nc, io)
        p.build(block, sems)


_BUILT = {}


def _get(name, fn):
    if name not in _BUILT:
        _BUILT[name] = fn()
    return _BUILT[name]


def kernel(L0_attn_x, L0_attn_Wqkv, L0_attn_bqkv, L0_attn_Wout, L0_attn_bout,
           L0_attn_omega, L0_attn_quad_nodes, L0_attn_quad_weights,
           L0_ffn_w_gate, L0_ffn_w_up, L0_ffn_w_down,
           L1_attn_w_qkv, L1_attn_b_qkv, L1_attn_w_out, L1_attn_b_out,
           L1_ffn_weights, L1_ffn_scale, L1_ffn_bias):
    asn = lambda a: np.ascontiguousarray(np.asarray(a))
    x = asn(L0_attn_x)
    Wqkv = asn(L0_attn_Wqkv); bqkv = asn(L0_attn_bqkv)
    Wout = asn(L0_attn_Wout); bout = asn(L0_attn_bout)
    omega = asn(L0_attn_omega)
    qn = asn(L0_attn_quad_nodes); qw = asn(L0_attn_quad_weights)
    wg = asn(L0_ffn_w_gate); wu = asn(L0_ffn_w_up); wd = asn(L0_ffn_w_down)
    wqkv1 = asn(L1_attn_w_qkv); bqkv1 = asn(L1_attn_b_qkv)
    wout1 = asn(L1_attn_w_out); bout1 = asn(L1_attn_b_out)
    w2 = asn(L1_ffn_weights); sc2 = asn(L1_ffn_scale); bi2 = asn(L1_ffn_bias)
    cores = list(range(NCORE))

    omh = np.ascontiguousarray(np.transpose(omega, (2, 1, 0, 3)).reshape(HD, H * FD))
    bqkvT = bqkv.reshape(24, 128)

    if True:
        in_maps = []
        for c in cores:
            b, s = c // 2, c % 2
            xs = np.ascontiguousarray(x[b, s * TL:(s + 1) * TL])
            wrA = np.zeros((6, 128, 1024), np.float32)
            wrB = np.empty((128, 2816), np.float32)
            for si in range(7):
                j = si * 8 + c
                ent = _DPREP[j]
                if ent is None:
                    continue
                wname, r = ent
                wmat = {"w_gate": wg, "w_up": wu, "w_down": wd}[wname]
                if si < 6:
                    wrA[si] = wmat[r * 128:(r + 1) * 128, :]
                else:
                    wrB[:] = wmat[r * 128:(r + 1) * 128, :]
            in_maps.append({
                "wrowsA": wrA, "wrowsB": wrB,
                "xT": np.ascontiguousarray(xs.T), "x": xs,
                "Wqkv": Wqkv, "bqkvT": bqkvT, "omh": omh, "qn": qn, "qw": qw,
                "Wout": Wout, "bout": bout,
                "w_gate": wg, "w_up": wu, "w_down": wd,
                "w_qkv1": wqkv1, "b_qkv1T": bqkv1.reshape(24, 128),
                "w_out1": wout1, "b_out1": bout1,
                "weightsT": np.ascontiguousarray(w2.T), "scale": sc2, "bias": bi2,
                "msk": np.array([1.0 - s, float(s)], np.float32),
            })
        res = run_bass_kernel_spmd(_get("FUSED", build_fused), in_maps, core_ids=cores).results
        out = np.empty((B, T, C), np.float32)
        for c in cores:
            b, s = c // 2, c % 2
            out[b, s * TL:(s + 1) * TL] = res[c]["out"]
        return out

    insA = []
    for c in cores:
        b, s = c // 2, c % 2
        xs = x[b, s * TL:(s + 1) * TL]
        insA.append({"xT": np.ascontiguousarray(xs.T), "Wqkv": Wqkv, "bqkvT": bqkvT,
                     "omh": omh, "qn": qn, "qw": qw})
    rA = run_bass_kernel_spmd(_get("A", build_A), insA, core_ids=cores).results

    z1 = np.zeros((1, H, FD, 65), np.float32)
    z0 = np.zeros((H, FD, 65), np.float32)
    insB = []
    for c in cores:
        b, s = c // 2, c % 2
        init = rA[2 * b]["stot"].astype(np.float32) if s == 1 else z0
        insB.append({"qf": rA[c]["qf"], "kf": rA[c]["kf"], "vt": rA[c]["vt"],
                     "sprev8": np.concatenate([z1, rA[c]["sprev"].astype(np.float32)], 0),
                     "init": init, "x": x[b, s * TL:(s + 1) * TL],
                     "Wout": Wout, "bout": bout})
    rB = run_bass_kernel_spmd(_get("B", build_B), insB, core_ids=cores).results

    insD = [{"x1": rB[c]["x1"], "w_gate": wg, "w_up": wu, "w_down": wd} for c in cores]
    rD = run_bass_kernel_spmd(_get("D", build_D), insD, core_ids=cores).results

    bqkv1T = bqkv1.reshape(24, 128)
    insF1 = []
    for c in cores:
        b, s = c // 2, c % 2
        if s == 0:
            wq = rD[c]["x2"][0:WQR]
        else:
            wq = np.concatenate([rD[2 * b]["x2"][2048:4096], rD[c]["x2"][0:256]], 0)
        insF1.append({"wqT": np.ascontiguousarray(wq.T), "w_qkv": wqkv1, "b_qkvT": bqkv1T})
    rF1 = run_bass_kernel_spmd(_get("F1", build_F1), insF1, core_ids=cores).results

    insF2 = [{"qk2": rF1[c]["qk2"], "vv": rF1[c]["vv"], "w_out": wout1, "b_out": bout1,
              "x2own": rD[c]["x2"]} for c in cores]
    rF2 = run_bass_kernel_spmd(_get("F2", build_F2), insF2, core_ids=cores).results

    w2T = np.ascontiguousarray(w2.T)
    insH = [{"x3": rF2[c]["x3"], "x3T": np.ascontiguousarray(rF2[c]["x3"].T),
             "weightsT": w2T, "scale": sc2, "bias": bi2} for c in cores]
    rH = run_bass_kernel_spmd(_get("H", build_H), insH, core_ids=cores).results

    out = np.empty((B, T, C), np.float32)
    for c in cores:
        b, s = c // 2, c % 2
        out[b, s * TL:(s + 1) * TL] = rH[c]["out"]
    return out
